# version 17
# baseline (speedup 1.0000x reference)
import math
import sys

sys.path.insert(0, "/opt/trn_rl_repo")

import numpy as np
import ml_dtypes

import concourse.bass as bass
import concourse.mybir as mybir
from concourse.bass_utils import run_bass_kernel_spmd

F32 = mybir.dt.float32
BF16 = mybir.dt.bfloat16
AF = mybir.ActivationFunctionType
ALU = mybir.AluOpType
AX = mybir.AxisListType
BF = ml_dtypes.bfloat16

T = 2048
D = 4096
HD = 128
H = 32
KVH = 8
DKV = 1024
NE = 8
RK = 64
NS = 16
NG = 4
EPS = 1e-5
ISQ = 1.0 / math.sqrt(HD)


class Sem:
    def __init__(self, nc, name):
        self.h = nc.alloc_semaphore(name=name)
        self.name = name
        self.val = 0


class Ctx:

    def __init__(self, sems):
        self.sems = sems
        self.ck = {}
        self.first = True
        self.emit = False
        self.pending = []

    def start_pass(self, emit):
        self.emit = emit
        self.cnt = {}
        for s in self.sems.values():
            s.val = 0

    def nid(self, tag):
        self.cnt[tag] = self.cnt.get(tag, 0) + 1
        return f"{tag}.{self.cnt[tag]}" 

    def op(self, f, sem=None, n=0, name=None):
        ins = f() if self.emit else None
        if sem is not None and n > 0:
            sem.val += n
            if name is not None:
                if self.first:
                    assert name not in self.ck, f"dup ck {name}"
                    self.ck[name] = (sem.name, sem.val)
                else:
                    assert self.ck[name] == (sem.name, sem.val), (
                        f"ck mismatch {name}")
            if self.emit:
                ins.then_inc(sem.h, n)
        return ins

    def wait(self, eng, name):
        if not self.emit:
            self.pending.append(name)
            return
        sname, val = self.ck[name]
        eng.wait_ge(self.sems[sname].h, val)

    def finish_pass1(self):
        missing = [n for n in self.pending if n not in self.ck]
        assert not missing, f"unresolved checkpoints: {sorted(set(missing))[:10]}"
        self.first = False


def _llama3_rope_tables():
    base = 500000.0
    inv_freq = 1.0 / base ** (np.arange(0, HD, 2, dtype=np.float32) / HD)
    factor, low_f, high_f, old_ctx = 32.0, 1.0, 4.0, 8192
    low_wl, high_wl = old_ctx / low_f, old_ctx / high_f
    wavelen = 2 * math.pi / inv_freq
    inv_llama = np.where(wavelen > low_wl, inv_freq / factor, inv_freq)
    smooth = (old_ctx / wavelen - low_f) / (high_f - low_f)
    smoothed = (1 - smooth) * inv_llama / factor + smooth * inv_llama
    is_med = ~(wavelen < high_wl) & ~(wavelen > low_wl)
    inv = np.where(is_med, smoothed, inv_llama).astype(np.float32)
    pos = np.arange(T, dtype=np.float32)
    freqs = pos[:, None] * inv[None, :]
    emb = np.concatenate([freqs, freqs], axis=-1)
    return np.cos(emb).astype(np.float32), np.sin(emb).astype(np.float32)


def build_program(dbg=False, mp=5, ndb=8, p2as=5):
    nc = bass.Bass("TRN2", target_bir_lowering=False)

    X = nc.dram_tensor("x", [T, D], F32, kind="ExternalInput")
    WQT = nc.dram_tensor("wqt", [D, D], F32, kind="ExternalInput")
    WKT = nc.dram_tensor("wkt", [D, DKV], F32, kind="ExternalInput")
    WVT = nc.dram_tensor("wvt", [D, DKV], F32, kind="ExternalInput")
    WOT = nc.dram_tensor("wot", [D, D], F32, kind="ExternalInput")
    AT = nc.dram_tensor("at", [D, NE * RK], F32, kind="ExternalInput")
    GWT = nc.dram_tensor("gwt", [128, 256], F32, kind="ExternalInput")
    BT = nc.dram_tensor("bt", [NE * RK, D], F32, kind="ExternalInput")
    COS = nc.dram_tensor("cos", [T, HD], F32, kind="ExternalInput")
    SIN = nc.dram_tensor("sin", [T, HD], F32, kind="ExternalInput")
    IDT = nc.dram_tensor("idt_d", [128, 128], BF16, kind="ExternalInput")
    TRI = nc.dram_tensor("tri_d", [128, 128], BF16, kind="ExternalInput")
    ONES2 = nc.dram_tensor("ones2_d", [128, 128], BF16, kind="ExternalInput")
    CVEC = nc.dram_tensor("cvec_d", [128, 2], F32, kind="ExternalInput")
    OUT = nc.dram_tensor("out", [T, D], F32, kind="ExternalOutput")

    skind = "ExternalOutput" if dbg else "Internal"
    n1t_d = nc.dram_tensor("n1t_d", [32, 128, T], BF16, kind=skind)
    ktc_d = nc.dram_tensor("ktc_d", [KVH, 128, T], BF16, kind=skind)
    vc_d = nc.dram_tensor("vc_d", [NS, 128, DKV], BF16, kind=skind)
    qt_d = nc.dram_tensor("qt_d", [H, 128, T], BF16, kind=skind)
    ot_d = nc.dram_tensor("ot_d", [32, 128, T], BF16, kind=skind)
    xp_d = nc.dram_tensor("xp_d", [T, D], F32, kind=skind)

    BIG = nc.alloc_sbuf_tensor("big", [128, 65536], BF16)
    MED = nc.alloc_sbuf_tensor("med", [128, 32768], BF16)
    SMF = nc.alloc_sbuf_tensor("smf", [128, 2048], F32)
    SMB = nc.alloc_sbuf_tensor("smb", [128, 2048], BF16)
    STT = nc.alloc_sbuf_tensor("stt", [128, 160], F32)
    ident = nc.alloc_sbuf_tensor("ident", [128, 128], BF16)
    tri = nc.alloc_sbuf_tensor("tri", [128, 128], BF16)
    ones2 = nc.alloc_sbuf_tensor("ones2", [128, 128], BF16)
    cvec = nc.alloc_sbuf_tensor("cvec", [128, 2], F32)

    ps4 = nc.alloc_psum_tensor("ps4", [128, 2048], F32)
    psE = nc.alloc_psum_tensor("psE", [128, 1024], F32)
    psC = nc.alloc_psum_tensor("psC", [128, 512], F32)
    psD = nc.alloc_psum_tensor("psD", [128, 512], F32)
    psEa = psE.ap()
    psA = psEa[:, 0:512]
    psB = psEa[:, 512:1024]
    psE_bf = psEa.bitcast(BF16)

    pbA = psE_bf[:, 0:1024]
    pbB = psE_bf[:, 1024:2048]
    pbC = psC.ap().bitcast(BF16)
    pbD = psD.ap().bitcast(BF16)

    def pbh(half, ph=0):
        if ph == 0:
            return (pbA, pbC)[half]
        return (pbC, pbD)[half]

    def bigv(c0, c1):
        return BIG[:, c0:c1]

    wk_sb = bigv(0, 32768)
    wv_sb = bigv(0, 32768)
    n1tg = bigv(32768, 49152)
    ktcg = bigv(49152, 53248)
    krot2 = [bigv(53248, 54272), bigv(64512, 65536)]
    n1_2 = [bigv(54272, 58368), SMF.ap().bitcast(BF16)]
    p1_tmp = bigv(58368, 62464).bitcast(F32)
    p1_cs = bigv(62464, 64512).bitcast(F32)
    vcb = bigv(49152, 53248)
    xb_p1 = MED.ap().bitcast(F32)

    n1t_f = bigv(0, 65536)
    wq_sb = MED
    qrot2 = [SMB[:, 0:512], SMB[:, 1536:2048]]
    qtw = [SMB[:, 512:1024], SMB[:, 1024:1536]]
    p2a_tmp = SMF[:, 0:1024]
    p2a_cs = SMF[:, 1024:2048]

    ktc_sb = bigv(0, 16384)
    vc_sb = bigv(16384, 32768)
    qt_sb = [bigv(32768, 49152), bigv(49152, 65536)]
    expT2 = [MED[:, 16384:18432], MED[:, 20480:22528]]
    outw = [MED[:, 22528:23040], MED[:, 23040:23552]]
    bc2 = [SMF[:, 0:512], SMF[:, 1024:1536]]
    t_sb = SMF[:, 512:1024]

    ot_sb = bigv(0, 65536)
    wo_sb = MED
    xsb = [SMF[:, 0:512], SMF[:, 512:1024]]
    xpw = [SMF[:, 1024:1536], SMF[:, 1536:2048]]

    aT_sb = bigv(0, 16384)
    bT_sb = bigv(16384, 32768)
    outb = bigv(32768, 49152).bitcast(F32)
    n2t2 = [bigv(49152, 53248), bigv(53248, 57344)]
    n2_2 = [bigv(57344, 61440), bigv(61440, 65536)]
    xp_sb = MED.ap().bitcast(F32)
    gw_sb = SMF.ap().bitcast(BF16)[:, 3072:3328]
    h_2 = [SMB[:, 0:512], SMB[:, 512:1024]]
    hT_2 = [SMB[:, 1024:1536], SMB[:, 1536:2048]]

    def bnstv(par):
        return STT[:, par * 54: par * 54 + 48]

    def statv(par, i):
        return STT[:, par * 54 + 48 + i: par * 54 + 48 + i + 1]

    negm = STT[:, 108:109]
    m1 = STT[:, 109:110]
    m2 = STT[:, 110:111]
    wsum = STT[:, 111:112]
    winv = STT[:, 112:113]
    probs = STT[:, 114:122]
    ltm = STT[:, 122:130]
    pmask = STT[:, 130:138]
    keept = STT[:, 138:146]
    wful = STT[:, 146:154]

    sem_names = (["t", "v", "s", "ld_c", "ld_n1g", "ld_n1f", "ld_kv", "ld_ot",
                  "g_wk", "g_wv", "g_ab", "g_gw", "s_ab", "s_ot"]
                 + [f"{b}{i}" for b in ["ld_x", "ld_cs", "ld_qt", "ld_x2",
                                        "ld_xp", "g_out", "g_n1w", "g_kw",
                                        "g_vc", "g_wq", "g_qw", "g_ot",
                                        "g_wo", "g_xp"] for i in (0, 1)])
    sems = {n: Sem(nc, n) for n in sem_names}
    ctx = Ctx(sems)
    S = sems
    st_, sv, ss = sems["t"], sems["v"], sems["s"]

    res = {}

    def use(key, wait_eng):
        if res.get(key) is not None:
            ctx.wait(wait_eng, res[key])

    def rel(key, name):
        res[key] = name

    def emit_rope(vec, src_ap, cs_ap, tmp_ap, out_ap, nh):
        def v3(ap_, lo):
            return ap_.rearrange("p (h x) -> p h x", h=nh)[:, :, lo * 64:(lo + 1) * 64]

        def csb(lo, which):
            return cs_ap[:, which * 128 + lo * 64: which * 128 + (lo + 1) * 64] \
                .unsqueeze(1).broadcast_to([128, nh, 64])

        t1 = tmp_ap[:, 0: nh * 64].rearrange("p (h x) -> p h x", h=nh)
        t2 = tmp_ap[:, nh * 64: nh * 128].rearrange("p (h x) -> p h x", h=nh)
        ctx.op(lambda: vec.tensor_tensor(t1, v3(src_ap, 1), csb(0, 1), ALU.mult))
        ctx.op(lambda: vec.tensor_tensor(t2, v3(src_ap, 0), csb(0, 0), ALU.mult))
        ctx.op(lambda: vec.tensor_tensor(v3(out_ap, 0), t2, t1, ALU.subtract))
        ctx.op(lambda: vec.tensor_tensor(t1, v3(src_ap, 0), csb(1, 1), ALU.mult))
        ctx.op(lambda: vec.tensor_tensor(t2, v3(src_ap, 1), csb(1, 0), ALU.mult))
        return lambda: vec.tensor_tensor(v3(out_ap, 1), t2, t1, ALU.add)

    def emit_ln_scalar_pre(sca, par=0):
        nm = ctx.nid("lnln")
        ctx.op(lambda: sca.activation(out=statv(par, 2), in_=statv(par, 1),
                                      func=AF.Ln, bias=cvec[:, 0:1]), ss, 1, nm)
        ctx.wait(sca, nm)
        return lambda: sca.activation(out=statv(par, 3), in_=statv(par, 2),
                                      func=AF.Exp, scale=-0.5)

    def emit_bn(vec, x_ap, par=0):
        nm = ctx.nid("bnst")
        for i in range(8):
            ctx.op(lambda i=i, par=par: vec.bn_stats(
                bnstv(par)[:, i * 6:(i + 1) * 6],
                x_ap[:, i * 512:(i + 1) * 512]),
                sv, 1 if i == 7 else 0, nm if i == 7 else None)
        ctx.wait(vec, nm)
        return lambda: vec.bn_aggr(STT[:, par * 54 + 48: par * 54 + 50], bnstv(par))


    def sync_finish(sync):
        if ctx.emit:
            for nm, sm in sems.items():
                if nm.startswith("ld_") and sm.val > 0:
                    sync.wait_ge(sm.h, sm.val)

    def gp_finish(gp):
        if ctx.emit:
            for nm, sm in sems.items():
                if nm.startswith("g_") and sm.val > 0:
                    gp.wait_ge(sm.h, sm.val)

    def emit_sync(sync):
        consts = [(ident, IDT), (tri, TRI), (cvec, CVEC), (ones2, ONES2)]
        for i, (tns, src) in enumerate(consts):
            ctx.op(lambda tns=tns, src=src: sync.dma_start(out=tns[:], in_=src[:, :]),
                   S["ld_c"], 16, "consts" if i == 3 else None)

        for s in range(NS):
            t0 = s * 128
            bi = s % 2
            use(("xb", bi), sync)
            ctx.op(lambda bi=bi, t0=t0: sync.dma_start(
                out=xb_p1[:, bi * 4096:(bi + 1) * 4096], in_=X[t0:t0 + 128, :]),
                S[f"ld_x{bi}"], 16, f"p1a.x.{s}")
            rel(("xb", bi), f"p1a.n1.{s}")
            use(("cs1", bi), sync)
            ctx.op(lambda bi=bi, t0=t0: sync.dma_start(
                out=p1_cs[:, bi * 512: bi * 512 + 128], in_=COS[t0:t0 + 128, :]),
                S[f"ld_cs{bi}"], 16)
            ctx.op(lambda bi=bi, t0=t0: sync.dma_start(
                out=p1_cs[:, bi * 512 + 128: bi * 512 + 256], in_=SIN[t0:t0 + 128, :]),
                S[f"ld_cs{bi}"], 16, f"p1a.cs.{s}")
            rel(("cs1", bi), f"p1a.ropek.{s}")

        if mp < 1:
            sync_finish(sync)
            return
        for g in [0, 1, 2]:
            ctx.wait(sync, f"p1a.n1tw.{g}")
            if g == 0:
                ctx.wait(sync, f"p1b.vmm.15")
            else:
                ctx.wait(sync, f"p1b.vmm.{4 * g - 1}")
            for q4 in range(4):
                src = n1t_d[:, :, g * 512:(g + 1) * 512].rearrange(
                    "c p t -> p c t")[:, 8 * q4:8 * q4 + 8, :]
                dst = n1tg.rearrange("p (c t) -> p c t", c=32)[:, 8 * q4:8 * q4 + 8, :]
                ctx.op(lambda src=src, dst=dst: sync.dma_start(out=dst, in_=src),
                       S["ld_n1g"], 16, f"p1b.n1t.{g}" if q4 == 3 else None)

        if mp < 2:
            sync_finish(sync)
            return
        def n1f_ld(ci):
            ctx.op(lambda ci=ci: sync.dma_start(
                out=n1t_f[:, ci * 2048:(ci + 1) * 2048], in_=n1t_d[ci, :, :]),
                S["ld_n1f"], 16, f"p2a.n1t.{ci}")

        ctx.wait(sync, "p1a.n1tw.2")
        ctx.wait(sync, "p1a.n1tw.3")
        ctx.wait(sync, "p1a.tr.14.3")
        ctx.wait(sync, "p1a.ropek.15")
        ctx.wait(sync, "p1a.ktr.15.0")
        for ci in range(26, 32):
            n1f_ld(ci)
        ctx.wait(sync, "p1b.vmm.11")
        for ci in range(0, 12):
            n1f_ld(ci)
        ctx.wait(sync, "p1b.vev.11.1")
        ctx.wait(sync, "p1b.vcw.11")
        ctx.wait(sync, "p1b.vcw.10")
        for ci in (24, 25):
            n1f_ld(ci)
        for db in range(ndb):
            for s in range(NS):
                t0 = s * 128
                bi = s % 2
                use(("cs2", bi), sync)
                ctx.op(lambda bi=bi, t0=t0: sync.dma_start(
                    out=p2a_cs[:, bi * 512: bi * 512 + 128], in_=COS[t0:t0 + 128, :]),
                    S[f"ld_cs{bi}"], 16)
                ctx.op(lambda bi=bi, t0=t0: sync.dma_start(
                    out=p2a_cs[:, bi * 512 + 128: bi * 512 + 256],
                    in_=SIN[t0:t0 + 128, :]), S[f"ld_cs{bi}"], 16,
                    f"p2a.cs.{db}.{s}")
                rel(("cs2", bi), f"p2a.rope.{db}.{s}")

        if mp < 3:
            sync_finish(sync)
            return
        ctx.wait(sync, "p2a.qmm.7.15")
        ctx.wait(sync, "p2a.qtr.7.15")
        ctx.wait(sync, "p1a.ktcw.3")
        ctx.wait(sync, "p1a.ktcw.2")
        ctx.op(lambda: sync.dma_start(
            out=ktc_sb.rearrange("p (h t) -> p h t", h=KVH),
            in_=ktc_d.ap().rearrange("h p t -> p h t")), S["ld_kv"], 16)
        for q2 in range(2):
            ctx.op(lambda q2=q2: sync.dma_start(
                out=vc_sb.rearrange("p (s d) -> p s d", s=NS)[:, 8 * q2:8 * q2 + 8, :],
                in_=vc_d.ap().rearrange("s p d -> p s d")[:, 8 * q2:8 * q2 + 8, :]),
                S["ld_kv"], 16, "p2b.kv" if q2 == 1 else None)
        for j in range(4):
            ctx.wait(sync, "p2a.qtw.7.15")
            ctx.wait(sync, "p2a.qtw.7.14")
            if j >= 2:
                ctx.wait(sync, f"p2b.sc.{j - 2}.31.{2 * (j - 2) + 1}")
            for q4 in range(4):
                src = qt_d[:, :, j * 512:(j + 1) * 512].rearrange(
                    "h p t -> p h t")[:, 8 * q4:8 * q4 + 8, :]
                dst = qt_sb[j % 2].rearrange("p (h t) -> p h t", h=H)[
                    :, 8 * q4:8 * q4 + 8, :]
                ctx.op(lambda src=src, dst=dst: sync.dma_start(out=dst, in_=src),
                       S[f"ld_qt{j % 2}"], 16, f"p2b.qt.{j}" if q4 == 3 else None)

        if mp < 4:
            sync_finish(sync)
            return
        ctx.wait(sync, "p2b.av.3.31.7")
        ctx.wait(sync, "p2b.otw.3.31")
        ctx.wait(sync, "p2b.otw.3.30")
        for ci in range(16):
            ctx.op(lambda ci=ci: sync.dma_start(
                out=ot_sb[:, ci * 2048:(ci + 1) * 2048], in_=ot_d[ci, :, :]),
                S["ld_ot"], 16, f"p2c.ot.{ci}")
        for db in range(8):
            for s in range(NS):
                t0 = s * 128
                bi = s % 2
                if db == 0 and s == 0:
                    ctx.wait(sync, "p2b.outev.3.31")
                use(("xs", bi), sync)
                ctx.op(lambda db=db, t0=t0, bi=bi: sync.dma_start(
                    out=xsb[bi], in_=X[t0:t0 + 128, db * 512:(db + 1) * 512]),
                    S[f"ld_x2{bi}"], 16, f"p2c.x.{db}.{s}")
                rel(("xs", bi), f"p2c.ev.{db}.{s}")

        if mp < 5:
            sync_finish(sync)
            return
        for s in range(NS):
            sl4 = s % 4
            ctx.wait(sync, f"p2c.xpw.7.{s}")
            if s < 2:
                ctx.wait(sync, "p2c.omm.6.15")
            elif s < 4:
                ctx.wait(sync, "p2c.omm.7.15")
            else:
                ctx.wait(sync, f"p2d.bev.{s - 4}.7")
            ctx.op(lambda s=s, sl4=sl4: sync.dma_start(
                out=xp_sb[:, sl4 * 4096:(sl4 + 1) * 4096],
                in_=xp_d[s * 128:(s + 1) * 128, :]),
                S[f"ld_xp{s % 2}"], 16, f"p2d.xp.{s}")
        sync_finish(sync)

    def emit_gpsimd(gp):
        for i in range(4):
            src = WKT.ap().rearrange("(c p) d -> p c d", p=128)[:, 8 * i: 8 * i + 8, :]
            dst = wk_sb.rearrange("p (c d) -> p c d", c=32)[:, 8 * i: 8 * i + 8, :]
            ctx.op(lambda src=src, dst=dst: gp.dma_start(out=dst, in_=src),
                   S["g_wk"], 16, f"p1a.wk.{i}")
        for g in range(NG):
            ctx.wait(gp, f"p1a.n1ev.{4 * g + 3}.3")
            for q4 in range(4):
                dsta = n1t_d[:, :, g * 512:(g + 1) * 512].rearrange(
                    "c p t -> p c t")[:, 8 * q4:8 * q4 + 8, :]
                srca = n1tg.rearrange("p (c t) -> p c t", c=32)[:, 8 * q4:8 * q4 + 8, :]
                ctx.op(lambda dsta=dsta, srca=srca: gp.dma_start(out=dsta, in_=srca),
                       S[f"g_n1w{g % 2}"], 16, f"p1a.n1tw.{g}" if q4 == 3 else None)
            ctx.wait(gp, f"p1a.kev.{4 * g + 3}.0")
            ctx.op(lambda g=g: gp.dma_start(
                out=ktc_d[:, :, g * 512:(g + 1) * 512].rearrange("h p t -> p h t"),
                in_=ktcg.rearrange("p (h t) -> p h t", h=KVH)),
                S[f"g_kw{g % 2}"], 16, f"p1a.ktcw.{g}")

        if mp < 1:
            gp_finish(gp)
            return
        ctx.wait(gp, "p1a.kmm.15")
        for i in range(4):
            src = WVT.ap().rearrange("(c p) d -> p c d", p=128)[:, 8 * i: 8 * i + 8, :]
            dst = wv_sb.rearrange("p (c d) -> p c d", c=32)[:, 8 * i: 8 * i + 8, :]
            ctx.op(lambda src=src, dst=dst: gp.dma_start(out=dst, in_=src),
                   S["g_wv"], 16, f"p1b.wv.{i}")
        for s in list(range(12, 16)) + list(range(12)):
            ctx.wait(gp, f"p1b.vev.{s}.1")
            ctx.op(lambda s=s: gp.dma_start(
                out=vc_d[s, :, :], in_=vcb[:, (s % 2) * 1024:(s % 2 + 1) * 1024]),
                S[f"g_vc{s % 2}"], 16, f"p1b.vcw.{s}")

        if mp < 2:
            gp_finish(gp)
            return
        def load_wq(db):
            for hh in range(2):
                if db >= 2:
                    ctx.wait(gp, f"p2a.qmm.{db - 2}.15")
                elif db == 0 and hh == 0:
                    ctx.wait(gp, "p1a.n1.15")
                for q4 in range(4):
                    src = WQT[:, db * 512 + hh * 256: db * 512 + (hh + 1) * 256] \
                        .rearrange("(c p) d -> p c d", p=128)[:, 8 * q4:8 * q4 + 8, :]
                    dst = wq_sb[:, (db % 2) * 16384:(db % 2 + 1) * 16384] \
                        .rearrange("p (c d) -> p c d", c=32)[
                        :, 8 * q4:8 * q4 + 8, hh * 256:(hh + 1) * 256]
                    ctx.op(lambda src=src, dst=dst: gp.dma_start(out=dst, in_=src),
                           S[f"g_wq{db % 2}"], 16,
                           f"p2a.wq.{db}.{hh}" if q4 == 3 else None)

        load_wq(0)
        if ndb > 1:
            load_wq(1)
        for db in range(ndb):
            if p2as >= 5:
                for s in range(NS):
                    ctx.wait(gp, f"p2a.qtev.{db}.{s}")
                    hbase = db * 4
                    dst = qt_d[hbase:hbase + 4, :, s * 128:(s + 1) * 128] \
                        .rearrange("h p t -> p h t")
                    src = qtw[s % 2].rearrange("p (h t) -> p h t", h=4)
                    ctx.op(lambda dst=dst, src=src: gp.dma_start(out=dst, in_=src),
                           S[f"g_qw{s % 2}"], 16, f"p2a.qtw.{db}.{s}")
            if db + 2 <= ndb - 1:
                load_wq(db + 2)

        if mp < 3:
            gp_finish(gp)
            return
        def load_wo(db):
            for hh in range(2):
                if db >= 2:
                    ctx.wait(gp, f"p2c.omm.{db - 2}.15")
                elif db == 0 and hh == 0:
                    ctx.wait(gp, "p2a.qmm.7.15")
                elif db == 1 and hh == 0:
                    ctx.wait(gp, "p2b.otw.3.31")
                    ctx.wait(gp, "p2b.otw.3.30")
                    ctx.wait(gp, "p2b.av.3.31.7")
                for q4 in range(4):
                    src = WOT[:, db * 512 + hh * 256: db * 512 + (hh + 1) * 256] \
                        .rearrange("(c p) d -> p c d", p=128)[:, 8 * q4:8 * q4 + 8, :]
                    dst = wo_sb[:, (db % 2) * 16384:(db % 2 + 1) * 16384] \
                        .rearrange("p (c d) -> p c d", c=32)[
                        :, 8 * q4:8 * q4 + 8, hh * 256:(hh + 1) * 256]
                    ctx.op(lambda src=src, dst=dst: gp.dma_start(out=dst, in_=src),
                           S[f"g_wo{db % 2}"], 16,
                           f"p2c.wo.{db}.{hh}" if q4 == 3 else None)

        load_wo(0)
        for j in range(4):
            for h in range(H):
                ctx.wait(gp, f"p2b.outev.{j}.{h}")
                ctx.op(lambda j=j, h=h: gp.dma_start(
                    out=ot_d[h, :, j * 512:(j + 1) * 512], in_=outw[h % 2]),
                    S[f"g_ot{h % 2}"], 16, f"p2b.otw.{j}.{h}")

        if mp < 4:
            gp_finish(gp)
            return
        load_wo(1)
        for db in range(8):
            for s in range(NS):
                t0 = s * 128
                ctx.wait(gp, f"p2c.ev.{db}.{s}")
                ctx.op(lambda db=db, t0=t0, s=s: gp.dma_start(
                    out=xp_d[t0:t0 + 128, db * 512:(db + 1) * 512], in_=xpw[s % 2]),
                    S[f"g_xp{s % 2}"], 16, f"p2c.xpw.{db}.{s}")
            if db + 2 <= 7:
                load_wo(db + 2)

        if mp < 5:
            gp_finish(gp)
            return
        ctx.wait(gp, "p2c.xpw.7.15")
        ctx.op(lambda: gp.dma_start(out=gw_sb, in_=GWT[:, :]),
               S["g_gw"], 16, "p2d.gw")
        ctx.wait(gp, "p2c.omm.7.15")
        for q4 in range(4):
            src = AT.ap().rearrange("(c p) r -> p c r", p=128)[:, 8 * q4:8 * q4 + 8, :]
            dst = aT_sb.rearrange("p (c r) -> p c r", c=32)[:, 8 * q4:8 * q4 + 8, :]
            ctx.op(lambda src=src, dst=dst: gp.dma_start(out=dst, in_=src),
                   S["g_ab"], 16, "p2d.at" if q4 == 3 else None)
        src = BT.ap().rearrange("(c p) d -> p c d", p=128)
        dst = bT_sb.rearrange("p (c d) -> p c d", c=4)
        ctx.op(lambda src=src, dst=dst: gp.dma_start(out=dst, in_=src),
               S["g_ab"], 16, "p2d.bt")
        for s in range(NS):
            ctx.wait(gp, f"p2d.bev.{s}.7")
            ctx.op(lambda s=s: gp.dma_start(
                out=OUT[s * 128:(s + 1) * 128, :],
                in_=outb[:, (s % 2) * 4096:(s % 2 + 1) * 4096]),
                S[f"g_out{s % 2}"], 16, f"out.{s}")
        gp_finish(gp)

    def emit_tensor(te):
        ctx.wait(te, "consts")
        def emit_ktr(s):
            kbi = s % 2
            half = s % 2
            ctx.wait(te, f"p1a.ropek.{s}")
            use(("pb1", half), te)
            for i in range(8):
                ctx.op(lambda half=half, i=i, kbi=kbi: te.transpose(
                    pbh(half, 0)[:, i * 128:(i + 1) * 128],
                    krot2[kbi][:, i * 128:(i + 1) * 128], ident[:]),
                    st_, 1 if i == 7 else 0,
                    f"p1a.ktr.{s}.0" if i == 7 else None)
            rel(("pb1", half), f"p1a.kev.{s}.0")

        for s in range(NS):
            g, sl = s // 4, s % 4
            bi = s % 2
            for b in range(4):
                if b == 0:
                    ctx.wait(te, f"p1a.n1.{s}")
                half = b % 2
                use(("pb1", half), te)
                for i in range(8):
                    ci = 8 * b + i
                    ctx.op(lambda half=half, ci=ci, i=i, bi=bi: te.transpose(
                        pbh(half, 0)[:, i * 128:(i + 1) * 128],
                        n1_2[bi][:, ci * 128:(ci + 1) * 128], ident[:]),
                        st_, 1 if i == 7 else 0,
                        f"p1a.tr.{s}.{b}" if i == 7 else None)
                rel(("pb1", half), f"p1a.n1ev.{s}.{b}")
            if s >= 1:
                emit_ktr(s - 1)
            ctx.wait(te, f"p1a.n1ev.{s}.3")
            if s == 0:
                ctx.wait(te, "p1a.wk.3")
            kb = s % 2
            use(("kps", kb), te)
            for dkb in range(2):
                for ci in range(32):
                    last = (dkb == 1 and ci == 31)
                    ctx.op(lambda kb=kb, dkb=dkb, ci=ci, sl=sl: te.matmul(
                        ps4[:, kb * 1024 + dkb * 512: kb * 1024 + (dkb + 1) * 512],
                        n1tg[:, ci * 512 + sl * 128: ci * 512 + sl * 128 + 128],
                        wk_sb[:, ci * 1024 + dkb * 512: ci * 1024 + (dkb + 1) * 512],
                        start=(ci == 0), stop=(ci == 31)),
                        st_, 1 if last else 0, f"p1a.kmm.{s}" if last else None)
            rel(("kps", kb), f"p1a.ropek.{s}")
        emit_ktr(NS - 1)

        if mp < 1:
            return
        rel(("ps4q", 0), "p1a.ropek.14")
        rel(("ps4q", 1), "p1a.ropek.14")
        rel(("ps4q", 2), "p1a.ropek.15")
        rel(("ps4q", 3), "p1a.ropek.15")
        first = True
        for s in list(range(12, 16)) + list(range(12)):
            g, sl = s // 4, s % 4
            if sl == 0 and g != 3:
                ctx.wait(te, f"p1b.n1t.{g}")
            if first:
                ctx.wait(te, "p1b.wv.3")
                ctx.wait(te, "p1a.kmm.15")
                first = False
            for dvb in range(2):
                q = (2 * s + dvb) % 4
                use(("ps4q", q), te)
                for ci in range(32):
                    name = (f"p1b.vmm.{s}" if dvb == 1 else f"p1b.vmm0.{s}") \
                        if ci == 31 else None
                    ctx.op(lambda q=q, ci=ci, sl=sl, dvb=dvb: te.matmul(
                        ps4[:, q * 512:(q + 1) * 512],
                        n1tg[:, ci * 512 + sl * 128: ci * 512 + sl * 128 + 128],
                        wv_sb[:, ci * 1024 + dvb * 512: ci * 1024 + (dvb + 1) * 512],
                        start=(ci == 0), stop=(ci == 31)),
                        st_, 1 if ci == 31 else 0, name)
                rel(("ps4q", q), f"p1b.vev.{s}.{dvb}")

        if mp < 2:
            return
        def emit_qtr(db, s):
            ctx.wait(te, f"p2a.rope.{db}.{s}")
            half = s % 2
            use(("pb1", half), te)
            for i in range(4):
                ctx.op(lambda half=half, i=i, s=s: te.transpose(
                    pbh(half, 0)[:, i * 128:(i + 1) * 128],
                    qrot2[s % 2][:, i * 128:(i + 1) * 128], ident[:]),
                    st_, 1 if i == 3 else 0,
                    f"p2a.qtr.{db}.{s}" if i == 3 else None)
            rel(("pb1", half), f"p2a.qtev.{db}.{s}")

        prev_q = None
        for db in range(8):
            for g in range(NG):
                for sl in range(4):
                    s = g * 4 + sl
                    if g == 0 and sl == 0:
                        ctx.wait(te, f"p2a.wq.{db}.1")
                    q = s % 4
                    use(("ps4q", q), te)
                    for ci in range(32):
                        if db == 0 and g == 0 and sl == 0:
                            ctx.wait(te, f"p2a.n1t.{ci}")
                        ctx.op(lambda q=q, ci=ci, s=s, db=db: te.matmul(
                            ps4[:, q * 512:(q + 1) * 512],
                            n1t_f[:, ci * 2048 + s * 128:
                                  ci * 2048 + s * 128 + 128],
                            wq_sb[:, (db % 2) * 16384 + ci * 512:
                                  (db % 2) * 16384 + (ci + 1) * 512],
                            start=(ci == 0), stop=(ci == 31)),
                            st_, 1 if ci == 31 else 0,
                            f"p2a.qmm.{db}.{s}" if ci == 31 else None)
                    rel(("ps4q", q), f"p2a.rope.{db}.{s}")
                    if prev_q is not None:
                        emit_qtr(*prev_q)
                    prev_q = (db, s)
        emit_qtr(*prev_q)


        if mp < 3:
            return
        rel(("sc", 0), "p2a.rope.7.13")
        rel(("sc", 1), "p2a.rope.7.15")
        rel(("psA",), res.get(("pb1", 0)))
        rel(("psC",), res.get(("pb1", 1)))
        rel(("psB",), "p2d.none" if False else None)
        res[("psB",)] = res.get(("psB",))
        def p2b_scores(j, h, gi):
            par = h % 2
            kvh = h // 4
            use(("sc", par), te)
            for p in range(2):
                kc = 2 * gi + p
                o = max(0, kc * 128 - j * 512)
                ctx.op(lambda par=par, p=p, kc=kc, kvh=kvh, h=h, j=j,
                       o=o: te.matmul(
                           ps4[:, par * 1024 + p * 512 + o:
                               par * 1024 + (p + 1) * 512],
                           ktc_sb[:, kvh * 2048 + kc * 128:
                                  kvh * 2048 + (kc + 1) * 128],
                           qt_sb[j % 2][:, h * 512 + o:(h + 1) * 512],
                           start=True, stop=True),
                       st_, 1 if p == 1 else 0,
                       f"p2b.sc.{j}.{h}.{gi}" if p == 1 else None)
            rel(("sc", par), f"p2b.exp.{j}.{h}.{gi}")

        def p2b_avsums(j, h, gi, nck):
            par = h % 2
            kvh = h // 4
            half = gi % 2
            psAC, kA = (psA, ("psA",)) if par == 0 else (psC, ("psC",))
            psBD, kB = (psB, ("psB",)) if par == 0 else (psD, ("psD",))
            ctx.wait(te, f"p2b.exp.{j}.{h}.{gi}")
            for p in range(2):
                kc = 2 * gi + p
                diag = kc >= 4 * j
                o = kc * 128 - j * 512 if diag else 0
                if diag:
                    ctx.wait(te, f"p2b.msk.{j}.{h}.{kc}")
                if kc == 0:
                    use(kA, te)
                    use(kB, te)
                last = (kc == nck - 1)
                e_sl = expT2[par][:, half * 1024 + p * 512 + o:
                                 half * 1024 + (p + 1) * 512]
                ctx.op(lambda kc=kc, o=o, e_sl=e_sl, kvh=kvh, last=last,
                       psAC=psAC: te.matmul(
                           psAC[:, o:512],
                           vc_sb[:, kc * 1024 + kvh * 128:
                                 kc * 1024 + (kvh + 1) * 128],
                           e_sl, start=(kc == 0), stop=last),
                       st_, 0)
                ctx.op(lambda kc=kc, o=o, e_sl=e_sl, last=last,
                       psBD=psBD: te.matmul(
                    psBD[:, o:512], ones2[:], e_sl,
                    start=(kc == 0), stop=last),
                    st_, 1 if p == 1 else 0,
                    f"p2b.av.{j}.{h}.{gi}" if p == 1 else None)
            if gi == nck // 2 - 1:
                rel(kA, f"p2b.outev.{j}.{h}")
                rel(kB, f"p2b.lns.{j}.{h}")

        for j in range(4):
            nck = 4 * j + 4
            ngrp = nck // 2
            for hp in range(0, H, 2):
                if hp == 0:
                    ctx.wait(te, f"p2b.qt.{j}")
                    if j == 0:
                        ctx.wait(te, "p2b.kv")
                for gi in range(ngrp):
                    p2b_scores(j, hp, gi)
                    p2b_scores(j, hp + 1, gi)
                    if gi > 0:
                        p2b_avsums(j, hp, gi - 1, nck)
                        p2b_avsums(j, hp + 1, gi - 1, nck)
                p2b_avsums(j, hp, ngrp - 1, nck)
                p2b_avsums(j, hp + 1, ngrp - 1, nck)


        if mp < 4:
            return
        rel(("ps4q", 0), "p2b.exp.3.30.7")
        rel(("ps4q", 1), "p2b.exp.3.30.7")
        rel(("ps4q", 2), "p2b.exp.3.31.7")
        rel(("ps4q", 3), "p2b.exp.3.31.7")
        for db in range(8):
            for s in range(NS):
                t0 = s * 128
                if s == 0:
                    ctx.wait(te, f"p2c.wo.{db}.1")
                q = s % 4
                use(("ps4q", q), te)
                for ci in range(32):
                    if db == 0 and s == 0:
                        ctx.wait(te, f"p2c.ot.{ci}")
                    ctx.op(lambda q=q, ci=ci, t0=t0, db=db: te.matmul(
                        ps4[:, q * 512:(q + 1) * 512],
                        ot_sb[:, ci * 2048 + t0: ci * 2048 + t0 + 128],
                        wo_sb[:, (db % 2) * 16384 + ci * 512:
                              (db % 2) * 16384 + (ci + 1) * 512],
                        start=(ci == 0), stop=(ci == 31)),
                        st_, 1 if ci == 31 else 0,
                        f"p2c.omm.{db}.{s}" if ci == 31 else None)
                rel(("ps4q", q), f"p2c.ev.{db}.{s}")

        if mp < 5:
            return
        rel(("pb1", 0), res.get(("psC",)))
        rel(("pb1", 1), res.get(("psD",)))

        def p2d_tr(s):
            for b in range(4):
                if b == 0:
                    ctx.wait(te, f"p2d.n2.{s}")
                half = b % 2
                use(("pb1", half), te)
                for i in range(8):
                    ci = 8 * b + i
                    ctx.op(lambda half=half, ci=ci, i=i, s=s: te.transpose(
                        pbh(half, 1)[:, i * 128:(i + 1) * 128],
                        n2_2[s % 2][:, ci * 128:(ci + 1) * 128], ident[:]),
                        st_, 1 if i == 7 else 0,
                        f"p2d.tr.{s}.{b}" if i == 7 else None)
                rel(("pb1", half), f"p2d.n2ev.{s}.{b}")

        def p2d_gmm(s):
            ctx.wait(te, f"p2d.n2ev.{s}.3")
            if s == 0:
                ctx.wait(te, "p2d.gw")
                use(("psB",), te)
            else:
                ctx.wait(te, f"p2d.probs.{s - 1}")
            for ci in range(32):
                ctx.op(lambda ci=ci, s=s: te.matmul(
                    psEa[:, 512:520], n2t2[s % 2][:, ci * 128:(ci + 1) * 128],
                    gw_sb[:, ci * 8:(ci + 1) * 8],
                    start=(ci == 0), stop=(ci == 31)),
                    st_, 1 if ci == 31 else 0, f"p2d.gmm.{s}" if ci == 31 else None)

        def p2d_amm(s):
            if s == 0:
                ctx.wait(te, "p2d.at")
                use(("psA",), te)
            for ci in range(32):
                ctx.op(lambda ci=ci, s=s: te.matmul(
                    psEa[:, 0:512], n2t2[s % 2][:, ci * 128:(ci + 1) * 128],
                    aT_sb[:, ci * 512:(ci + 1) * 512],
                    start=(ci == 0), stop=(ci == 31)),
                    st_, 1 if ci == 31 else 0, f"p2d.amm.{s}" if ci == 31 else None)

        def p2d_htr(s):
            ctx.wait(te, f"p2d.hsc.{s}")
            if s >= 1:
                ctx.wait(te, f"p2d.htev.{s - 1}")
            for i in range(4):
                ctx.op(lambda i=i, s=s: te.transpose(
                    psE_bf[:, 1536 + i * 128:1536 + (i + 1) * 128],
                    h_2[s % 2][:, i * 128:(i + 1) * 128], ident[:]),
                    st_, 1 if i == 3 else 0, f"p2d.htr.{s}" if i == 3 else None)

        def p2d_bmm(s):
            ctx.wait(te, f"p2d.htev.{s}")
            if s == 0:
                ctx.wait(te, "p2d.bt")
            for db in range(8):
                q = db % 4
                use(("ps4q", q), te)
                for rc in range(4):
                    ctx.op(lambda q=q, rc=rc, db=db, s=s: te.matmul(
                        ps4[:, q * 512:(q + 1) * 512],
                        hT_2[s % 2][:, rc * 128:(rc + 1) * 128],
                        bT_sb[:, rc * 4096 + db * 512: rc * 4096 + (db + 1) * 512],
                        start=(rc == 0), stop=(rc == 3)),
                        st_, 1 if rc == 3 else 0,
                        f"p2d.bmm.{s}.{db}" if rc == 3 else None)
                rel(("ps4q", q), f"p2d.bev.{s}.{db}")

        for s in range(NS):
            p2d_tr(s)
            if s >= 1:
                p2d_htr(s - 1)
                p2d_bmm(s - 1)
            p2d_gmm(s)
            p2d_amm(s)
        p2d_htr(NS - 1)
        p2d_bmm(NS - 1)

    def emit_scalar(sca):
        for s in range(NS):
            ctx.wait(sca, f"p1a.stats.{s}")
            fin = emit_ln_scalar_pre(sca, s % 2)
            ctx.op(fin, ss, 1, f"p1a.rstd.{s}")
            for b in range(4):
                half = b % 2
                ctx.wait(sca, f"p1a.tr.{s}.{b}")
                if s % 4 == 0 and b == 0 and s >= 4:
                    ctx.wait(sca, f"p1a.n1tw.{s // 4 - 1}")
                dst = n1tg.rearrange("p (c t) -> p c t", c=32)[
                    :, 8 * b: 8 * b + 8, (s % 4) * 128:(s % 4 + 1) * 128]
                srcp = pbh(half, 0).rearrange("p (c t) -> p c t", c=8)
                ctx.op(lambda dst=dst, srcp=srcp: sca.copy(dst, srcp),
                       ss, 1, f"p1a.n1ev.{s}.{b}")
            for sk in ([s - 1] if s >= 1 else []) + \
                    ([NS - 1] if s == NS - 1 else []):
                half = sk % 2
                ctx.wait(sca, f"p1a.ktr.{sk}.0")
                if sk % 4 == 0 and sk >= 4:
                    ctx.wait(sca, f"p1a.ktcw.{sk // 4 - 1}")
                dst = ktcg.rearrange("p (h t) -> p h t", h=KVH)[
                    :, :, (sk % 4) * 128:(sk % 4 + 1) * 128]
                srcp = pbh(half, 0).rearrange("p (h t) -> p h t", h=8)
                ctx.op(lambda dst=dst, srcp=srcp: sca.copy(dst, srcp),
                       ss, 1, f"p1a.kev.{sk}.0")
        if mp < 3:
            return
        ctx.wait(sca, "p1b.vmm.11")
        for ci in range(12, 24):
            ctx.op(lambda ci=ci: sca.dma_start(
                out=n1t_f[:, ci * 2048:(ci + 1) * 2048], in_=n1t_d[ci, :, :]),
                S["s_ab"], 16, f"p2a.n1t.{ci}")
        for db in range(8):
            for s in range(NS):
                bi = s % 2
                half = s % 2
                ctx.wait(sca, f"p2a.qtr.{db}.{s}")
                use(("qtw", bi), sca)
                ctx.op(lambda bi=bi, half=half: sca.copy(
                    qtw[bi], pbh(half, 0)[:, 0:512]),
                    ss, 1, f"p2a.qtev.{db}.{s}")
                rel(("qtw", bi), f"p2a.qtw.{db}.{s}")
        def p2b_exp(j, h, gi):
            par = h % 2
            half = gi % 2
            ctx.wait(sca, f"p2b.sc.{j}.{h}.{gi}")
            use(("expT", par, half), sca)
            o0 = max(0, 2 * gi * 128 - j * 512)
            o1 = max(0, (2 * gi + 1) * 128 - j * 512)
            if o0 == 0 and o1 == 0:
                ctx.op(lambda par=par, half=half: sca.activation(
                    out=expT2[par][:, half * 1024:(half + 1) * 1024],
                    in_=ps4[:, par * 1024:(par + 1) * 1024],
                    func=AF.Exp, scale=ISQ),
                    ss, 1, f"p2b.exp.{j}.{h}.{gi}")
            else:
                for p, o in ((0, o0), (1, o1)):
                    ctx.op(lambda par=par, half=half, p=p, o=o: sca.activation(
                        out=expT2[par][:, half * 1024 + p * 512 + o:
                                       half * 1024 + (p + 1) * 512],
                        in_=ps4[:, par * 1024 + p * 512 + o:
                                par * 1024 + (p + 1) * 512],
                        func=AF.Exp, scale=ISQ),
                        ss, 1 if p == 1 else 0,
                        f"p2b.exp.{j}.{h}.{gi}" if p == 1 else None)
            rel(("expT", par, half), f"p2b.av.{j}.{h}.{gi}")

        def p2b_tail_s(j, h, ngrp):
            psBD = psB if h % 2 == 0 else psD
            ctx.wait(sca, f"p2b.av.{j}.{h}.{ngrp - 1}")
            ctx.op(lambda psBD=psBD: sca.activation(
                out=t_sb, in_=psBD[:, 0:512], func=AF.Ln),
                ss, 1, f"p2b.lns.{j}.{h}")
            use(("bcs", h % 2), sca)
            ctx.op(lambda h=h: sca.activation(
                out=bc2[h % 2], in_=t_sb, func=AF.Exp, scale=-1.0),
                ss, 1, f"p2b.bcs.{j}.{h}")
            rel(("bcs", h % 2), f"p2b.outev.{j}.{h}")

        for j in range(4):
            ngrp = (4 * j + 4) // 2
            for hp in range(0, H, 2):
                for gi in range(ngrp):
                    p2b_exp(j, hp, gi)
                    p2b_exp(j, hp + 1, gi)
                p2b_tail_s(j, hp, ngrp)
                p2b_tail_s(j, hp + 1, ngrp)
        if mp < 4:
            return
        ctx.wait(sca, "p2b.av.3.31.7")
        ctx.wait(sca, "p2b.otw.3.31")
        ctx.wait(sca, "p2b.otw.3.30")
        for ci in range(16, 32):
            ctx.op(lambda ci=ci: sca.dma_start(
                out=ot_sb[:, ci * 2048:(ci + 1) * 2048], in_=ot_d[ci, :, :]),
                S["s_ot"], 16, f"p2c.ot.{ci}")
        if mp < 5:
            return
        ctx.wait(sca, "p2d.stats.0")
        fin = emit_ln_scalar_pre(sca, 0)
        ctx.op(fin, ss, 1, "p2d.rstd.0")
        for s in range(NS):
            for b in (0, 1):
                ctx.wait(sca, f"p2d.tr.{s}.{b}")
                ctx.op(lambda s=s, b=b: sca.copy(
                    n2t2[s % 2][:, b * 1024:(b + 1) * 1024], pbh(b % 2, 1)),
                    ss, 1, f"p2d.n2ev.{s}.{b}")
            if s >= 1:
                ctx.wait(sca, f"p2d.htr.{s - 1}")
                ctx.op(lambda s=s: sca.copy(
                    hT_2[(s - 1) % 2], psE_bf[:, 1536:2048]),
                    ss, 1, f"p2d.htev.{s - 1}")
            for b in (2, 3):
                ctx.wait(sca, f"p2d.tr.{s}.{b}")
                ctx.op(lambda s=s, b=b: sca.copy(
                    n2t2[s % 2][:, b * 1024:(b + 1) * 1024], pbh(b % 2, 1)),
                    ss, 1, f"p2d.n2ev.{s}.{b}")
            if s + 1 < NS:
                ctx.wait(sca, f"p2d.stats.{s + 1}")
                fin = emit_ln_scalar_pre(sca, (s + 1) % 2)
                ctx.op(fin, ss, 1, f"p2d.rstd.{s + 1}")
            ctx.wait(sca, f"p2d.gmm.{s}")
            if s >= 1:
                ctx.wait(sca, f"p2d.g5.{s - 1}")
            ctx.op(lambda: sca.activation(out=probs, in_=psEa[:, 512:520],
                                          func=AF.Exp),
                   ss, 1, f"p2d.probs.{s}")
        ctx.wait(sca, f"p2d.htr.{NS - 1}")
        ctx.op(lambda: sca.copy(hT_2[(NS - 1) % 2], psE_bf[:, 1536:2048]),
               ss, 1, f"p2d.htev.{NS - 1}")
        if ctx.emit:
            for nm in ("s_ab", "s_ot"):
                if sems[nm].val > 0:
                    sca.wait_ge(sems[nm].h, sems[nm].val)

    def emit_vector(vec):
        def emit_ropek(s):
            bi = s % 2
            ctx.wait(vec, f"p1a.kmm.{s}")
            ctx.wait(vec, f"p1a.cs.{s}")
            if s >= 2:
                ctx.wait(vec, f"p1a.ktr.{s - 2}.0")
            kb = s % 2
            fin = emit_rope(vec, ps4[:, kb * 1024:(kb + 1) * 1024],
                            p1_cs[:, bi * 512:(bi + 1) * 512],
                            p1_tmp, krot2[bi], 8)
            ctx.op(fin, sv, 1, f"p1a.ropek.{s}")

        for s in range(NS):
            bi = s % 2
            ctx.wait(vec, f"p1a.x.{s}")
            if s >= 2:
                ctx.wait(vec, f"p1a.rstd.{s - 2}")
            fin = emit_bn(vec, xb_p1[:, bi * 4096:(bi + 1) * 4096], bi)
            ctx.op(fin, sv, 1, f"p1a.stats.{s}")
            ctx.wait(vec, f"p1a.rstd.{s}")
            if s >= 2:
                ctx.wait(vec, f"p1a.tr.{s - 2}.3")
            ctx.op(lambda bi=bi: vec.tensor_scalar(
                n1_2[bi], xb_p1[:, bi * 4096:(bi + 1) * 4096],
                statv(bi, 0), statv(bi, 3), ALU.subtract, ALU.mult),
                sv, 1, f"p1a.n1.{s}")
            if s >= 1:
                emit_ropek(s - 1)
        emit_ropek(NS - 1)

        if mp < 1:
            return
        sseq = list(range(12, 16)) + list(range(12))
        for si, s in enumerate(sseq):
            for dvb in range(2):
                q = (2 * s + dvb) % 4
                ctx.wait(vec, f"p1b.vmm0.{s}" if dvb == 0 else f"p1b.vmm.{s}")
                if si == 0 and dvb == 0:
                    ctx.wait(vec, "p1a.ktcw.2")
                    ctx.wait(vec, "p1a.ktcw.3")
                if si >= 2 and dvb == 0:
                    ctx.wait(vec, f"p1b.vcw.{sseq[si - 2]}")
                ctx.op(lambda s=s, dvb=dvb, q=q: vec.tensor_copy(
                    vcb[:, (s % 2) * 1024 + dvb * 512:
                        (s % 2) * 1024 + (dvb + 1) * 512],
                    ps4[:, q * 512:(q + 1) * 512]),
                    sv, 1, f"p1b.vev.{s}.{dvb}")

        if mp < 2:
            return
        prevroq = {}
        for db in range(8):
            for s in range(NS):
                bi = s % 2
                q = s % 4
                ctx.wait(vec, f"p2a.qmm.{db}.{s}")
                ctx.wait(vec, f"p2a.cs.{db}.{s}")
                if bi in prevroq:
                    ctx.wait(vec, prevroq[bi])
                fin = emit_rope(vec, ps4[:, q * 512:(q + 1) * 512],
                                p2a_cs[:, bi * 512:(bi + 1) * 512],
                                p2a_tmp, qrot2[bi], 4)
                ctx.op(fin, sv, 1, f"p2a.rope.{db}.{s}")
                prevroq[bi] = f"p2a.qtr.{db}.{s}"

        if mp < 3:
            return
        def p2b_masks(j, h):
            par = h % 2
            nck = 4 * j + 4
            for kc in range(4 * j, nck):
                gi = kc // 2
                half = gi % 2
                o = kc * 128 - j * 512
                ctx.wait(vec, f"p2b.exp.{j}.{h}.{gi}")
                sl_ = expT2[par][:, half * 1024 + (kc % 2) * 512 + o:
                                half * 1024 + (kc % 2) * 512 + o + 128]
                ctx.op(lambda sl_=sl_: vec.tensor_tensor(
                    sl_, sl_, tri[:], ALU.mult),
                    sv, 1, f"p2b.msk.{j}.{h}.{kc}")

        def p2b_outev(j, h):
            ctx.wait(vec, f"p2b.bcs.{j}.{h}")
            psAC = psA if h % 2 == 0 else psC
            use(("outw", h % 2), vec)
            ctx.op(lambda h=h, psAC=psAC: vec.tensor_tensor(
                outw[h % 2], psAC[:, 0:512], bc2[h % 2], ALU.mult),
                sv, 1, f"p2b.outev.{j}.{h}")
            rel(("outw", h % 2), f"p2b.otw.{j}.{h}")

        for j in range(4):
            for hp in range(0, H, 2):
                p2b_masks(j, hp)
                p2b_masks(j, hp + 1)
                p2b_outev(j, hp)
                p2b_outev(j, hp + 1)

        if mp < 4:
            return
        for db in range(8):
            for s in range(NS):
                q = s % 4
                bi = s % 2
                ctx.wait(vec, f"p2c.omm.{db}.{s}")
                ctx.wait(vec, f"p2c.x.{db}.{s}")
                use(("xpw", bi), vec)
                ctx.op(lambda q=q, bi=bi: vec.tensor_tensor(
                    xpw[bi], ps4[:, q * 512:(q + 1) * 512], xsb[bi], ALU.add),
                    sv, 1, f"p2c.ev.{db}.{s}")
                rel(("xpw", bi), f"p2c.xpw.{db}.{s}")

        if mp < 5:
            return
        if mp < 5:
            return
        def p2d_stats(s):
            ctx.wait(vec, f"p2d.xp.{s}")
            fin = emit_bn(vec, xp_sb[:, (s % 4) * 4096:(s % 4 + 1) * 4096],
                          s % 2)
            ctx.op(fin, sv, 1, f"p2d.stats.{s}")

        def p2d_n2(s):
            ctx.wait(vec, f"p2d.rstd.{s}")
            if s <= 1:
                ctx.wait(vec, "p2c.omm.7.15")
            if s >= 2:
                ctx.wait(vec, f"p2d.tr.{s - 2}.3")
            ctx.op(lambda s=s: vec.tensor_scalar(
                n2_2[s % 2], xp_sb[:, (s % 4) * 4096:(s % 4 + 1) * 4096],
                statv(s % 2, 0), statv(s % 2, 3), ALU.subtract, ALU.mult),
                sv, 1, f"p2d.n2.{s}")

        def p2d_gate_hsc(s):
            ctx.wait(vec, f"p2d.probs.{s}")
            if s >= 2:
                ctx.wait(vec, f"p2d.htr.{s - 2}")
            def vedge(f, tag, s=s):
                ctx.op(f, sv, 1, f"p2d.g{tag}.{s}")
                ctx.wait(vec, f"p2d.g{tag}.{s}")
            vedge(lambda: vec.max(ltm, probs), 1)
            vedge(lambda: vec.tensor_scalar(keept, probs, ltm[:, 1:2], None,
                                            ALU.is_ge), 2)
            vedge(lambda: vec.tensor_scalar(wsum, ltm[:, 0:1], ltm[:, 1:2],
                                            None, ALU.add), 3)
            vedge(lambda: vec.reciprocal(winv, wsum), 4)
            vedge(lambda: vec.tensor_tensor(wful, probs, keept, ALU.mult), 5)
            vedge(lambda: vec.tensor_scalar(wful, wful, winv, None, ALU.mult), 6)
            ctx.wait(vec, f"p2d.amm.{s}")
            for e in range(NE):
                ctx.op(lambda e=e, s=s: vec.tensor_scalar(
                    h_2[s % 2][:, e * 64:(e + 1) * 64],
                    psEa[:, e * 64:(e + 1) * 64],
                    wful[:, e:e + 1], None, ALU.mult),
                    sv, 1 if e == 7 else 0, f"p2d.hsc.{s}" if e == 7 else None)

        def p2d_bev(s):
            for db in range(8):
                q = db % 4
                ctx.wait(vec, f"p2d.bmm.{s}.{db}")
                if s >= 2 and db == 0:
                    ctx.wait(vec, f"out.{s - 2}")
                ctx.op(lambda s=s, db=db, q=q: vec.tensor_tensor(
                    outb[:, (s % 2) * 4096 + db * 512:
                         (s % 2) * 4096 + (db + 1) * 512],
                    ps4[:, q * 512:(q + 1) * 512],
                    xp_sb[:, (s % 4) * 4096 + db * 512:
                          (s % 4) * 4096 + (db + 1) * 512],
                    ALU.add),
                    sv, 1, f"p2d.bev.{s}.{db}")

        p2d_stats(0)
        p2d_n2(0)
        for s in range(NS):
            if s >= 1:
                p2d_gate_hsc(s - 1)
            if s + 1 < NS:
                p2d_stats(s + 1)
            if s >= 1:
                p2d_bev(s - 1)
            if s + 1 < NS:
                p2d_n2(s + 1)
        p2d_gate_hsc(NS - 1)
        p2d_bev(NS - 1)

    emitters = [emit_sync, emit_gpsimd, emit_tensor, emit_scalar, emit_vector]
    ctx.start_pass(emit=False)
    res.clear()
    for f in emitters:
        f(None)
    ctx.finish_pass1()

    sem_nums = sorted(s.h.num for s in sems.values())
    assert sem_nums == list(range(sem_nums[0], sem_nums[0] + len(sem_nums)))
    srange = range(sem_nums[0], sem_nums[-1] + 1)
    nc.all_engine_barrier()
    nc.gpsimd.dma_reset(srange)
    nc.gpsimd.sem_clear(srange)
    nc.all_engine_barrier()

    ctx.start_pass(emit=True)
    res.clear()
    with nc.Block() as block:
        @block.sync
        def _(sync):
            emit_sync(sync)

        @block.gpsimd
        def _(gp):
            emit_gpsimd(gp)

        @block.tensor
        def _(te):
            emit_tensor(te)

        @block.scalar
        def _(sca):
            emit_scalar(sca)

        @block.vector
        def _(vec):
            emit_vector(vec)

    return nc


def host_inputs(L0_attn_x, L0_attn_Wq, L0_attn_Wk, L0_attn_Wv, L0_attn_Wo,
                L0_ffn_gate_w, L0_ffn_lora_A, L0_ffn_lora_B):
    cos, sin = _llama3_rope_tables()
    gwt = np.ascontiguousarray(
        np.asarray(L0_ffn_gate_w, dtype=np.float32).T.reshape(32, 128, 8)
        .transpose(1, 0, 2).reshape(128, 256))
    shared = {
        "wqt": np.ascontiguousarray(np.asarray(L0_attn_Wq, np.float32).T),
        "wkt": np.ascontiguousarray(np.asarray(L0_attn_Wk, np.float32).T),
        "wvt": np.ascontiguousarray(np.asarray(L0_attn_Wv, np.float32).T),
        "wot": np.ascontiguousarray(np.asarray(L0_attn_Wo, np.float32).T),
        "gwt": gwt,
        "at": np.ascontiguousarray(
            np.asarray(L0_ffn_lora_A, np.float32).transpose(2, 0, 1)
            .reshape(D, NE * RK)),
        "bt": np.ascontiguousarray(
            np.asarray(L0_ffn_lora_B, np.float32).transpose(0, 2, 1)
            .reshape(NE * RK, D)),
        "cos": cos,
        "sin": sin,
        "idt_d": np.eye(128, dtype=BF),
        "tri_d": np.ascontiguousarray(
            (np.arange(128)[None, :] >= np.arange(128)[:, None]).astype(BF)),
        "ones2_d": np.ones((128, 128), dtype=BF),
        "cvec_d": np.full((128, 2), EPS, dtype=np.float32),
    }
    x = np.asarray(L0_attn_x, np.float32)
    in_maps = []
    for c in range(8):
        m = dict(shared)
        m["x"] = np.ascontiguousarray(x[c])
        in_maps.append(m)
    return in_maps


def kernel(L0_attn_x, L0_attn_Wq, L0_attn_Wk, L0_attn_Wv, L0_attn_Wo,
           L0_ffn_gate_w, L0_ffn_lora_A, L0_ffn_lora_B):
    in_maps = host_inputs(L0_attn_x, L0_attn_Wq, L0_attn_Wk, L0_attn_Wv,
                          L0_attn_Wo, L0_ffn_gate_w, L0_ffn_lora_A, L0_ffn_lora_B)
    nc = build_program(dbg=False)
    res = run_bass_kernel_spmd(nc, in_maps, core_ids=list(range(8)))
    out = np.stack([np.asarray(r["out"], dtype=np.float32) for r in res.results])
    return out


if __name__ == "__main__":
    nc = build_program(dbg=False)
    print("built ok")


# version 18
# speedup vs baseline: 1.0406x; 1.0406x over previous
import math
import sys

sys.path.insert(0, "/opt/trn_rl_repo")

import numpy as np
import ml_dtypes

import concourse.bass as bass
import concourse.mybir as mybir
from concourse.bass_utils import run_bass_kernel_spmd

F32 = mybir.dt.float32
BF16 = mybir.dt.bfloat16
AF = mybir.ActivationFunctionType
ALU = mybir.AluOpType
AX = mybir.AxisListType
BF = ml_dtypes.bfloat16

T = 2048
D = 4096
HD = 128
H = 32
KVH = 8
DKV = 1024
NE = 8
RK = 64
NS = 16
NG = 4
EPS = 1e-5
ISQ = 1.0 / math.sqrt(HD)


class Sem:
    def __init__(self, nc, name):
        self.h = nc.alloc_semaphore(name=name)
        self.name = name
        self.val = 0


class Ctx:

    def __init__(self, sems):
        self.sems = sems
        self.ck = {}
        self.first = True
        self.emit = False
        self.pending = []

    def start_pass(self, emit):
        self.emit = emit
        self.cnt = {}
        for s in self.sems.values():
            s.val = 0

    def nid(self, tag):
        self.cnt[tag] = self.cnt.get(tag, 0) + 1
        return f"{tag}.{self.cnt[tag]}" 

    def op(self, f, sem=None, n=0, name=None):
        ins = f() if self.emit else None
        if sem is not None and n > 0:
            sem.val += n
            if name is not None:
                if self.first:
                    assert name not in self.ck, f"dup ck {name}"
                    self.ck[name] = (sem.name, sem.val)
                else:
                    assert self.ck[name] == (sem.name, sem.val), (
                        f"ck mismatch {name}")
            if self.emit:
                ins.then_inc(sem.h, n)
        return ins

    def wait(self, eng, name):
        if not self.emit:
            self.pending.append(name)
            return
        sname, val = self.ck[name]
        eng.wait_ge(self.sems[sname].h, val)

    def finish_pass1(self):
        missing = [n for n in self.pending if n not in self.ck]
        assert not missing, f"unresolved checkpoints: {sorted(set(missing))[:10]}"
        self.first = False


def _llama3_rope_tables():
    base = 500000.0
    inv_freq = 1.0 / base ** (np.arange(0, HD, 2, dtype=np.float32) / HD)
    factor, low_f, high_f, old_ctx = 32.0, 1.0, 4.0, 8192
    low_wl, high_wl = old_ctx / low_f, old_ctx / high_f
    wavelen = 2 * math.pi / inv_freq
    inv_llama = np.where(wavelen > low_wl, inv_freq / factor, inv_freq)
    smooth = (old_ctx / wavelen - low_f) / (high_f - low_f)
    smoothed = (1 - smooth) * inv_llama / factor + smooth * inv_llama
    is_med = ~(wavelen < high_wl) & ~(wavelen > low_wl)
    inv = np.where(is_med, smoothed, inv_llama).astype(np.float32)
    pos = np.arange(T, dtype=np.float32)
    freqs = pos[:, None] * inv[None, :]
    emb = np.concatenate([freqs, freqs], axis=-1)
    return np.cos(emb).astype(np.float32), np.sin(emb).astype(np.float32)


def build_program(dbg=False, mp=5, ndb=8, p2as=5):
    nc = bass.Bass("TRN2", target_bir_lowering=False)

    X = nc.dram_tensor("x", [T, D], F32, kind="ExternalInput")
    WQT = nc.dram_tensor("wqt", [D, D], F32, kind="ExternalInput")
    WKT = nc.dram_tensor("wkt", [D, DKV], F32, kind="ExternalInput")
    WVT = nc.dram_tensor("wvt", [D, DKV], F32, kind="ExternalInput")
    WOT = nc.dram_tensor("wot", [D, D], F32, kind="ExternalInput")
    AT = nc.dram_tensor("at", [D, NE * RK], F32, kind="ExternalInput")
    GWT = nc.dram_tensor("gwt", [128, 256], F32, kind="ExternalInput")
    BT = nc.dram_tensor("bt", [NE * RK, D], F32, kind="ExternalInput")
    COS = nc.dram_tensor("cos", [T, HD], F32, kind="ExternalInput")
    SIN = nc.dram_tensor("sin", [T, HD], F32, kind="ExternalInput")
    IDT = nc.dram_tensor("idt_d", [128, 128], BF16, kind="ExternalInput")
    TRI = nc.dram_tensor("tri_d", [128, 128], BF16, kind="ExternalInput")
    ONES2 = nc.dram_tensor("ones2_d", [128, 128], BF16, kind="ExternalInput")
    CVEC = nc.dram_tensor("cvec_d", [128, 2], F32, kind="ExternalInput")
    OUT = nc.dram_tensor("out", [T, D], F32, kind="ExternalOutput")

    skind = "ExternalOutput" if dbg else "Internal"
    n1t_d = nc.dram_tensor("n1t_d", [32, 128, T], BF16, kind=skind)
    ktc_d = nc.dram_tensor("ktc_d", [KVH, 128, T], BF16, kind=skind)
    vc_d = nc.dram_tensor("vc_d", [NS, 128, DKV], BF16, kind=skind)
    qt_d = nc.dram_tensor("qt_d", [H, 128, T], BF16, kind=skind)
    ot_d = nc.dram_tensor("ot_d", [32, 128, T], BF16, kind=skind)
    xp_d = nc.dram_tensor("xp_d", [T, D], F32, kind=skind)

    BIG = nc.alloc_sbuf_tensor("big", [128, 65536], BF16)
    MED = nc.alloc_sbuf_tensor("med", [128, 32768], BF16)
    SMF = nc.alloc_sbuf_tensor("smf", [128, 2048], F32)
    SMB = nc.alloc_sbuf_tensor("smb", [128, 2048], BF16)
    STT = nc.alloc_sbuf_tensor("stt", [128, 160], F32)
    ident = nc.alloc_sbuf_tensor("ident", [128, 128], BF16)
    tri = nc.alloc_sbuf_tensor("tri", [128, 128], BF16)
    ones2 = nc.alloc_sbuf_tensor("ones2", [128, 128], BF16)
    cvec = nc.alloc_sbuf_tensor("cvec", [128, 2], F32)

    ps4 = nc.alloc_psum_tensor("ps4", [128, 2048], F32)
    psE = nc.alloc_psum_tensor("psE", [128, 1024], F32)
    psC = nc.alloc_psum_tensor("psC", [128, 512], F32)
    psD = nc.alloc_psum_tensor("psD", [128, 512], F32)
    psEa = psE.ap()
    psA = psEa[:, 0:512]
    psB = psEa[:, 512:1024]
    psE_bf = psEa.bitcast(BF16)

    pbA = psE_bf[:, 0:1024]
    pbB = psE_bf[:, 1024:2048]
    pbC = psC.ap().bitcast(BF16)
    pbD = psD.ap().bitcast(BF16)

    def pbh(half, ph=0):
        if ph == 0:
            return (pbA, pbC)[half]
        return (pbC, pbD)[half]

    def bigv(c0, c1):
        return BIG[:, c0:c1]

    wk_sb = bigv(0, 32768)
    wv_sb = bigv(0, 32768)
    n1tg = bigv(32768, 49152)
    ktcg = bigv(49152, 53248)
    krot2 = [bigv(53248, 54272), bigv(64512, 65536)]
    n1_2 = [bigv(54272, 58368), SMF.ap().bitcast(BF16)]
    p1_tmp = bigv(58368, 62464).bitcast(F32)
    p1_cs = bigv(62464, 64512).bitcast(F32)
    vcb = bigv(49152, 53248)
    xb_p1 = MED.ap().bitcast(F32)

    n1t_f = bigv(0, 65536)
    wq_sb = MED
    qrot2 = [SMB[:, 0:512], SMB[:, 1536:2048]]
    qtw = [SMB[:, 512:1024], SMB[:, 1024:1536]]
    p2a_tmp = SMF[:, 0:1024]
    p2a_cs = SMF[:, 1024:2048]

    ktc_sb = bigv(0, 16384)
    vc_sb = bigv(16384, 32768)
    qt_sb = [bigv(32768, 49152), bigv(49152, 65536)]
    expT2 = [MED[:, 16384:18432], MED[:, 20480:22528]]
    outw = [MED[:, 22528:23040], MED[:, 23040:23552]]
    bc2 = [SMF[:, 0:512], SMF[:, 1024:1536]]
    t_sb = SMF[:, 512:1024]

    ot_sb = bigv(0, 65536)
    wo_sb = MED
    xsb = [SMF[:, 0:512], SMF[:, 512:1024]]
    xpw = [SMF[:, 1024:1536], SMF[:, 1536:2048]]

    aT_sb = bigv(0, 16384)
    bT_sb = bigv(16384, 32768)
    outb = bigv(32768, 49152).bitcast(F32)
    n2t2 = [bigv(49152, 53248), bigv(53248, 57344)]
    n2_2 = [bigv(57344, 61440), bigv(61440, 65536)]
    xp_sb = MED.ap().bitcast(F32)
    gw_sb = SMF.ap().bitcast(BF16)[:, 3072:3328]
    h_2 = [SMB[:, 0:512], SMB[:, 512:1024]]
    hT_2 = [SMB[:, 1024:1536], SMB[:, 1536:2048]]

    def bnstv(par):
        return STT[:, par * 54: par * 54 + 48]

    def statv(par, i):
        return STT[:, par * 54 + 48 + i: par * 54 + 48 + i + 1]

    negm = STT[:, 108:109]
    m1 = STT[:, 109:110]
    m2 = STT[:, 110:111]
    wsum = STT[:, 111:112]
    winv = STT[:, 112:113]
    probs = STT[:, 114:122]
    ltm = STT[:, 122:130]
    pmask = STT[:, 130:138]
    keept = STT[:, 138:146]
    wful = STT[:, 146:154]

    sem_names = (["t", "v", "s", "ld_c", "ld_n1g", "ld_n1f", "ld_kv", "ld_ot",
                  "g_wk", "g_wv", "g_ab", "g_gw", "s_ab", "s_ot"]
                 + [f"{b}{i}" for b in ["ld_x", "ld_cs", "ld_qt", "ld_x2",
                                        "ld_xp", "g_out", "g_n1w", "g_kw",
                                        "g_vc", "g_wq", "g_qw", "g_ot",
                                        "g_wo", "g_xp"] for i in (0, 1)])
    sems = {n: Sem(nc, n) for n in sem_names}
    ctx = Ctx(sems)
    S = sems
    st_, sv, ss = sems["t"], sems["v"], sems["s"]

    res = {}

    def use(key, wait_eng):
        if res.get(key) is not None:
            ctx.wait(wait_eng, res[key])

    def rel(key, name):
        res[key] = name

    def emit_rope(vec, src_ap, cs_ap, tmp_ap, out_ap, nh):
        def v3(ap_, lo):
            return ap_.rearrange("p (h x) -> p h x", h=nh)[:, :, lo * 64:(lo + 1) * 64]

        def csb(lo, which):
            return cs_ap[:, which * 128 + lo * 64: which * 128 + (lo + 1) * 64] \
                .unsqueeze(1).broadcast_to([128, nh, 64])

        t1 = tmp_ap[:, 0: nh * 64].rearrange("p (h x) -> p h x", h=nh)
        t2 = tmp_ap[:, nh * 64: nh * 128].rearrange("p (h x) -> p h x", h=nh)
        ctx.op(lambda: vec.tensor_tensor(t1, v3(src_ap, 1), csb(0, 1), ALU.mult))
        ctx.op(lambda: vec.tensor_tensor(t2, v3(src_ap, 0), csb(0, 0), ALU.mult))
        ctx.op(lambda: vec.tensor_tensor(v3(out_ap, 0), t2, t1, ALU.subtract))
        ctx.op(lambda: vec.tensor_tensor(t1, v3(src_ap, 0), csb(1, 1), ALU.mult))
        ctx.op(lambda: vec.tensor_tensor(t2, v3(src_ap, 1), csb(1, 0), ALU.mult))
        return lambda: vec.tensor_tensor(v3(out_ap, 1), t2, t1, ALU.add)

    def emit_ln_scalar_pre(sca, par=0):
        nm = ctx.nid("lnln")
        ctx.op(lambda: sca.activation(out=statv(par, 2), in_=statv(par, 1),
                                      func=AF.Ln, bias=cvec[:, 0:1]), ss, 1, nm)
        ctx.wait(sca, nm)
        return lambda: sca.activation(out=statv(par, 3), in_=statv(par, 2),
                                      func=AF.Exp, scale=-0.5)

    def emit_bn(vec, x_ap, par=0):
        nm = ctx.nid("bnst")
        for i in range(8):
            ctx.op(lambda i=i, par=par: vec.bn_stats(
                bnstv(par)[:, i * 6:(i + 1) * 6],
                x_ap[:, i * 512:(i + 1) * 512]),
                sv, 1 if i == 7 else 0, nm if i == 7 else None)
        ctx.wait(vec, nm)
        return lambda: vec.bn_aggr(STT[:, par * 54 + 48: par * 54 + 50], bnstv(par))


    def sync_finish(sync):
        if ctx.emit:
            for nm, sm in sems.items():
                if nm.startswith("ld_") and sm.val > 0:
                    sync.wait_ge(sm.h, sm.val)

    def gp_finish(gp):
        if ctx.emit:
            for nm, sm in sems.items():
                if nm.startswith("g_") and sm.val > 0:
                    gp.wait_ge(sm.h, sm.val)

    def emit_sync(sync):
        consts = [(ident, IDT), (tri, TRI), (cvec, CVEC), (ones2, ONES2)]
        for i, (tns, src) in enumerate(consts):
            ctx.op(lambda tns=tns, src=src: sync.dma_start(out=tns[:], in_=src[:, :]),
                   S["ld_c"], 16, "consts" if i == 3 else None)

        for s in range(NS):
            t0 = s * 128
            bi = s % 2
            use(("xb", bi), sync)
            ctx.op(lambda bi=bi, t0=t0: sync.dma_start(
                out=xb_p1[:, bi * 4096:(bi + 1) * 4096], in_=X[t0:t0 + 128, :]),
                S[f"ld_x{bi}"], 16, f"p1a.x.{s}")
            rel(("xb", bi), f"p1a.n1.{s}")
            use(("cs1", bi), sync)
            ctx.op(lambda bi=bi, t0=t0: sync.dma_start(
                out=p1_cs[:, bi * 512: bi * 512 + 128], in_=COS[t0:t0 + 128, :]),
                S[f"ld_cs{bi}"], 16)
            ctx.op(lambda bi=bi, t0=t0: sync.dma_start(
                out=p1_cs[:, bi * 512 + 128: bi * 512 + 256], in_=SIN[t0:t0 + 128, :]),
                S[f"ld_cs{bi}"], 16, f"p1a.cs.{s}")
            rel(("cs1", bi), f"p1a.ropek.{s}")

        if mp < 1:
            sync_finish(sync)
            return
        for g in [0, 1, 2]:
            ctx.wait(sync, f"p1a.n1tw.{g}")
            if g == 0:
                ctx.wait(sync, f"p1b.vmm.15")
            else:
                ctx.wait(sync, f"p1b.vmm.{4 * g - 1}")
            for q4 in range(4):
                src = n1t_d[:, :, g * 512:(g + 1) * 512].rearrange(
                    "c p t -> p c t")[:, 8 * q4:8 * q4 + 8, :]
                dst = n1tg.rearrange("p (c t) -> p c t", c=32)[:, 8 * q4:8 * q4 + 8, :]
                ctx.op(lambda src=src, dst=dst: sync.dma_start(out=dst, in_=src),
                       S["ld_n1g"], 16, f"p1b.n1t.{g}" if q4 == 3 else None)

        if mp < 2:
            sync_finish(sync)
            return
        def n1f_ld(ci):
            ctx.op(lambda ci=ci: sync.dma_start(
                out=n1t_f[:, ci * 2048:(ci + 1) * 2048], in_=n1t_d[ci, :, :]),
                S["ld_n1f"], 16, f"p2a.n1t.{ci}")

        ctx.wait(sync, "p1a.n1tw.2")
        ctx.wait(sync, "p1a.n1tw.3")
        ctx.wait(sync, "p1b.vmm.11")
        for ci in range(0, 12):
            n1f_ld(ci)
        ctx.wait(sync, "p1b.vev.11.1")
        ctx.wait(sync, "p1b.vcw.11")
        ctx.wait(sync, "p1b.vcw.10")
        for ci in range(24, 32):
            n1f_ld(ci)
        for db in range(ndb):
            for s in range(NS):
                t0 = s * 128
                bi = s % 2
                use(("cs2", bi), sync)
                ctx.op(lambda bi=bi, t0=t0: sync.dma_start(
                    out=p2a_cs[:, bi * 512: bi * 512 + 128], in_=COS[t0:t0 + 128, :]),
                    S[f"ld_cs{bi}"], 16)
                ctx.op(lambda bi=bi, t0=t0: sync.dma_start(
                    out=p2a_cs[:, bi * 512 + 128: bi * 512 + 256],
                    in_=SIN[t0:t0 + 128, :]), S[f"ld_cs{bi}"], 16,
                    f"p2a.cs.{db}.{s}")
                rel(("cs2", bi), f"p2a.rope.{db}.{s}")

        if mp < 3:
            sync_finish(sync)
            return
        ctx.wait(sync, "p2a.qmm.7.15")
        ctx.wait(sync, "p2a.qtr.7.15")
        ctx.wait(sync, "p1a.ktcw.3")
        ctx.wait(sync, "p1a.ktcw.2")
        ctx.op(lambda: sync.dma_start(
            out=ktc_sb.rearrange("p (h t) -> p h t", h=KVH),
            in_=ktc_d.ap().rearrange("h p t -> p h t")), S["ld_kv"], 16)
        for q2 in range(2):
            ctx.op(lambda q2=q2: sync.dma_start(
                out=vc_sb.rearrange("p (s d) -> p s d", s=NS)[:, 8 * q2:8 * q2 + 8, :],
                in_=vc_d.ap().rearrange("s p d -> p s d")[:, 8 * q2:8 * q2 + 8, :]),
                S["ld_kv"], 16, "p2b.kv" if q2 == 1 else None)
        for j in range(4):
            ctx.wait(sync, "p2a.qtw.7.15")
            ctx.wait(sync, "p2a.qtw.7.14")
            if j >= 2:
                ctx.wait(sync, f"p2b.sc.{j - 2}.31.{2 * (j - 2) + 1}")
            for q4 in range(4):
                src = qt_d[:, :, j * 512:(j + 1) * 512].rearrange(
                    "h p t -> p h t")[:, 8 * q4:8 * q4 + 8, :]
                dst = qt_sb[j % 2].rearrange("p (h t) -> p h t", h=H)[
                    :, 8 * q4:8 * q4 + 8, :]
                ctx.op(lambda src=src, dst=dst: sync.dma_start(out=dst, in_=src),
                       S[f"ld_qt{j % 2}"], 16, f"p2b.qt.{j}" if q4 == 3 else None)

        if mp < 4:
            sync_finish(sync)
            return
        ctx.wait(sync, "p2b.av.3.31.7")
        ctx.wait(sync, "p2b.otw.3.31")
        ctx.wait(sync, "p2b.otw.3.30")
        for ci in range(16):
            ctx.op(lambda ci=ci: sync.dma_start(
                out=ot_sb[:, ci * 2048:(ci + 1) * 2048], in_=ot_d[ci, :, :]),
                S["ld_ot"], 16, f"p2c.ot.{ci}")
        for db in range(8):
            for s in range(NS):
                t0 = s * 128
                bi = s % 2
                if db == 0 and s == 0:
                    ctx.wait(sync, "p2b.outev.3.31")
                use(("xs", bi), sync)
                ctx.op(lambda db=db, t0=t0, bi=bi: sync.dma_start(
                    out=xsb[bi], in_=X[t0:t0 + 128, db * 512:(db + 1) * 512]),
                    S[f"ld_x2{bi}"], 16, f"p2c.x.{db}.{s}")
                rel(("xs", bi), f"p2c.ev.{db}.{s}")

        if mp < 5:
            sync_finish(sync)
            return
        for s in range(NS):
            sl4 = s % 4
            ctx.wait(sync, f"p2c.xpw.7.{s}")
            if s < 2:
                ctx.wait(sync, "p2c.omm.6.15")
            elif s < 4:
                ctx.wait(sync, "p2c.omm.7.15")
            else:
                ctx.wait(sync, f"p2d.bev.{s - 4}.7")
            ctx.op(lambda s=s, sl4=sl4: sync.dma_start(
                out=xp_sb[:, sl4 * 4096:(sl4 + 1) * 4096],
                in_=xp_d[s * 128:(s + 1) * 128, :]),
                S[f"ld_xp{s % 2}"], 16, f"p2d.xp.{s}")
        sync_finish(sync)

    def emit_gpsimd(gp):
        for i in range(4):
            src = WKT.ap().rearrange("(c p) d -> p c d", p=128)[:, 8 * i: 8 * i + 8, :]
            dst = wk_sb.rearrange("p (c d) -> p c d", c=32)[:, 8 * i: 8 * i + 8, :]
            ctx.op(lambda src=src, dst=dst: gp.dma_start(out=dst, in_=src),
                   S["g_wk"], 16, f"p1a.wk.{i}")
        for g in range(NG):
            ctx.wait(gp, f"p1a.n1ev.{4 * g + 3}.3")
            for q4 in range(4):
                dsta = n1t_d[:, :, g * 512:(g + 1) * 512].rearrange(
                    "c p t -> p c t")[:, 8 * q4:8 * q4 + 8, :]
                srca = n1tg.rearrange("p (c t) -> p c t", c=32)[:, 8 * q4:8 * q4 + 8, :]
                ctx.op(lambda dsta=dsta, srca=srca: gp.dma_start(out=dsta, in_=srca),
                       S[f"g_n1w{g % 2}"], 16, f"p1a.n1tw.{g}" if q4 == 3 else None)
            ctx.wait(gp, f"p1a.kev.{4 * g + 3}.0")
            ctx.op(lambda g=g: gp.dma_start(
                out=ktc_d[:, :, g * 512:(g + 1) * 512].rearrange("h p t -> p h t"),
                in_=ktcg.rearrange("p (h t) -> p h t", h=KVH)),
                S[f"g_kw{g % 2}"], 16, f"p1a.ktcw.{g}")

        if mp < 1:
            gp_finish(gp)
            return
        ctx.wait(gp, "p1a.kmm.15")
        for i in range(4):
            src = WVT.ap().rearrange("(c p) d -> p c d", p=128)[:, 8 * i: 8 * i + 8, :]
            dst = wv_sb.rearrange("p (c d) -> p c d", c=32)[:, 8 * i: 8 * i + 8, :]
            ctx.op(lambda src=src, dst=dst: gp.dma_start(out=dst, in_=src),
                   S["g_wv"], 16, f"p1b.wv.{i}")
        for s in list(range(12, 16)) + list(range(12)):
            ctx.wait(gp, f"p1b.vev.{s}.1")
            ctx.op(lambda s=s: gp.dma_start(
                out=vc_d[s, :, :], in_=vcb[:, (s % 2) * 1024:(s % 2 + 1) * 1024]),
                S[f"g_vc{s % 2}"], 16, f"p1b.vcw.{s}")

        if mp < 2:
            gp_finish(gp)
            return
        def load_wq(db):
            for hh in range(2):
                if db >= 2:
                    ctx.wait(gp, f"p2a.qmm.{db - 2}.15")
                elif db == 0 and hh == 0:
                    ctx.wait(gp, "p1a.n1.15")
                for q4 in range(4):
                    src = WQT[:, db * 512 + hh * 256: db * 512 + (hh + 1) * 256] \
                        .rearrange("(c p) d -> p c d", p=128)[:, 8 * q4:8 * q4 + 8, :]
                    dst = wq_sb[:, (db % 2) * 16384:(db % 2 + 1) * 16384] \
                        .rearrange("p (c d) -> p c d", c=32)[
                        :, 8 * q4:8 * q4 + 8, hh * 256:(hh + 1) * 256]
                    ctx.op(lambda src=src, dst=dst: gp.dma_start(out=dst, in_=src),
                           S[f"g_wq{db % 2}"], 16,
                           f"p2a.wq.{db}.{hh}" if q4 == 3 else None)

        load_wq(0)
        if ndb > 1:
            load_wq(1)
        for db in range(ndb):
            if p2as >= 5:
                for s in range(NS):
                    ctx.wait(gp, f"p2a.qtev.{db}.{s}")
                    hbase = db * 4
                    dst = qt_d[hbase:hbase + 4, :, s * 128:(s + 1) * 128] \
                        .rearrange("h p t -> p h t")
                    src = qtw[s % 2].rearrange("p (h t) -> p h t", h=4)
                    ctx.op(lambda dst=dst, src=src: gp.dma_start(out=dst, in_=src),
                           S[f"g_qw{s % 2}"], 16, f"p2a.qtw.{db}.{s}")
            if db + 2 <= ndb - 1:
                load_wq(db + 2)

        if mp < 3:
            gp_finish(gp)
            return
        def load_wo(db):
            for hh in range(2):
                if db >= 2:
                    ctx.wait(gp, f"p2c.omm.{db - 2}.15")
                elif db == 0 and hh == 0:
                    ctx.wait(gp, "p2a.qmm.7.15")
                elif db == 1 and hh == 0:
                    ctx.wait(gp, "p2b.otw.3.31")
                    ctx.wait(gp, "p2b.otw.3.30")
                    ctx.wait(gp, "p2b.av.3.31.7")
                for q4 in range(4):
                    src = WOT[:, db * 512 + hh * 256: db * 512 + (hh + 1) * 256] \
                        .rearrange("(c p) d -> p c d", p=128)[:, 8 * q4:8 * q4 + 8, :]
                    dst = wo_sb[:, (db % 2) * 16384:(db % 2 + 1) * 16384] \
                        .rearrange("p (c d) -> p c d", c=32)[
                        :, 8 * q4:8 * q4 + 8, hh * 256:(hh + 1) * 256]
                    ctx.op(lambda src=src, dst=dst: gp.dma_start(out=dst, in_=src),
                           S[f"g_wo{db % 2}"], 16,
                           f"p2c.wo.{db}.{hh}" if q4 == 3 else None)

        load_wo(0)
        for j in range(4):
            for h in range(H):
                ctx.wait(gp, f"p2b.outev.{j}.{h}")
                ctx.op(lambda j=j, h=h: gp.dma_start(
                    out=ot_d[h, :, j * 512:(j + 1) * 512], in_=outw[h % 2]),
                    S[f"g_ot{h % 2}"], 16, f"p2b.otw.{j}.{h}")

        if mp < 4:
            gp_finish(gp)
            return
        load_wo(1)
        for db in range(8):
            for s in range(NS):
                t0 = s * 128
                ctx.wait(gp, f"p2c.ev.{db}.{s}")
                ctx.op(lambda db=db, t0=t0, s=s: gp.dma_start(
                    out=xp_d[t0:t0 + 128, db * 512:(db + 1) * 512], in_=xpw[s % 2]),
                    S[f"g_xp{s % 2}"], 16, f"p2c.xpw.{db}.{s}")
            if db + 2 <= 7:
                load_wo(db + 2)

        if mp < 5:
            gp_finish(gp)
            return
        ctx.wait(gp, "p2c.xpw.7.15")
        ctx.op(lambda: gp.dma_start(out=gw_sb, in_=GWT[:, :]),
               S["g_gw"], 16, "p2d.gw")
        ctx.wait(gp, "p2c.omm.7.15")
        for q4 in range(4):
            src = AT.ap().rearrange("(c p) r -> p c r", p=128)[:, 8 * q4:8 * q4 + 8, :]
            dst = aT_sb.rearrange("p (c r) -> p c r", c=32)[:, 8 * q4:8 * q4 + 8, :]
            ctx.op(lambda src=src, dst=dst: gp.dma_start(out=dst, in_=src),
                   S["g_ab"], 16, "p2d.at" if q4 == 3 else None)
        src = BT.ap().rearrange("(c p) d -> p c d", p=128)
        dst = bT_sb.rearrange("p (c d) -> p c d", c=4)
        ctx.op(lambda src=src, dst=dst: gp.dma_start(out=dst, in_=src),
               S["g_ab"], 16, "p2d.bt")
        for s in range(NS):
            ctx.wait(gp, f"p2d.bev.{s}.7")
            ctx.op(lambda s=s: gp.dma_start(
                out=OUT[s * 128:(s + 1) * 128, :],
                in_=outb[:, (s % 2) * 4096:(s % 2 + 1) * 4096]),
                S[f"g_out{s % 2}"], 16, f"out.{s}")
        gp_finish(gp)

    def emit_tensor(te):
        ctx.wait(te, "consts")
        def emit_ktr(s):
            kbi = s % 2
            half = s % 2
            ctx.wait(te, f"p1a.ropek.{s}")
            use(("pb1", half), te)
            for i in range(8):
                ctx.op(lambda half=half, i=i, kbi=kbi: te.transpose(
                    pbh(half, 0)[:, i * 128:(i + 1) * 128],
                    krot2[kbi][:, i * 128:(i + 1) * 128], ident[:]),
                    st_, 1 if i == 7 else 0,
                    f"p1a.ktr.{s}.0" if i == 7 else None)
            rel(("pb1", half), f"p1a.kev.{s}.0")

        for s in range(NS):
            g, sl = s // 4, s % 4
            bi = s % 2
            for b in range(4):
                if b == 0:
                    ctx.wait(te, f"p1a.n1.{s}")
                half = b % 2
                use(("pb1", half), te)
                for i in range(8):
                    ci = 8 * b + i
                    ctx.op(lambda half=half, ci=ci, i=i, bi=bi: te.transpose(
                        pbh(half, 0)[:, i * 128:(i + 1) * 128],
                        n1_2[bi][:, ci * 128:(ci + 1) * 128], ident[:]),
                        st_, 1 if i == 7 else 0,
                        f"p1a.tr.{s}.{b}" if i == 7 else None)
                rel(("pb1", half), f"p1a.n1ev.{s}.{b}")
            if s >= 1:
                emit_ktr(s - 1)
            ctx.wait(te, f"p1a.n1ev.{s}.3")
            if s == 0:
                ctx.wait(te, "p1a.wk.3")
            kb = s % 2
            use(("kps", kb), te)
            for dkb in range(2):
                for ci in range(32):
                    last = (dkb == 1 and ci == 31)
                    ctx.op(lambda kb=kb, dkb=dkb, ci=ci, sl=sl: te.matmul(
                        ps4[:, kb * 1024 + dkb * 512: kb * 1024 + (dkb + 1) * 512],
                        n1tg[:, ci * 512 + sl * 128: ci * 512 + sl * 128 + 128],
                        wk_sb[:, ci * 1024 + dkb * 512: ci * 1024 + (dkb + 1) * 512],
                        start=(ci == 0), stop=(ci == 31)),
                        st_, 1 if last else 0, f"p1a.kmm.{s}" if last else None)
            rel(("kps", kb), f"p1a.ropek.{s}")
        emit_ktr(NS - 1)

        if mp < 1:
            return
        rel(("ps4q", 0), "p1a.ropek.14")
        rel(("ps4q", 1), "p1a.ropek.14")
        rel(("ps4q", 2), "p1a.ropek.15")
        rel(("ps4q", 3), "p1a.ropek.15")
        first = True
        for s in list(range(12, 16)) + list(range(12)):
            g, sl = s // 4, s % 4
            if sl == 0 and g != 3:
                ctx.wait(te, f"p1b.n1t.{g}")
            if first:
                ctx.wait(te, "p1b.wv.3")
                ctx.wait(te, "p1a.kmm.15")
                first = False
            for dvb in range(2):
                q = (2 * s + dvb) % 4
                use(("ps4q", q), te)
                for ci in range(32):
                    name = (f"p1b.vmm.{s}" if dvb == 1 else f"p1b.vmm0.{s}") \
                        if ci == 31 else None
                    ctx.op(lambda q=q, ci=ci, sl=sl, dvb=dvb: te.matmul(
                        ps4[:, q * 512:(q + 1) * 512],
                        n1tg[:, ci * 512 + sl * 128: ci * 512 + sl * 128 + 128],
                        wv_sb[:, ci * 1024 + dvb * 512: ci * 1024 + (dvb + 1) * 512],
                        start=(ci == 0), stop=(ci == 31)),
                        st_, 1 if ci == 31 else 0, name)
                rel(("ps4q", q), f"p1b.vev.{s}.{dvb}")

        if mp < 2:
            return
        def emit_qtr(db, s):
            ctx.wait(te, f"p2a.rope.{db}.{s}")
            half = s % 2
            use(("pb1", half), te)
            for i in range(4):
                ctx.op(lambda half=half, i=i, s=s: te.transpose(
                    pbh(half, 0)[:, i * 128:(i + 1) * 128],
                    qrot2[s % 2][:, i * 128:(i + 1) * 128], ident[:]),
                    st_, 1 if i == 3 else 0,
                    f"p2a.qtr.{db}.{s}" if i == 3 else None)
            rel(("pb1", half), f"p2a.qtev.{db}.{s}")

        prev_q = None
        for db in range(8):
            for g in range(NG):
                for sl in range(4):
                    s = g * 4 + sl
                    if g == 0 and sl == 0:
                        ctx.wait(te, f"p2a.wq.{db}.1")
                    q = s % 4
                    use(("ps4q", q), te)
                    for ci in range(32):
                        if db == 0 and g == 0 and sl == 0:
                            ctx.wait(te, f"p2a.n1t.{ci}")
                        ctx.op(lambda q=q, ci=ci, s=s, db=db: te.matmul(
                            ps4[:, q * 512:(q + 1) * 512],
                            n1t_f[:, ci * 2048 + s * 128:
                                  ci * 2048 + s * 128 + 128],
                            wq_sb[:, (db % 2) * 16384 + ci * 512:
                                  (db % 2) * 16384 + (ci + 1) * 512],
                            start=(ci == 0), stop=(ci == 31)),
                            st_, 1 if ci == 31 else 0,
                            f"p2a.qmm.{db}.{s}" if ci == 31 else None)
                    rel(("ps4q", q), f"p2a.rope.{db}.{s}")
                    if prev_q is not None:
                        emit_qtr(*prev_q)
                    prev_q = (db, s)
        emit_qtr(*prev_q)


        if mp < 3:
            return
        rel(("sc", 0), "p2a.rope.7.13")
        rel(("sc", 1), "p2a.rope.7.15")
        rel(("psA",), res.get(("pb1", 0)))
        rel(("psC",), res.get(("pb1", 1)))
        rel(("psB",), "p2d.none" if False else None)
        res[("psB",)] = res.get(("psB",))
        def p2b_scores(j, h, gi):
            par = h % 2
            kvh = h // 4
            use(("sc", par), te)
            for p in range(2):
                kc = 2 * gi + p
                o = max(0, kc * 128 - j * 512)
                ctx.op(lambda par=par, p=p, kc=kc, kvh=kvh, h=h, j=j,
                       o=o: te.matmul(
                           ps4[:, par * 1024 + p * 512 + o:
                               par * 1024 + (p + 1) * 512],
                           ktc_sb[:, kvh * 2048 + kc * 128:
                                  kvh * 2048 + (kc + 1) * 128],
                           qt_sb[j % 2][:, h * 512 + o:(h + 1) * 512],
                           start=True, stop=True),
                       st_, 1 if p == 1 else 0,
                       f"p2b.sc.{j}.{h}.{gi}" if p == 1 else None)
            rel(("sc", par), f"p2b.exp.{j}.{h}.{gi}")

        def p2b_avsums(j, h, gi, nck):
            par = h % 2
            kvh = h // 4
            half = gi % 2
            psAC, kA = (psA, ("psA",)) if par == 0 else (psC, ("psC",))
            psBD, kB = (psB, ("psB",)) if par == 0 else (psD, ("psD",))
            ctx.wait(te, f"p2b.exp.{j}.{h}.{gi}")
            for p in range(2):
                kc = 2 * gi + p
                diag = kc >= 4 * j
                o = kc * 128 - j * 512 if diag else 0
                if diag:
                    ctx.wait(te, f"p2b.msk.{j}.{h}.{kc}")
                if kc == 0:
                    use(kA, te)
                    use(kB, te)
                last = (kc == nck - 1)
                e_sl = expT2[par][:, half * 1024 + p * 512 + o:
                                 half * 1024 + (p + 1) * 512]
                ctx.op(lambda kc=kc, o=o, e_sl=e_sl, kvh=kvh, last=last,
                       psAC=psAC: te.matmul(
                           psAC[:, o:512],
                           vc_sb[:, kc * 1024 + kvh * 128:
                                 kc * 1024 + (kvh + 1) * 128],
                           e_sl, start=(kc == 0), stop=last),
                       st_, 0)
                ctx.op(lambda kc=kc, o=o, e_sl=e_sl, last=last,
                       psBD=psBD: te.matmul(
                    psBD[:, o:512], ones2[:], e_sl,
                    start=(kc == 0), stop=last),
                    st_, 1 if p == 1 else 0,
                    f"p2b.av.{j}.{h}.{gi}" if p == 1 else None)
            if gi == nck // 2 - 1:
                rel(kA, f"p2b.outev.{j}.{h}")
                rel(kB, f"p2b.lns.{j}.{h}")

        for j in range(4):
            nck = 4 * j + 4
            ngrp = nck // 2
            for hp in range(0, H, 2):
                if hp == 0:
                    ctx.wait(te, f"p2b.qt.{j}")
                    if j == 0:
                        ctx.wait(te, "p2b.kv")
                for gi in range(ngrp):
                    p2b_scores(j, hp, gi)
                    p2b_scores(j, hp + 1, gi)
                    if gi > 0:
                        p2b_avsums(j, hp, gi - 1, nck)
                        p2b_avsums(j, hp + 1, gi - 1, nck)
                p2b_avsums(j, hp, ngrp - 1, nck)
                p2b_avsums(j, hp + 1, ngrp - 1, nck)


        if mp < 4:
            return
        rel(("ps4q", 0), "p2b.exp.3.30.7")
        rel(("ps4q", 1), "p2b.exp.3.30.7")
        rel(("ps4q", 2), "p2b.exp.3.31.7")
        rel(("ps4q", 3), "p2b.exp.3.31.7")
        for db in range(8):
            for s in range(NS):
                t0 = s * 128
                if s == 0:
                    ctx.wait(te, f"p2c.wo.{db}.1")
                q = s % 4
                use(("ps4q", q), te)
                for ci in range(32):
                    if db == 0 and s == 0:
                        ctx.wait(te, f"p2c.ot.{ci}")
                    ctx.op(lambda q=q, ci=ci, t0=t0, db=db: te.matmul(
                        ps4[:, q * 512:(q + 1) * 512],
                        ot_sb[:, ci * 2048 + t0: ci * 2048 + t0 + 128],
                        wo_sb[:, (db % 2) * 16384 + ci * 512:
                              (db % 2) * 16384 + (ci + 1) * 512],
                        start=(ci == 0), stop=(ci == 31)),
                        st_, 1 if ci == 31 else 0,
                        f"p2c.omm.{db}.{s}" if ci == 31 else None)
                rel(("ps4q", q), f"p2c.ev.{db}.{s}")

        if mp < 5:
            return
        rel(("pb1", 0), res.get(("psC",)))
        rel(("pb1", 1), res.get(("psD",)))

        def p2d_tr(s):
            for b in range(4):
                if b == 0:
                    ctx.wait(te, f"p2d.n2.{s}")
                half = b % 2
                use(("pb1", half), te)
                for i in range(8):
                    ci = 8 * b + i
                    ctx.op(lambda half=half, ci=ci, i=i, s=s: te.transpose(
                        pbh(half, 1)[:, i * 128:(i + 1) * 128],
                        n2_2[s % 2][:, ci * 128:(ci + 1) * 128], ident[:]),
                        st_, 1 if i == 7 else 0,
                        f"p2d.tr.{s}.{b}" if i == 7 else None)
                rel(("pb1", half), f"p2d.n2ev.{s}.{b}")

        def p2d_gmm(s):
            ctx.wait(te, f"p2d.n2ev.{s}.3")
            if s == 0:
                ctx.wait(te, "p2d.gw")
                use(("psB",), te)
            else:
                ctx.wait(te, f"p2d.probs.{s - 1}")
            for ci in range(32):
                ctx.op(lambda ci=ci, s=s: te.matmul(
                    psEa[:, 512:520], n2t2[s % 2][:, ci * 128:(ci + 1) * 128],
                    gw_sb[:, ci * 8:(ci + 1) * 8],
                    start=(ci == 0), stop=(ci == 31)),
                    st_, 1 if ci == 31 else 0, f"p2d.gmm.{s}" if ci == 31 else None)

        def p2d_amm(s):
            if s == 0:
                ctx.wait(te, "p2d.at")
                use(("psA",), te)
            for ci in range(32):
                ctx.op(lambda ci=ci, s=s: te.matmul(
                    psEa[:, 0:512], n2t2[s % 2][:, ci * 128:(ci + 1) * 128],
                    aT_sb[:, ci * 512:(ci + 1) * 512],
                    start=(ci == 0), stop=(ci == 31)),
                    st_, 1 if ci == 31 else 0, f"p2d.amm.{s}" if ci == 31 else None)

        def p2d_htr(s):
            ctx.wait(te, f"p2d.hsc.{s}")
            if s >= 1:
                ctx.wait(te, f"p2d.htev.{s - 1}")
            for i in range(4):
                ctx.op(lambda i=i, s=s: te.transpose(
                    psE_bf[:, 1536 + i * 128:1536 + (i + 1) * 128],
                    h_2[s % 2][:, i * 128:(i + 1) * 128], ident[:]),
                    st_, 1 if i == 3 else 0, f"p2d.htr.{s}" if i == 3 else None)

        def p2d_bmm(s):
            ctx.wait(te, f"p2d.htev.{s}")
            if s == 0:
                ctx.wait(te, "p2d.bt")
            for db in range(8):
                q = db % 4
                use(("ps4q", q), te)
                for rc in range(4):
                    ctx.op(lambda q=q, rc=rc, db=db, s=s: te.matmul(
                        ps4[:, q * 512:(q + 1) * 512],
                        hT_2[s % 2][:, rc * 128:(rc + 1) * 128],
                        bT_sb[:, rc * 4096 + db * 512: rc * 4096 + (db + 1) * 512],
                        start=(rc == 0), stop=(rc == 3)),
                        st_, 1 if rc == 3 else 0,
                        f"p2d.bmm.{s}.{db}" if rc == 3 else None)
                rel(("ps4q", q), f"p2d.bev.{s}.{db}")

        for s in range(NS):
            p2d_tr(s)
            if s >= 1:
                p2d_htr(s - 1)
                p2d_bmm(s - 1)
            p2d_gmm(s)
            p2d_amm(s)
        p2d_htr(NS - 1)
        p2d_bmm(NS - 1)

    def emit_scalar(sca):
        for s in range(NS):
            ctx.wait(sca, f"p1a.stats.{s}")
            fin = emit_ln_scalar_pre(sca, s % 2)
            ctx.op(fin, ss, 1, f"p1a.rstd.{s}")
            for b in range(4):
                half = b % 2
                ctx.wait(sca, f"p1a.tr.{s}.{b}")
                if s % 4 == 0 and b == 0 and s >= 4:
                    ctx.wait(sca, f"p1a.n1tw.{s // 4 - 1}")
                dst = n1tg.rearrange("p (c t) -> p c t", c=32)[
                    :, 8 * b: 8 * b + 8, (s % 4) * 128:(s % 4 + 1) * 128]
                srcp = pbh(half, 0).rearrange("p (c t) -> p c t", c=8)
                ctx.op(lambda dst=dst, srcp=srcp: sca.copy(dst, srcp),
                       ss, 1, f"p1a.n1ev.{s}.{b}")
            for sk in ([s - 1] if s >= 1 else []) + \
                    ([NS - 1] if s == NS - 1 else []):
                half = sk % 2
                ctx.wait(sca, f"p1a.ktr.{sk}.0")
                if sk % 4 == 0 and sk >= 4:
                    ctx.wait(sca, f"p1a.ktcw.{sk // 4 - 1}")
                dst = ktcg.rearrange("p (h t) -> p h t", h=KVH)[
                    :, :, (sk % 4) * 128:(sk % 4 + 1) * 128]
                srcp = pbh(half, 0).rearrange("p (h t) -> p h t", h=8)
                ctx.op(lambda dst=dst, srcp=srcp: sca.copy(dst, srcp),
                       ss, 1, f"p1a.kev.{sk}.0")
        if mp < 3:
            return
        ctx.wait(sca, "p1b.vmm.11")
        for ci in range(12, 24):
            ctx.op(lambda ci=ci: sca.dma_start(
                out=n1t_f[:, ci * 2048:(ci + 1) * 2048], in_=n1t_d[ci, :, :]),
                S["s_ab"], 16, f"p2a.n1t.{ci}")
        for db in range(8):
            for s in range(NS):
                bi = s % 2
                half = s % 2
                ctx.wait(sca, f"p2a.qtr.{db}.{s}")
                use(("qtw", bi), sca)
                ctx.op(lambda bi=bi, half=half: sca.copy(
                    qtw[bi], pbh(half, 0)[:, 0:512]),
                    ss, 1, f"p2a.qtev.{db}.{s}")
                rel(("qtw", bi), f"p2a.qtw.{db}.{s}")
        def p2b_exp(j, h, gi):
            par = h % 2
            half = gi % 2
            ctx.wait(sca, f"p2b.sc.{j}.{h}.{gi}")
            use(("expT", par, half), sca)
            o0 = max(0, 2 * gi * 128 - j * 512)
            o1 = max(0, (2 * gi + 1) * 128 - j * 512)
            if o0 == 0 and o1 == 0:
                ctx.op(lambda par=par, half=half: sca.activation(
                    out=expT2[par][:, half * 1024:(half + 1) * 1024],
                    in_=ps4[:, par * 1024:(par + 1) * 1024],
                    func=AF.Exp, scale=ISQ),
                    ss, 1, f"p2b.exp.{j}.{h}.{gi}")
            else:
                for p, o in ((0, o0), (1, o1)):
                    ctx.op(lambda par=par, half=half, p=p, o=o: sca.activation(
                        out=expT2[par][:, half * 1024 + p * 512 + o:
                                       half * 1024 + (p + 1) * 512],
                        in_=ps4[:, par * 1024 + p * 512 + o:
                                par * 1024 + (p + 1) * 512],
                        func=AF.Exp, scale=ISQ),
                        ss, 1 if p == 1 else 0,
                        f"p2b.exp.{j}.{h}.{gi}" if p == 1 else None)
            rel(("expT", par, half), f"p2b.av.{j}.{h}.{gi}")

        def p2b_tail_s(j, h, ngrp):
            psBD = psB if h % 2 == 0 else psD
            ctx.wait(sca, f"p2b.av.{j}.{h}.{ngrp - 1}")
            ctx.op(lambda psBD=psBD: sca.activation(
                out=t_sb, in_=psBD[:, 0:512], func=AF.Ln),
                ss, 1, f"p2b.lns.{j}.{h}")
            use(("bcs", h % 2), sca)
            ctx.op(lambda h=h: sca.activation(
                out=bc2[h % 2], in_=t_sb, func=AF.Exp, scale=-1.0),
                ss, 1, f"p2b.bcs.{j}.{h}")
            rel(("bcs", h % 2), f"p2b.outev.{j}.{h}")

        for j in range(4):
            ngrp = (4 * j + 4) // 2
            for hp in range(0, H, 2):
                for gi in range(ngrp):
                    p2b_exp(j, hp, gi)
                    p2b_exp(j, hp + 1, gi)
                p2b_tail_s(j, hp, ngrp)
                p2b_tail_s(j, hp + 1, ngrp)
        if mp < 4:
            return
        ctx.wait(sca, "p2b.av.3.31.7")
        ctx.wait(sca, "p2b.otw.3.31")
        ctx.wait(sca, "p2b.otw.3.30")
        for ci in range(16, 32):
            ctx.op(lambda ci=ci: sca.dma_start(
                out=ot_sb[:, ci * 2048:(ci + 1) * 2048], in_=ot_d[ci, :, :]),
                S["s_ot"], 16, f"p2c.ot.{ci}")
        if mp < 5:
            return
        ctx.wait(sca, "p2d.stats.0")
        fin = emit_ln_scalar_pre(sca, 0)
        ctx.op(fin, ss, 1, "p2d.rstd.0")
        for s in range(NS):
            for b in (0, 1):
                ctx.wait(sca, f"p2d.tr.{s}.{b}")
                ctx.op(lambda s=s, b=b: sca.copy(
                    n2t2[s % 2][:, b * 1024:(b + 1) * 1024], pbh(b % 2, 1)),
                    ss, 1, f"p2d.n2ev.{s}.{b}")
            if s >= 1:
                ctx.wait(sca, f"p2d.htr.{s - 1}")
                ctx.op(lambda s=s: sca.copy(
                    hT_2[(s - 1) % 2], psE_bf[:, 1536:2048]),
                    ss, 1, f"p2d.htev.{s - 1}")
            for b in (2, 3):
                ctx.wait(sca, f"p2d.tr.{s}.{b}")
                ctx.op(lambda s=s, b=b: sca.copy(
                    n2t2[s % 2][:, b * 1024:(b + 1) * 1024], pbh(b % 2, 1)),
                    ss, 1, f"p2d.n2ev.{s}.{b}")
            if s + 1 < NS:
                ctx.wait(sca, f"p2d.stats.{s + 1}")
                fin = emit_ln_scalar_pre(sca, (s + 1) % 2)
                ctx.op(fin, ss, 1, f"p2d.rstd.{s + 1}")
            ctx.wait(sca, f"p2d.gmm.{s}")
            if s >= 1:
                ctx.wait(sca, f"p2d.g5.{s - 1}")
            ctx.op(lambda: sca.activation(out=probs, in_=psEa[:, 512:520],
                                          func=AF.Exp),
                   ss, 1, f"p2d.probs.{s}")
        ctx.wait(sca, f"p2d.htr.{NS - 1}")
        ctx.op(lambda: sca.copy(hT_2[(NS - 1) % 2], psE_bf[:, 1536:2048]),
               ss, 1, f"p2d.htev.{NS - 1}")
        if ctx.emit:
            for nm in ("s_ab", "s_ot"):
                if sems[nm].val > 0:
                    sca.wait_ge(sems[nm].h, sems[nm].val)

    def emit_vector(vec):
        def emit_ropek(s):
            bi = s % 2
            ctx.wait(vec, f"p1a.kmm.{s}")
            ctx.wait(vec, f"p1a.cs.{s}")
            if s >= 2:
                ctx.wait(vec, f"p1a.ktr.{s - 2}.0")
            kb = s % 2
            fin = emit_rope(vec, ps4[:, kb * 1024:(kb + 1) * 1024],
                            p1_cs[:, bi * 512:(bi + 1) * 512],
                            p1_tmp, krot2[bi], 8)
            ctx.op(fin, sv, 1, f"p1a.ropek.{s}")

        for s in range(NS):
            bi = s % 2
            ctx.wait(vec, f"p1a.x.{s}")
            if s >= 2:
                ctx.wait(vec, f"p1a.rstd.{s - 2}")
            fin = emit_bn(vec, xb_p1[:, bi * 4096:(bi + 1) * 4096], bi)
            ctx.op(fin, sv, 1, f"p1a.stats.{s}")
            ctx.wait(vec, f"p1a.rstd.{s}")
            if s >= 2:
                ctx.wait(vec, f"p1a.tr.{s - 2}.3")
            ctx.op(lambda bi=bi: vec.tensor_scalar(
                n1_2[bi], xb_p1[:, bi * 4096:(bi + 1) * 4096],
                statv(bi, 0), statv(bi, 3), ALU.subtract, ALU.mult),
                sv, 1, f"p1a.n1.{s}")
            if s >= 1:
                emit_ropek(s - 1)
        emit_ropek(NS - 1)

        if mp < 1:
            return
        sseq = list(range(12, 16)) + list(range(12))
        for si, s in enumerate(sseq):
            for dvb in range(2):
                q = (2 * s + dvb) % 4
                ctx.wait(vec, f"p1b.vmm0.{s}" if dvb == 0 else f"p1b.vmm.{s}")
                if si == 0 and dvb == 0:
                    ctx.wait(vec, "p1a.ktcw.2")
                    ctx.wait(vec, "p1a.ktcw.3")
                if si >= 2 and dvb == 0:
                    ctx.wait(vec, f"p1b.vcw.{sseq[si - 2]}")
                ctx.op(lambda s=s, dvb=dvb, q=q: vec.tensor_copy(
                    vcb[:, (s % 2) * 1024 + dvb * 512:
                        (s % 2) * 1024 + (dvb + 1) * 512],
                    ps4[:, q * 512:(q + 1) * 512]),
                    sv, 1, f"p1b.vev.{s}.{dvb}")

        if mp < 2:
            return
        prevroq = {}
        for db in range(8):
            for s in range(NS):
                bi = s % 2
                q = s % 4
                ctx.wait(vec, f"p2a.qmm.{db}.{s}")
                ctx.wait(vec, f"p2a.cs.{db}.{s}")
                if bi in prevroq:
                    ctx.wait(vec, prevroq[bi])
                fin = emit_rope(vec, ps4[:, q * 512:(q + 1) * 512],
                                p2a_cs[:, bi * 512:(bi + 1) * 512],
                                p2a_tmp, qrot2[bi], 4)
                ctx.op(fin, sv, 1, f"p2a.rope.{db}.{s}")
                prevroq[bi] = f"p2a.qtr.{db}.{s}"

        if mp < 3:
            return
        def p2b_masks(j, h):
            par = h % 2
            nck = 4 * j + 4
            for kc in range(4 * j, nck):
                gi = kc // 2
                half = gi % 2
                o = kc * 128 - j * 512
                ctx.wait(vec, f"p2b.exp.{j}.{h}.{gi}")
                sl_ = expT2[par][:, half * 1024 + (kc % 2) * 512 + o:
                                half * 1024 + (kc % 2) * 512 + o + 128]
                ctx.op(lambda sl_=sl_: vec.tensor_tensor(
                    sl_, sl_, tri[:], ALU.mult),
                    sv, 1, f"p2b.msk.{j}.{h}.{kc}")

        def p2b_outev(j, h):
            ctx.wait(vec, f"p2b.bcs.{j}.{h}")
            psAC = psA if h % 2 == 0 else psC
            use(("outw", h % 2), vec)
            ctx.op(lambda h=h, psAC=psAC: vec.tensor_tensor(
                outw[h % 2], psAC[:, 0:512], bc2[h % 2], ALU.mult),
                sv, 1, f"p2b.outev.{j}.{h}")
            rel(("outw", h % 2), f"p2b.otw.{j}.{h}")

        for j in range(4):
            for hp in range(0, H, 2):
                p2b_masks(j, hp)
                p2b_masks(j, hp + 1)
                p2b_outev(j, hp)
                p2b_outev(j, hp + 1)

        if mp < 4:
            return
        for db in range(8):
            for s in range(NS):
                q = s % 4
                bi = s % 2
                ctx.wait(vec, f"p2c.omm.{db}.{s}")
                ctx.wait(vec, f"p2c.x.{db}.{s}")
                use(("xpw", bi), vec)
                ctx.op(lambda q=q, bi=bi: vec.tensor_tensor(
                    xpw[bi], ps4[:, q * 512:(q + 1) * 512], xsb[bi], ALU.add),
                    sv, 1, f"p2c.ev.{db}.{s}")
                rel(("xpw", bi), f"p2c.xpw.{db}.{s}")

        if mp < 5:
            return
        if mp < 5:
            return
        def p2d_stats(s):
            ctx.wait(vec, f"p2d.xp.{s}")
            fin = emit_bn(vec, xp_sb[:, (s % 4) * 4096:(s % 4 + 1) * 4096],
                          s % 2)
            ctx.op(fin, sv, 1, f"p2d.stats.{s}")

        def p2d_n2(s):
            ctx.wait(vec, f"p2d.rstd.{s}")
            if s <= 1:
                ctx.wait(vec, "p2c.omm.7.15")
            if s >= 2:
                ctx.wait(vec, f"p2d.tr.{s - 2}.3")
            ctx.op(lambda s=s: vec.tensor_scalar(
                n2_2[s % 2], xp_sb[:, (s % 4) * 4096:(s % 4 + 1) * 4096],
                statv(s % 2, 0), statv(s % 2, 3), ALU.subtract, ALU.mult),
                sv, 1, f"p2d.n2.{s}")

        def p2d_gate_hsc(s):
            ctx.wait(vec, f"p2d.probs.{s}")
            if s >= 2:
                ctx.wait(vec, f"p2d.htr.{s - 2}")
            def vedge(f, tag, s=s):
                ctx.op(f, sv, 1, f"p2d.g{tag}.{s}")
                ctx.wait(vec, f"p2d.g{tag}.{s}")
            vedge(lambda: vec.max(ltm, probs), 1)
            vedge(lambda: vec.tensor_scalar(keept, probs, ltm[:, 1:2], None,
                                            ALU.is_ge), 2)
            vedge(lambda: vec.tensor_scalar(wsum, ltm[:, 0:1], ltm[:, 1:2],
                                            None, ALU.add), 3)
            vedge(lambda: vec.reciprocal(winv, wsum), 4)
            vedge(lambda: vec.tensor_tensor(wful, probs, keept, ALU.mult), 5)
            vedge(lambda: vec.tensor_scalar(wful, wful, winv, None, ALU.mult), 6)
            ctx.wait(vec, f"p2d.amm.{s}")
            for e in range(NE):
                ctx.op(lambda e=e, s=s: vec.tensor_scalar(
                    h_2[s % 2][:, e * 64:(e + 1) * 64],
                    psEa[:, e * 64:(e + 1) * 64],
                    wful[:, e:e + 1], None, ALU.mult),
                    sv, 1 if e == 7 else 0, f"p2d.hsc.{s}" if e == 7 else None)

        def p2d_bev(s):
            for db in range(8):
                q = db % 4
                ctx.wait(vec, f"p2d.bmm.{s}.{db}")
                if s >= 2 and db == 0:
                    ctx.wait(vec, f"out.{s - 2}")
                ctx.op(lambda s=s, db=db, q=q: vec.tensor_tensor(
                    outb[:, (s % 2) * 4096 + db * 512:
                         (s % 2) * 4096 + (db + 1) * 512],
                    ps4[:, q * 512:(q + 1) * 512],
                    xp_sb[:, (s % 4) * 4096 + db * 512:
                          (s % 4) * 4096 + (db + 1) * 512],
                    ALU.add),
                    sv, 1, f"p2d.bev.{s}.{db}")

        p2d_stats(0)
        p2d_n2(0)
        for s in range(NS):
            if s >= 1:
                p2d_gate_hsc(s - 1)
            if s + 1 < NS:
                p2d_stats(s + 1)
            if s >= 1:
                p2d_bev(s - 1)
            if s + 1 < NS:
                p2d_n2(s + 1)
        p2d_gate_hsc(NS - 1)
        p2d_bev(NS - 1)

    emitters = [emit_sync, emit_gpsimd, emit_tensor, emit_scalar, emit_vector]
    ctx.start_pass(emit=False)
    res.clear()
    for f in emitters:
        f(None)
    ctx.finish_pass1()

    sem_nums = sorted(s.h.num for s in sems.values())
    assert sem_nums == list(range(sem_nums[0], sem_nums[0] + len(sem_nums)))
    srange = range(sem_nums[0], sem_nums[-1] + 1)
    nc.all_engine_barrier()
    nc.gpsimd.dma_reset(srange)
    nc.gpsimd.sem_clear(srange)
    nc.all_engine_barrier()

    ctx.start_pass(emit=True)
    res.clear()
    with nc.Block() as block:
        @block.sync
        def _(sync):
            emit_sync(sync)

        @block.gpsimd
        def _(gp):
            emit_gpsimd(gp)

        @block.tensor
        def _(te):
            emit_tensor(te)

        @block.scalar
        def _(sca):
            emit_scalar(sca)

        @block.vector
        def _(vec):
            emit_vector(vec)

    return nc


def host_inputs(L0_attn_x, L0_attn_Wq, L0_attn_Wk, L0_attn_Wv, L0_attn_Wo,
                L0_ffn_gate_w, L0_ffn_lora_A, L0_ffn_lora_B):
    cos, sin = _llama3_rope_tables()
    gwt = np.ascontiguousarray(
        np.asarray(L0_ffn_gate_w, dtype=np.float32).T.reshape(32, 128, 8)
        .transpose(1, 0, 2).reshape(128, 256))
    shared = {
        "wqt": np.ascontiguousarray(np.asarray(L0_attn_Wq, np.float32).T),
        "wkt": np.ascontiguousarray(np.asarray(L0_attn_Wk, np.float32).T),
        "wvt": np.ascontiguousarray(np.asarray(L0_attn_Wv, np.float32).T),
        "wot": np.ascontiguousarray(np.asarray(L0_attn_Wo, np.float32).T),
        "gwt": gwt,
        "at": np.ascontiguousarray(
            np.asarray(L0_ffn_lora_A, np.float32).transpose(2, 0, 1)
            .reshape(D, NE * RK)),
        "bt": np.ascontiguousarray(
            np.asarray(L0_ffn_lora_B, np.float32).transpose(0, 2, 1)
            .reshape(NE * RK, D)),
        "cos": cos,
        "sin": sin,
        "idt_d": np.eye(128, dtype=BF),
        "tri_d": np.ascontiguousarray(
            (np.arange(128)[None, :] >= np.arange(128)[:, None]).astype(BF)),
        "ones2_d": np.ones((128, 128), dtype=BF),
        "cvec_d": np.full((128, 2), EPS, dtype=np.float32),
    }
    x = np.asarray(L0_attn_x, np.float32)
    in_maps = []
    for c in range(8):
        m = dict(shared)
        m["x"] = np.ascontiguousarray(x[c])
        in_maps.append(m)
    return in_maps


def kernel(L0_attn_x, L0_attn_Wq, L0_attn_Wk, L0_attn_Wv, L0_attn_Wo,
           L0_ffn_gate_w, L0_ffn_lora_A, L0_ffn_lora_B):
    in_maps = host_inputs(L0_attn_x, L0_attn_Wq, L0_attn_Wk, L0_attn_Wv,
                          L0_attn_Wo, L0_ffn_gate_w, L0_ffn_lora_A, L0_ffn_lora_B)
    nc = build_program(dbg=False)
    res = run_bass_kernel_spmd(nc, in_maps, core_ids=list(range(8)))
    out = np.stack([np.asarray(r["out"], dtype=np.float32) for r in res.results])
    return out


if __name__ == "__main__":
    nc = build_program(dbg=False)
    print("built ok")


# version 19
# speedup vs baseline: 1.0532x; 1.0120x over previous
import math
import sys

sys.path.insert(0, "/opt/trn_rl_repo")

import numpy as np
import ml_dtypes

import concourse.bass as bass
import concourse.mybir as mybir
from concourse.bass_utils import run_bass_kernel_spmd

F32 = mybir.dt.float32
BF16 = mybir.dt.bfloat16
AF = mybir.ActivationFunctionType
ALU = mybir.AluOpType
AX = mybir.AxisListType
BF = ml_dtypes.bfloat16

T = 2048
D = 4096
HD = 128
H = 32
KVH = 8
DKV = 1024
NE = 8
RK = 64
NS = 16
NG = 4
EPS = 1e-5
ISQ = 1.0 / math.sqrt(HD)


class Sem:
    def __init__(self, nc, name):
        self.h = nc.alloc_semaphore(name=name)
        self.name = name
        self.val = 0


class Ctx:

    def __init__(self, sems):
        self.sems = sems
        self.ck = {}
        self.first = True
        self.emit = False
        self.pending = []

    def start_pass(self, emit):
        self.emit = emit
        self.cnt = {}
        for s in self.sems.values():
            s.val = 0

    def nid(self, tag):
        self.cnt[tag] = self.cnt.get(tag, 0) + 1
        return f"{tag}.{self.cnt[tag]}" 

    def op(self, f, sem=None, n=0, name=None):
        ins = f() if self.emit else None
        if sem is not None and n > 0:
            sem.val += n
            if name is not None:
                if self.first:
                    assert name not in self.ck, f"dup ck {name}"
                    self.ck[name] = (sem.name, sem.val)
                else:
                    assert self.ck[name] == (sem.name, sem.val), (
                        f"ck mismatch {name}")
            if self.emit:
                ins.then_inc(sem.h, n)
        return ins

    def wait(self, eng, name):
        if not self.emit:
            self.pending.append(name)
            return
        sname, val = self.ck[name]
        eng.wait_ge(self.sems[sname].h, val)

    def finish_pass1(self):
        missing = [n for n in self.pending if n not in self.ck]
        assert not missing, f"unresolved checkpoints: {sorted(set(missing))[:10]}"
        self.first = False


def _llama3_rope_tables():
    base = 500000.0
    inv_freq = 1.0 / base ** (np.arange(0, HD, 2, dtype=np.float32) / HD)
    factor, low_f, high_f, old_ctx = 32.0, 1.0, 4.0, 8192
    low_wl, high_wl = old_ctx / low_f, old_ctx / high_f
    wavelen = 2 * math.pi / inv_freq
    inv_llama = np.where(wavelen > low_wl, inv_freq / factor, inv_freq)
    smooth = (old_ctx / wavelen - low_f) / (high_f - low_f)
    smoothed = (1 - smooth) * inv_llama / factor + smooth * inv_llama
    is_med = ~(wavelen < high_wl) & ~(wavelen > low_wl)
    inv = np.where(is_med, smoothed, inv_llama).astype(np.float32)
    pos = np.arange(T, dtype=np.float32)
    freqs = pos[:, None] * inv[None, :]
    emb = np.concatenate([freqs, freqs], axis=-1)
    return np.cos(emb).astype(np.float32), np.sin(emb).astype(np.float32)


def build_program(dbg=False, mp=5, ndb=8, p2as=5):
    nc = bass.Bass("TRN2", target_bir_lowering=False)

    X = nc.dram_tensor("x", [T, D], F32, kind="ExternalInput")
    WQT = nc.dram_tensor("wqt", [D, D], F32, kind="ExternalInput")
    WKT = nc.dram_tensor("wkt", [D, DKV], F32, kind="ExternalInput")
    WVT = nc.dram_tensor("wvt", [D, DKV], F32, kind="ExternalInput")
    WOT = nc.dram_tensor("wot", [D, D], F32, kind="ExternalInput")
    AT = nc.dram_tensor("at", [D, NE * RK], F32, kind="ExternalInput")
    GWT = nc.dram_tensor("gwt", [128, 256], F32, kind="ExternalInput")
    BT = nc.dram_tensor("bt", [NE * RK, D], F32, kind="ExternalInput")
    COS = nc.dram_tensor("cos", [T, HD], F32, kind="ExternalInput")
    SIN = nc.dram_tensor("sin", [T, HD], F32, kind="ExternalInput")
    IDT = nc.dram_tensor("idt_d", [128, 128], BF16, kind="ExternalInput")
    TRI = nc.dram_tensor("tri_d", [128, 128], BF16, kind="ExternalInput")
    ONES2 = nc.dram_tensor("ones2_d", [128, 128], BF16, kind="ExternalInput")
    CVEC = nc.dram_tensor("cvec_d", [128, 2], F32, kind="ExternalInput")
    OUT = nc.dram_tensor("out", [T, D], F32, kind="ExternalOutput")

    skind = "ExternalOutput" if dbg else "Internal"
    n1t_d = nc.dram_tensor("n1t_d", [32, 128, T], BF16, kind=skind)
    ktc_d = nc.dram_tensor("ktc_d", [KVH, 128, T], BF16, kind=skind)
    vc_d = nc.dram_tensor("vc_d", [NS, 128, DKV], BF16, kind=skind)
    qt_d = nc.dram_tensor("qt_d", [H, 128, T], BF16, kind=skind)
    ot_d = nc.dram_tensor("ot_d", [32, 128, T], BF16, kind=skind)
    xp_d = nc.dram_tensor("xp_d", [T, D], F32, kind=skind)

    BIG = nc.alloc_sbuf_tensor("big", [128, 65536], BF16)
    MED = nc.alloc_sbuf_tensor("med", [128, 32768], BF16)
    SMF = nc.alloc_sbuf_tensor("smf", [128, 2048], F32)
    SMB = nc.alloc_sbuf_tensor("smb", [128, 2048], BF16)
    STT = nc.alloc_sbuf_tensor("stt", [128, 160], F32)
    ident = nc.alloc_sbuf_tensor("ident", [128, 128], BF16)
    tri = nc.alloc_sbuf_tensor("tri", [128, 128], BF16)
    ones2 = nc.alloc_sbuf_tensor("ones2", [128, 128], BF16)
    cvec = nc.alloc_sbuf_tensor("cvec", [128, 2], F32)

    ps4 = nc.alloc_psum_tensor("ps4", [128, 2048], F32)
    psE = nc.alloc_psum_tensor("psE", [128, 1024], F32)
    psC = nc.alloc_psum_tensor("psC", [128, 512], F32)
    psD = nc.alloc_psum_tensor("psD", [128, 512], F32)
    psEa = psE.ap()
    psA = psEa[:, 0:512]
    psB = psEa[:, 512:1024]
    psE_bf = psEa.bitcast(BF16)

    pbA = psE_bf[:, 0:1024]
    pbB = psE_bf[:, 1024:2048]
    pbC = psC.ap().bitcast(BF16)
    pbD = psD.ap().bitcast(BF16)

    def pbh(half, ph=0):
        if ph == 0:
            return (pbA, pbC)[half]
        return (pbC, pbD)[half]

    def bigv(c0, c1):
        return BIG[:, c0:c1]

    wk_sb = bigv(0, 32768)
    wv_sb = bigv(0, 32768)
    n1tg = bigv(32768, 49152)
    ktcg = bigv(49152, 53248)
    krot2 = [bigv(53248, 54272), bigv(64512, 65536)]
    n1_2 = [bigv(54272, 58368), SMF.ap().bitcast(BF16)]
    p1_tmp = bigv(58368, 62464).bitcast(F32)
    p1_cs = bigv(62464, 64512).bitcast(F32)
    vcb = bigv(49152, 53248)
    xb_p1 = MED.ap().bitcast(F32)

    n1t_f = bigv(0, 65536)
    wq_sb = MED
    qrot2 = [SMB[:, 0:512], SMB[:, 1536:2048]]
    qtw = [SMB[:, 512:1024], SMB[:, 1024:1536]]
    p2a_tmp = SMF[:, 0:1024]
    p2a_cs = SMF[:, 1024:2048]

    ktc_sb = bigv(0, 16384)
    vc_sb = bigv(16384, 32768)
    qt_sb = [bigv(32768, 49152), bigv(49152, 65536)]
    expT2 = [MED[:, 16384:18432], MED[:, 20480:22528]]
    outw = [MED[:, 22528:23040], MED[:, 23040:23552]]
    bc2 = [SMF[:, 0:512], SMF[:, 1024:1536]]
    t_sb = SMF[:, 512:1024]

    ot_sb = bigv(0, 65536)
    wo_sb = MED
    xsb = [SMF[:, 0:512], SMF[:, 512:1024]]
    xpw = [SMF[:, 1024:1536], SMF[:, 1536:2048]]

    aT_sb = bigv(0, 16384)
    bT_sb = bigv(16384, 32768)
    outb = bigv(32768, 49152).bitcast(F32)
    n2t2 = [bigv(49152, 53248), bigv(53248, 57344)]
    n2_2 = [bigv(57344, 61440), bigv(61440, 65536)]
    xp_sb = MED.ap().bitcast(F32)
    gw_sb = SMF.ap().bitcast(BF16)[:, 3072:3328]
    h_2 = [SMB[:, 0:512], SMB[:, 512:1024]]
    hT_2 = [SMB[:, 1024:1536], SMB[:, 1536:2048]]

    def bnstv(par):
        return STT[:, par * 54: par * 54 + 48]

    def statv(par, i):
        return STT[:, par * 54 + 48 + i: par * 54 + 48 + i + 1]

    negm = STT[:, 108:109]
    m1 = STT[:, 109:110]
    m2 = STT[:, 110:111]
    wsum = STT[:, 111:112]
    winv = STT[:, 112:113]
    probs = STT[:, 114:122]
    ltm = STT[:, 122:130]
    pmask = STT[:, 130:138]
    keept = STT[:, 138:146]
    wful = STT[:, 146:154]

    sem_names = (["t", "v", "s", "ld_c", "ld_n1g", "ld_n1f", "ld_kv", "ld_ot",
                  "g_wk", "g_wv", "g_ab", "g_gw", "s_ab", "s_ot"]
                 + [f"{b}{i}" for b in ["ld_x", "ld_cs", "ld_qt", "ld_x2",
                                        "ld_xp", "g_out", "g_n1w", "g_kw",
                                        "g_vc", "g_wq", "g_qw", "g_ot",
                                        "g_wo", "g_xp"] for i in (0, 1)])
    sems = {n: Sem(nc, n) for n in sem_names}
    ctx = Ctx(sems)
    S = sems
    st_, sv, ss = sems["t"], sems["v"], sems["s"]

    res = {}

    def use(key, wait_eng):
        if res.get(key) is not None:
            ctx.wait(wait_eng, res[key])

    def rel(key, name):
        res[key] = name

    def emit_rope(vec, src_ap, cs_ap, tmp_ap, out_ap, nh):
        def v3(ap_, lo):
            return ap_.rearrange("p (h x) -> p h x", h=nh)[:, :, lo * 64:(lo + 1) * 64]

        def csb(lo, which):
            return cs_ap[:, which * 128 + lo * 64: which * 128 + (lo + 1) * 64] \
                .unsqueeze(1).broadcast_to([128, nh, 64])

        t1 = tmp_ap[:, 0: nh * 64].rearrange("p (h x) -> p h x", h=nh)
        t2 = tmp_ap[:, nh * 64: nh * 128].rearrange("p (h x) -> p h x", h=nh)
        ctx.op(lambda: vec.tensor_tensor(t1, v3(src_ap, 1), csb(0, 1), ALU.mult))
        ctx.op(lambda: vec.tensor_tensor(t2, v3(src_ap, 0), csb(0, 0), ALU.mult))
        ctx.op(lambda: vec.tensor_tensor(v3(out_ap, 0), t2, t1, ALU.subtract))
        ctx.op(lambda: vec.tensor_tensor(t1, v3(src_ap, 0), csb(1, 1), ALU.mult))
        ctx.op(lambda: vec.tensor_tensor(t2, v3(src_ap, 1), csb(1, 0), ALU.mult))
        return lambda: vec.tensor_tensor(v3(out_ap, 1), t2, t1, ALU.add)

    def emit_ln_scalar_pre(sca, par=0):
        nm = ctx.nid("lnln")
        ctx.op(lambda: sca.activation(out=statv(par, 2), in_=statv(par, 1),
                                      func=AF.Ln, bias=cvec[:, 0:1]), ss, 1, nm)
        ctx.wait(sca, nm)
        return lambda: sca.activation(out=statv(par, 3), in_=statv(par, 2),
                                      func=AF.Exp, scale=-0.5)

    def emit_bn(vec, x_ap, par=0):
        nm = ctx.nid("bnst")
        for i in range(8):
            ctx.op(lambda i=i, par=par: vec.bn_stats(
                bnstv(par)[:, i * 6:(i + 1) * 6],
                x_ap[:, i * 512:(i + 1) * 512]),
                sv, 1 if i == 7 else 0, nm if i == 7 else None)
        ctx.wait(vec, nm)
        return lambda: vec.bn_aggr(STT[:, par * 54 + 48: par * 54 + 50], bnstv(par))


    def sync_finish(sync):
        if ctx.emit:
            for nm, sm in sems.items():
                if nm.startswith("ld_") and sm.val > 0:
                    sync.wait_ge(sm.h, sm.val)

    def gp_finish(gp):
        if ctx.emit:
            for nm, sm in sems.items():
                if nm.startswith("g_") and sm.val > 0:
                    gp.wait_ge(sm.h, sm.val)

    def emit_sync(sync):
        consts = [(ident, IDT), (tri, TRI), (cvec, CVEC), (ones2, ONES2)]
        for i, (tns, src) in enumerate(consts):
            ctx.op(lambda tns=tns, src=src: sync.dma_start(out=tns[:], in_=src[:, :]),
                   S["ld_c"], 16, "consts" if i == 3 else None)

        for s in range(NS):
            t0 = s * 128
            bi = s % 2
            use(("xb", bi), sync)
            ctx.op(lambda bi=bi, t0=t0: sync.dma_start(
                out=xb_p1[:, bi * 4096:(bi + 1) * 4096], in_=X[t0:t0 + 128, :]),
                S[f"ld_x{bi}"], 16, f"p1a.x.{s}")
            rel(("xb", bi), f"p1a.n1.{s}")
            use(("cs1", bi), sync)
            ctx.op(lambda bi=bi, t0=t0: sync.dma_start(
                out=p1_cs[:, bi * 512: bi * 512 + 128], in_=COS[t0:t0 + 128, :]),
                S[f"ld_cs{bi}"], 16)
            ctx.op(lambda bi=bi, t0=t0: sync.dma_start(
                out=p1_cs[:, bi * 512 + 128: bi * 512 + 256], in_=SIN[t0:t0 + 128, :]),
                S[f"ld_cs{bi}"], 16, f"p1a.cs.{s}")
            rel(("cs1", bi), f"p1a.ropek.{s}")

        if mp < 1:
            sync_finish(sync)
            return
        for g in [0, 1, 2]:
            ctx.wait(sync, f"p1a.n1tw.{g}")
            if g == 0:
                ctx.wait(sync, f"p1b.vmm.15")
            else:
                ctx.wait(sync, f"p1b.vmm.{4 * g - 1}")
            for q4 in range(4):
                src = n1t_d[:, :, g * 512:(g + 1) * 512].rearrange(
                    "c p t -> p c t")[:, 8 * q4:8 * q4 + 8, :]
                dst = n1tg.rearrange("p (c t) -> p c t", c=32)[:, 8 * q4:8 * q4 + 8, :]
                ctx.op(lambda src=src, dst=dst: sync.dma_start(out=dst, in_=src),
                       S["ld_n1g"], 16, f"p1b.n1t.{g}" if q4 == 3 else None)

        if mp < 2:
            sync_finish(sync)
            return
        def n1f_ld(ci):
            ctx.op(lambda ci=ci: sync.dma_start(
                out=n1t_f[:, ci * 2048:(ci + 1) * 2048], in_=n1t_d[ci, :, :]),
                S["ld_n1f"], 16, f"p2a.n1t.{ci}")

        ctx.wait(sync, "p1a.n1tw.2")
        ctx.wait(sync, "p1a.n1tw.3")
        ctx.wait(sync, "p1a.tr.14.3")
        ctx.wait(sync, "p1a.ropek.15")
        ctx.wait(sync, "p1a.ktr.15.0")
        for ci in range(26, 32):
            n1f_ld(ci)
        ctx.wait(sync, "p1b.vmm.11")
        for ci in range(0, 12):
            n1f_ld(ci)
        ctx.wait(sync, "p1b.vev.11.1")
        ctx.wait(sync, "p1b.vcw.11")
        ctx.wait(sync, "p1b.vcw.10")
        for ci in (24, 25):
            n1f_ld(ci)
        for db in range(ndb):
            for s in range(NS):
                t0 = s * 128
                bi = s % 2
                use(("cs2", bi), sync)
                ctx.op(lambda bi=bi, t0=t0: sync.dma_start(
                    out=p2a_cs[:, bi * 512: bi * 512 + 128], in_=COS[t0:t0 + 128, :]),
                    S[f"ld_cs{bi}"], 16)
                ctx.op(lambda bi=bi, t0=t0: sync.dma_start(
                    out=p2a_cs[:, bi * 512 + 128: bi * 512 + 256],
                    in_=SIN[t0:t0 + 128, :]), S[f"ld_cs{bi}"], 16,
                    f"p2a.cs.{db}.{s}")
                rel(("cs2", bi), f"p2a.rope.{db}.{s}")

        if mp < 3:
            sync_finish(sync)
            return
        ctx.wait(sync, "p2a.qmm.7.15")
        ctx.wait(sync, "p2a.qtr.7.15")
        ctx.wait(sync, "p1a.ktcw.3")
        ctx.wait(sync, "p1a.ktcw.2")
        ctx.op(lambda: sync.dma_start(
            out=ktc_sb.rearrange("p (h t) -> p h t", h=KVH),
            in_=ktc_d.ap().rearrange("h p t -> p h t")), S["ld_kv"], 16)
        for q2 in range(2):
            ctx.op(lambda q2=q2: sync.dma_start(
                out=vc_sb.rearrange("p (s d) -> p s d", s=NS)[:, 8 * q2:8 * q2 + 8, :],
                in_=vc_d.ap().rearrange("s p d -> p s d")[:, 8 * q2:8 * q2 + 8, :]),
                S["ld_kv"], 16, "p2b.kv" if q2 == 1 else None)
        for j in range(4):
            ctx.wait(sync, "p2a.qtw.7.15")
            ctx.wait(sync, "p2a.qtw.7.14")
            if j >= 2:
                ctx.wait(sync, f"p2b.sc.{j - 2}.31.{2 * (j - 2) + 1}")
            for q4 in range(4):
                src = qt_d[:, :, j * 512:(j + 1) * 512].rearrange(
                    "h p t -> p h t")[:, 8 * q4:8 * q4 + 8, :]
                dst = qt_sb[j % 2].rearrange("p (h t) -> p h t", h=H)[
                    :, 8 * q4:8 * q4 + 8, :]
                ctx.op(lambda src=src, dst=dst: sync.dma_start(out=dst, in_=src),
                       S[f"ld_qt{j % 2}"], 16, f"p2b.qt.{j}" if q4 == 3 else None)

        if mp < 4:
            sync_finish(sync)
            return
        ctx.wait(sync, "p2b.av.3.31.7")
        ctx.wait(sync, "p2b.otw.3.31")
        ctx.wait(sync, "p2b.otw.3.30")
        for ci in range(16):
            ctx.op(lambda ci=ci: sync.dma_start(
                out=ot_sb[:, ci * 2048:(ci + 1) * 2048], in_=ot_d[ci, :, :]),
                S["ld_ot"], 16, f"p2c.ot.{ci}")
        for db in range(8):
            for s in range(NS):
                t0 = s * 128
                bi = s % 2
                if db == 0 and s == 0:
                    ctx.wait(sync, "p2b.outev.3.31")
                use(("xs", bi), sync)
                ctx.op(lambda db=db, t0=t0, bi=bi: sync.dma_start(
                    out=xsb[bi], in_=X[t0:t0 + 128, db * 512:(db + 1) * 512]),
                    S[f"ld_x2{bi}"], 16, f"p2c.x.{db}.{s}")
                rel(("xs", bi), f"p2c.ev.{db}.{s}")

        if mp < 5:
            sync_finish(sync)
            return
        for s in range(NS):
            sl4 = s % 4
            ctx.wait(sync, f"p2c.xpw.7.{s}")
            if s < 2:
                ctx.wait(sync, "p2c.omm.6.15")
            elif s < 4:
                ctx.wait(sync, "p2c.omm.7.15")
            else:
                ctx.wait(sync, f"p2d.bev.{s - 4}.7")
            ctx.op(lambda s=s, sl4=sl4: sync.dma_start(
                out=xp_sb[:, sl4 * 4096:(sl4 + 1) * 4096],
                in_=xp_d[s * 128:(s + 1) * 128, :]),
                S[f"ld_xp{s % 2}"], 16, f"p2d.xp.{s}")
        sync_finish(sync)

    def emit_gpsimd(gp):
        for i in range(4):
            src = WKT.ap().rearrange("(c p) d -> p c d", p=128)[:, 8 * i: 8 * i + 8, :]
            dst = wk_sb.rearrange("p (c d) -> p c d", c=32)[:, 8 * i: 8 * i + 8, :]
            ctx.op(lambda src=src, dst=dst: gp.dma_start(out=dst, in_=src),
                   S["g_wk"], 16, f"p1a.wk.{i}")
        for g in range(NG):
            ctx.wait(gp, f"p1a.n1ev.{4 * g + 3}.3")
            for q4 in range(4):
                dsta = n1t_d[:, :, g * 512:(g + 1) * 512].rearrange(
                    "c p t -> p c t")[:, 8 * q4:8 * q4 + 8, :]
                srca = n1tg.rearrange("p (c t) -> p c t", c=32)[:, 8 * q4:8 * q4 + 8, :]
                ctx.op(lambda dsta=dsta, srca=srca: gp.dma_start(out=dsta, in_=srca),
                       S[f"g_n1w{g % 2}"], 16, f"p1a.n1tw.{g}" if q4 == 3 else None)
            ctx.wait(gp, f"p1a.kev.{4 * g + 3}.0")
            ctx.op(lambda g=g: gp.dma_start(
                out=ktc_d[:, :, g * 512:(g + 1) * 512].rearrange("h p t -> p h t"),
                in_=ktcg.rearrange("p (h t) -> p h t", h=KVH)),
                S[f"g_kw{g % 2}"], 16, f"p1a.ktcw.{g}")

        if mp < 1:
            gp_finish(gp)
            return
        ctx.wait(gp, "p1a.kmm.15")
        for i in range(4):
            src = WVT.ap().rearrange("(c p) d -> p c d", p=128)[:, 8 * i: 8 * i + 8, :]
            dst = wv_sb.rearrange("p (c d) -> p c d", c=32)[:, 8 * i: 8 * i + 8, :]
            ctx.op(lambda src=src, dst=dst: gp.dma_start(out=dst, in_=src),
                   S["g_wv"], 16, f"p1b.wv.{i}")
        for s in list(range(12, 16)) + list(range(12)):
            ctx.wait(gp, f"p1b.vev.{s}.1")
            ctx.op(lambda s=s: gp.dma_start(
                out=vc_d[s, :, :], in_=vcb[:, (s % 2) * 1024:(s % 2 + 1) * 1024]),
                S[f"g_vc{s % 2}"], 16, f"p1b.vcw.{s}")

        if mp < 2:
            gp_finish(gp)
            return
        def load_wq(db):
            for hh in range(2):
                if db >= 2:
                    ctx.wait(gp, f"p2a.qmm.{db - 2}.15")
                elif db == 0 and hh == 0:
                    ctx.wait(gp, "p1a.n1.15")
                for q4 in range(4):
                    src = WQT[:, db * 512 + hh * 256: db * 512 + (hh + 1) * 256] \
                        .rearrange("(c p) d -> p c d", p=128)[:, 8 * q4:8 * q4 + 8, :]
                    dst = wq_sb[:, (db % 2) * 16384:(db % 2 + 1) * 16384] \
                        .rearrange("p (c d) -> p c d", c=32)[
                        :, 8 * q4:8 * q4 + 8, hh * 256:(hh + 1) * 256]
                    ctx.op(lambda src=src, dst=dst: gp.dma_start(out=dst, in_=src),
                           S[f"g_wq{db % 2}"], 16,
                           f"p2a.wq.{db}.{hh}" if q4 == 3 else None)

        load_wq(0)
        if ndb > 1:
            load_wq(1)
        for db in range(ndb):
            if p2as >= 5:
                for s in range(NS):
                    ctx.wait(gp, f"p2a.qtev.{db}.{s}")
                    hbase = db * 4
                    dst = qt_d[hbase:hbase + 4, :, s * 128:(s + 1) * 128] \
                        .rearrange("h p t -> p h t")
                    src = qtw[s % 2].rearrange("p (h t) -> p h t", h=4)
                    ctx.op(lambda dst=dst, src=src: gp.dma_start(out=dst, in_=src),
                           S[f"g_qw{s % 2}"], 16, f"p2a.qtw.{db}.{s}")
            if db + 2 <= ndb - 1:
                load_wq(db + 2)

        if mp < 3:
            gp_finish(gp)
            return
        def load_wo(db):
            for hh in range(2):
                if db >= 2:
                    ctx.wait(gp, f"p2c.omm.{db - 2}.15")
                elif db == 0 and hh == 0:
                    ctx.wait(gp, "p2a.qmm.7.15")
                elif db == 1 and hh == 0:
                    ctx.wait(gp, "p2b.otw.3.31")
                    ctx.wait(gp, "p2b.otw.3.30")
                    ctx.wait(gp, "p2b.av.3.31.7")
                for q4 in range(4):
                    src = WOT[:, db * 512 + hh * 256: db * 512 + (hh + 1) * 256] \
                        .rearrange("(c p) d -> p c d", p=128)[:, 8 * q4:8 * q4 + 8, :]
                    dst = wo_sb[:, (db % 2) * 16384:(db % 2 + 1) * 16384] \
                        .rearrange("p (c d) -> p c d", c=32)[
                        :, 8 * q4:8 * q4 + 8, hh * 256:(hh + 1) * 256]
                    ctx.op(lambda src=src, dst=dst: gp.dma_start(out=dst, in_=src),
                           S[f"g_wo{db % 2}"], 16,
                           f"p2c.wo.{db}.{hh}" if q4 == 3 else None)

        load_wo(0)
        for j in range(4):
            for h in range(H):
                ctx.wait(gp, f"p2b.outev.{j}.{h}")
                ctx.op(lambda j=j, h=h: gp.dma_start(
                    out=ot_d[h, :, j * 512:(j + 1) * 512], in_=outw[h % 2]),
                    S[f"g_ot{h % 2}"], 16, f"p2b.otw.{j}.{h}")

        if mp < 4:
            gp_finish(gp)
            return
        load_wo(1)
        for db in range(8):
            for s in range(NS):
                t0 = s * 128
                ctx.wait(gp, f"p2c.ev.{db}.{s}")
                ctx.op(lambda db=db, t0=t0, s=s: gp.dma_start(
                    out=xp_d[t0:t0 + 128, db * 512:(db + 1) * 512], in_=xpw[s % 2]),
                    S[f"g_xp{s % 2}"], 16, f"p2c.xpw.{db}.{s}")
            if db + 2 <= 7:
                load_wo(db + 2)

        if mp < 5:
            gp_finish(gp)
            return
        ctx.wait(gp, "p2c.xpw.7.15")
        ctx.op(lambda: gp.dma_start(out=gw_sb, in_=GWT[:, :]),
               S["g_gw"], 16, "p2d.gw")
        ctx.wait(gp, "p2c.omm.7.15")
        for q4 in range(4):
            src = AT.ap().rearrange("(c p) r -> p c r", p=128)[:, 8 * q4:8 * q4 + 8, :]
            dst = aT_sb.rearrange("p (c r) -> p c r", c=32)[:, 8 * q4:8 * q4 + 8, :]
            ctx.op(lambda src=src, dst=dst: gp.dma_start(out=dst, in_=src),
                   S["g_ab"], 16, "p2d.at" if q4 == 3 else None)
        src = BT.ap().rearrange("(c p) d -> p c d", p=128)
        dst = bT_sb.rearrange("p (c d) -> p c d", c=4)
        ctx.op(lambda src=src, dst=dst: gp.dma_start(out=dst, in_=src),
               S["g_ab"], 16, "p2d.bt")
        for s in range(NS):
            ctx.wait(gp, f"p2d.bev.{s}.7")
            ctx.op(lambda s=s: gp.dma_start(
                out=OUT[s * 128:(s + 1) * 128, :],
                in_=outb[:, (s % 2) * 4096:(s % 2 + 1) * 4096]),
                S[f"g_out{s % 2}"], 16, f"out.{s}")
        gp_finish(gp)

    def emit_tensor(te):
        ctx.wait(te, "consts")
        def emit_ktr(s):
            kbi = s % 2
            half = s % 2
            ctx.wait(te, f"p1a.ropek.{s}")
            use(("pb1", half), te)
            for i in range(8):
                ctx.op(lambda half=half, i=i, kbi=kbi: te.transpose(
                    pbh(half, 0)[:, i * 128:(i + 1) * 128],
                    krot2[kbi][:, i * 128:(i + 1) * 128], ident[:]),
                    st_, 1 if i == 7 else 0,
                    f"p1a.ktr.{s}.0" if i == 7 else None)
            rel(("pb1", half), f"p1a.kev.{s}.0")

        for s in range(NS):
            g, sl = s // 4, s % 4
            bi = s % 2
            for b in range(4):
                if b == 0:
                    ctx.wait(te, f"p1a.n1.{s}")
                half = b % 2
                use(("pb1", half), te)
                for i in range(8):
                    ci = 8 * b + i
                    ctx.op(lambda half=half, ci=ci, i=i, bi=bi: te.transpose(
                        pbh(half, 0)[:, i * 128:(i + 1) * 128],
                        n1_2[bi][:, ci * 128:(ci + 1) * 128], ident[:]),
                        st_, 1 if i == 7 else 0,
                        f"p1a.tr.{s}.{b}" if i == 7 else None)
                rel(("pb1", half), f"p1a.n1ev.{s}.{b}")
            if s >= 1:
                emit_ktr(s - 1)
            ctx.wait(te, f"p1a.n1ev.{s}.3")
            if s == 0:
                ctx.wait(te, "p1a.wk.3")
            kb = s % 2
            use(("kps", kb), te)
            for dkb in range(2):
                for ci in range(32):
                    last = (dkb == 1 and ci == 31)
                    ctx.op(lambda kb=kb, dkb=dkb, ci=ci, sl=sl: te.matmul(
                        ps4[:, kb * 1024 + dkb * 512: kb * 1024 + (dkb + 1) * 512],
                        n1tg[:, ci * 512 + sl * 128: ci * 512 + sl * 128 + 128],
                        wk_sb[:, ci * 1024 + dkb * 512: ci * 1024 + (dkb + 1) * 512],
                        start=(ci == 0), stop=(ci == 31)),
                        st_, 1 if last else 0, f"p1a.kmm.{s}" if last else None)
            rel(("kps", kb), f"p1a.ropek.{s}")
        emit_ktr(NS - 1)

        if mp < 1:
            return
        rel(("ps4q", 0), "p1a.ropek.14")
        rel(("ps4q", 1), "p1a.ropek.14")
        rel(("ps4q", 2), "p1a.ropek.15")
        rel(("ps4q", 3), "p1a.ropek.15")
        first = True
        for s in list(range(12, 16)) + list(range(12)):
            g, sl = s // 4, s % 4
            if sl == 0 and g != 3:
                ctx.wait(te, f"p1b.n1t.{g}")
            if first:
                ctx.wait(te, "p1b.wv.3")
                ctx.wait(te, "p1a.kmm.15")
                first = False
            for dvb in range(2):
                q = (2 * s + dvb) % 4
                use(("ps4q", q), te)
                for ci in range(32):
                    name = (f"p1b.vmm.{s}" if dvb == 1 else f"p1b.vmm0.{s}") \
                        if ci == 31 else None
                    ctx.op(lambda q=q, ci=ci, sl=sl, dvb=dvb: te.matmul(
                        ps4[:, q * 512:(q + 1) * 512],
                        n1tg[:, ci * 512 + sl * 128: ci * 512 + sl * 128 + 128],
                        wv_sb[:, ci * 1024 + dvb * 512: ci * 1024 + (dvb + 1) * 512],
                        start=(ci == 0), stop=(ci == 31)),
                        st_, 1 if ci == 31 else 0, name)
                rel(("ps4q", q), f"p1b.vev.{s}.{dvb}")

        if mp < 2:
            return
        def emit_qtr(db, s):
            ctx.wait(te, f"p2a.rope.{db}.{s}")
            half = s % 2
            use(("pb1", half), te)
            for i in range(4):
                ctx.op(lambda half=half, i=i, s=s: te.transpose(
                    pbh(half, 0)[:, i * 128:(i + 1) * 128],
                    qrot2[s % 2][:, i * 128:(i + 1) * 128], ident[:]),
                    st_, 1 if i == 3 else 0,
                    f"p2a.qtr.{db}.{s}" if i == 3 else None)
            rel(("pb1", half), f"p2a.qtev.{db}.{s}")

        prev_q = None
        for db in range(8):
            for g in range(NG):
                for sl in range(4):
                    s = g * 4 + sl
                    if g == 0 and sl == 0:
                        ctx.wait(te, f"p2a.wq.{db}.1")
                    q = s % 4
                    use(("ps4q", q), te)
                    for ci in range(32):
                        if db == 0 and g == 0 and sl == 0:
                            ctx.wait(te, f"p2a.n1t.{ci}")
                        ctx.op(lambda q=q, ci=ci, s=s, db=db: te.matmul(
                            ps4[:, q * 512:(q + 1) * 512],
                            n1t_f[:, ci * 2048 + s * 128:
                                  ci * 2048 + s * 128 + 128],
                            wq_sb[:, (db % 2) * 16384 + ci * 512:
                                  (db % 2) * 16384 + (ci + 1) * 512],
                            start=(ci == 0), stop=(ci == 31)),
                            st_, 1 if ci == 31 else 0,
                            f"p2a.qmm.{db}.{s}" if ci == 31 else None)
                    rel(("ps4q", q), f"p2a.rope.{db}.{s}")
                    if prev_q is not None:
                        emit_qtr(*prev_q)
                    prev_q = (db, s)
        emit_qtr(*prev_q)


        if mp < 3:
            return
        rel(("sc", 0), "p2a.rope.7.13")
        rel(("sc", 1), "p2a.rope.7.15")
        rel(("psA",), res.get(("pb1", 0)))
        rel(("psC",), res.get(("pb1", 1)))
        rel(("psB",), "p2d.none" if False else None)
        res[("psB",)] = res.get(("psB",))
        def p2b_scores(j, h, gi):
            par = h % 2
            kvh = h // 4
            use(("sc", par), te)
            for p in range(2):
                kc = 2 * gi + p
                o = max(0, kc * 128 - j * 512)
                ctx.op(lambda par=par, p=p, kc=kc, kvh=kvh, h=h, j=j,
                       o=o: te.matmul(
                           ps4[:, par * 1024 + p * 512 + o:
                               par * 1024 + (p + 1) * 512],
                           ktc_sb[:, kvh * 2048 + kc * 128:
                                  kvh * 2048 + (kc + 1) * 128],
                           qt_sb[j % 2][:, h * 512 + o:(h + 1) * 512],
                           start=True, stop=True),
                       st_, 1 if p == 1 else 0,
                       f"p2b.sc.{j}.{h}.{gi}" if p == 1 else None)
            rel(("sc", par), f"p2b.exp.{j}.{h}.{gi}")

        def p2b_avsums(j, h, gi, nck):
            par = h % 2
            kvh = h // 4
            half = gi % 2
            psAC, kA = (psA, ("psA",)) if par == 0 else (psC, ("psC",))
            psBD, kB = (psB, ("psB",)) if par == 0 else (psD, ("psD",))
            ctx.wait(te, f"p2b.exp.{j}.{h}.{gi}")
            for p in range(2):
                kc = 2 * gi + p
                diag = kc >= 4 * j
                o = kc * 128 - j * 512 if diag else 0
                if diag:
                    ctx.wait(te, f"p2b.msk.{j}.{h}.{kc}")
                if kc == 0:
                    use(kA, te)
                    use(kB, te)
                last = (kc == nck - 1)
                e_sl = expT2[par][:, half * 1024 + p * 512 + o:
                                 half * 1024 + (p + 1) * 512]
                ctx.op(lambda kc=kc, o=o, e_sl=e_sl, kvh=kvh, last=last,
                       psAC=psAC: te.matmul(
                           psAC[:, o:512],
                           vc_sb[:, kc * 1024 + kvh * 128:
                                 kc * 1024 + (kvh + 1) * 128],
                           e_sl, start=(kc == 0), stop=last),
                       st_, 0)
                ctx.op(lambda kc=kc, o=o, e_sl=e_sl, last=last,
                       psBD=psBD: te.matmul(
                    psBD[:, o:512], ones2[:], e_sl,
                    start=(kc == 0), stop=last),
                    st_, 1 if p == 1 else 0,
                    f"p2b.av.{j}.{h}.{gi}" if p == 1 else None)
            if gi == nck // 2 - 1:
                rel(kA, f"p2b.outev.{j}.{h}")
                rel(kB, f"p2b.lns.{j}.{h}")

        for j in range(4):
            nck = 4 * j + 4
            ngrp = nck // 2
            for hp in range(0, H, 2):
                if hp == 0:
                    ctx.wait(te, f"p2b.qt.{j}")
                    if j == 0:
                        ctx.wait(te, "p2b.kv")
                for gi in range(ngrp):
                    p2b_scores(j, hp, gi)
                    p2b_scores(j, hp + 1, gi)
                    if gi > 0:
                        p2b_avsums(j, hp, gi - 1, nck)
                        p2b_avsums(j, hp + 1, gi - 1, nck)
                p2b_avsums(j, hp, ngrp - 1, nck)
                p2b_avsums(j, hp + 1, ngrp - 1, nck)


        if mp < 4:
            return
        rel(("ps4q", 0), "p2b.exp.3.30.7")
        rel(("ps4q", 1), "p2b.exp.3.30.7")
        rel(("ps4q", 2), "p2b.exp.3.31.7")
        rel(("ps4q", 3), "p2b.exp.3.31.7")
        for db in range(8):
            for s in range(NS):
                t0 = s * 128
                if s == 0:
                    ctx.wait(te, f"p2c.wo.{db}.1")
                q = s % 4
                use(("ps4q", q), te)
                for ci in range(32):
                    if db == 0 and s == 0:
                        ctx.wait(te, f"p2c.ot.{ci}")
                    ctx.op(lambda q=q, ci=ci, t0=t0, db=db: te.matmul(
                        ps4[:, q * 512:(q + 1) * 512],
                        ot_sb[:, ci * 2048 + t0: ci * 2048 + t0 + 128],
                        wo_sb[:, (db % 2) * 16384 + ci * 512:
                              (db % 2) * 16384 + (ci + 1) * 512],
                        start=(ci == 0), stop=(ci == 31)),
                        st_, 1 if ci == 31 else 0,
                        f"p2c.omm.{db}.{s}" if ci == 31 else None)
                rel(("ps4q", q), f"p2c.ev.{db}.{s}")

        if mp < 5:
            return
        rel(("pb1", 0), res.get(("psC",)))
        rel(("pb1", 1), res.get(("psD",)))

        def p2d_tr(s):
            for b in range(4):
                if b == 0:
                    ctx.wait(te, f"p2d.n2.{s}")
                half = b % 2
                use(("pb1", half), te)
                for i in range(8):
                    ci = 8 * b + i
                    ctx.op(lambda half=half, ci=ci, i=i, s=s: te.transpose(
                        pbh(half, 1)[:, i * 128:(i + 1) * 128],
                        n2_2[s % 2][:, ci * 128:(ci + 1) * 128], ident[:]),
                        st_, 1 if i == 7 else 0,
                        f"p2d.tr.{s}.{b}" if i == 7 else None)
                rel(("pb1", half), f"p2d.n2ev.{s}.{b}")

        def p2d_gmm(s):
            ctx.wait(te, f"p2d.n2ev.{s}.3")
            if s == 0:
                ctx.wait(te, "p2d.gw")
                use(("psB",), te)
            else:
                ctx.wait(te, f"p2d.probs.{s - 1}")
            for ci in range(32):
                ctx.op(lambda ci=ci, s=s: te.matmul(
                    psEa[:, 512:520], n2t2[s % 2][:, ci * 128:(ci + 1) * 128],
                    gw_sb[:, ci * 8:(ci + 1) * 8],
                    start=(ci == 0), stop=(ci == 31)),
                    st_, 1 if ci == 31 else 0, f"p2d.gmm.{s}" if ci == 31 else None)

        def p2d_amm(s):
            if s == 0:
                ctx.wait(te, "p2d.at")
                use(("psA",), te)
            for ci in range(32):
                ctx.op(lambda ci=ci, s=s: te.matmul(
                    psEa[:, 0:512], n2t2[s % 2][:, ci * 128:(ci + 1) * 128],
                    aT_sb[:, ci * 512:(ci + 1) * 512],
                    start=(ci == 0), stop=(ci == 31)),
                    st_, 1 if ci == 31 else 0, f"p2d.amm.{s}" if ci == 31 else None)

        def p2d_htr(s):
            ctx.wait(te, f"p2d.hsc.{s}")
            if s >= 1:
                ctx.wait(te, f"p2d.htev.{s - 1}")
            for i in range(4):
                ctx.op(lambda i=i, s=s: te.transpose(
                    psE_bf[:, 1536 + i * 128:1536 + (i + 1) * 128],
                    h_2[s % 2][:, i * 128:(i + 1) * 128], ident[:]),
                    st_, 1 if i == 3 else 0, f"p2d.htr.{s}" if i == 3 else None)

        def p2d_bmm(s):
            ctx.wait(te, f"p2d.htev.{s}")
            if s == 0:
                ctx.wait(te, "p2d.bt")
            for db in range(8):
                q = db % 4
                use(("ps4q", q), te)
                for rc in range(4):
                    ctx.op(lambda q=q, rc=rc, db=db, s=s: te.matmul(
                        ps4[:, q * 512:(q + 1) * 512],
                        hT_2[s % 2][:, rc * 128:(rc + 1) * 128],
                        bT_sb[:, rc * 4096 + db * 512: rc * 4096 + (db + 1) * 512],
                        start=(rc == 0), stop=(rc == 3)),
                        st_, 1 if rc == 3 else 0,
                        f"p2d.bmm.{s}.{db}" if rc == 3 else None)
                rel(("ps4q", q), f"p2d.bev.{s}.{db}")

        for s in range(NS):
            p2d_tr(s)
            if s >= 1:
                p2d_htr(s - 1)
                p2d_bmm(s - 1)
            p2d_gmm(s)
            p2d_amm(s)
        p2d_htr(NS - 1)
        p2d_bmm(NS - 1)

    def emit_scalar(sca):
        for s in range(NS):
            ctx.wait(sca, f"p1a.stats.{s}")
            fin = emit_ln_scalar_pre(sca, s % 2)
            ctx.op(fin, ss, 1, f"p1a.rstd.{s}")
            for b in range(4):
                half = b % 2
                ctx.wait(sca, f"p1a.tr.{s}.{b}")
                if s % 4 == 0 and b == 0 and s >= 4:
                    ctx.wait(sca, f"p1a.n1tw.{s // 4 - 1}")
                dst = n1tg.rearrange("p (c t) -> p c t", c=32)[
                    :, 8 * b: 8 * b + 8, (s % 4) * 128:(s % 4 + 1) * 128]
                srcp = pbh(half, 0).rearrange("p (c t) -> p c t", c=8)
                ctx.op(lambda dst=dst, srcp=srcp: sca.copy(dst, srcp),
                       ss, 1, f"p1a.n1ev.{s}.{b}")
            for sk in ([s - 1] if s >= 1 else []) + \
                    ([NS - 1] if s == NS - 1 else []):
                half = sk % 2
                ctx.wait(sca, f"p1a.ktr.{sk}.0")
                if sk % 4 == 0 and sk >= 4:
                    ctx.wait(sca, f"p1a.ktcw.{sk // 4 - 1}")
                dst = ktcg.rearrange("p (h t) -> p h t", h=KVH)[
                    :, :, (sk % 4) * 128:(sk % 4 + 1) * 128]
                srcp = pbh(half, 0).rearrange("p (h t) -> p h t", h=8)
                ctx.op(lambda dst=dst, srcp=srcp: sca.copy(dst, srcp),
                       ss, 1, f"p1a.kev.{sk}.0")
        if mp < 3:
            return
        ctx.wait(sca, "p1a.n1tw.2")
        ctx.wait(sca, "p1a.n1tw.3")
        ctx.wait(sca, "p1b.vmm.11")
        for ci in range(12, 24):
            ctx.op(lambda ci=ci: sca.dma_start(
                out=n1t_f[:, ci * 2048:(ci + 1) * 2048], in_=n1t_d[ci, :, :]),
                S["s_ab"], 16, f"p2a.n1t.{ci}")
        for db in range(8):
            for s in range(NS):
                bi = s % 2
                half = s % 2
                ctx.wait(sca, f"p2a.qtr.{db}.{s}")
                use(("qtw", bi), sca)
                ctx.op(lambda bi=bi, half=half: sca.copy(
                    qtw[bi], pbh(half, 0)[:, 0:512]),
                    ss, 1, f"p2a.qtev.{db}.{s}")
                rel(("qtw", bi), f"p2a.qtw.{db}.{s}")
        def p2b_exp(j, h, gi):
            par = h % 2
            half = gi % 2
            ctx.wait(sca, f"p2b.sc.{j}.{h}.{gi}")
            use(("expT", par, half), sca)
            o0 = max(0, 2 * gi * 128 - j * 512)
            o1 = max(0, (2 * gi + 1) * 128 - j * 512)
            if o0 == 0 and o1 == 0:
                ctx.op(lambda par=par, half=half: sca.activation(
                    out=expT2[par][:, half * 1024:(half + 1) * 1024],
                    in_=ps4[:, par * 1024:(par + 1) * 1024],
                    func=AF.Exp, scale=ISQ),
                    ss, 1, f"p2b.exp.{j}.{h}.{gi}")
            else:
                for p, o in ((0, o0), (1, o1)):
                    ctx.op(lambda par=par, half=half, p=p, o=o: sca.activation(
                        out=expT2[par][:, half * 1024 + p * 512 + o:
                                       half * 1024 + (p + 1) * 512],
                        in_=ps4[:, par * 1024 + p * 512 + o:
                                par * 1024 + (p + 1) * 512],
                        func=AF.Exp, scale=ISQ),
                        ss, 1 if p == 1 else 0,
                        f"p2b.exp.{j}.{h}.{gi}" if p == 1 else None)
            rel(("expT", par, half), f"p2b.av.{j}.{h}.{gi}")

        def p2b_tail_s(j, h, ngrp):
            psBD = psB if h % 2 == 0 else psD
            ctx.wait(sca, f"p2b.av.{j}.{h}.{ngrp - 1}")
            ctx.op(lambda psBD=psBD: sca.activation(
                out=t_sb, in_=psBD[:, 0:512], func=AF.Ln),
                ss, 1, f"p2b.lns.{j}.{h}")
            use(("bcs", h % 2), sca)
            ctx.op(lambda h=h: sca.activation(
                out=bc2[h % 2], in_=t_sb, func=AF.Exp, scale=-1.0),
                ss, 1, f"p2b.bcs.{j}.{h}")
            rel(("bcs", h % 2), f"p2b.outev.{j}.{h}")

        for j in range(4):
            ngrp = (4 * j + 4) // 2
            for hp in range(0, H, 2):
                for gi in range(ngrp):
                    p2b_exp(j, hp, gi)
                    p2b_exp(j, hp + 1, gi)
                p2b_tail_s(j, hp, ngrp)
                p2b_tail_s(j, hp + 1, ngrp)
        if mp < 4:
            return
        ctx.wait(sca, "p2b.av.3.31.7")
        ctx.wait(sca, "p2b.otw.3.31")
        ctx.wait(sca, "p2b.otw.3.30")
        for ci in range(16, 32):
            ctx.op(lambda ci=ci: sca.dma_start(
                out=ot_sb[:, ci * 2048:(ci + 1) * 2048], in_=ot_d[ci, :, :]),
                S["s_ot"], 16, f"p2c.ot.{ci}")
        if mp < 5:
            return
        ctx.wait(sca, "p2d.stats.0")
        fin = emit_ln_scalar_pre(sca, 0)
        ctx.op(fin, ss, 1, "p2d.rstd.0")
        for s in range(NS):
            for b in (0, 1):
                ctx.wait(sca, f"p2d.tr.{s}.{b}")
                ctx.op(lambda s=s, b=b: sca.copy(
                    n2t2[s % 2][:, b * 1024:(b + 1) * 1024], pbh(b % 2, 1)),
                    ss, 1, f"p2d.n2ev.{s}.{b}")
            if s >= 1:
                ctx.wait(sca, f"p2d.htr.{s - 1}")
                ctx.op(lambda s=s: sca.copy(
                    hT_2[(s - 1) % 2], psE_bf[:, 1536:2048]),
                    ss, 1, f"p2d.htev.{s - 1}")
            for b in (2, 3):
                ctx.wait(sca, f"p2d.tr.{s}.{b}")
                ctx.op(lambda s=s, b=b: sca.copy(
                    n2t2[s % 2][:, b * 1024:(b + 1) * 1024], pbh(b % 2, 1)),
                    ss, 1, f"p2d.n2ev.{s}.{b}")
            if s + 1 < NS:
                ctx.wait(sca, f"p2d.stats.{s + 1}")
                fin = emit_ln_scalar_pre(sca, (s + 1) % 2)
                ctx.op(fin, ss, 1, f"p2d.rstd.{s + 1}")
            ctx.wait(sca, f"p2d.gmm.{s}")
            if s >= 1:
                ctx.wait(sca, f"p2d.g5.{s - 1}")
            ctx.op(lambda: sca.activation(out=probs, in_=psEa[:, 512:520],
                                          func=AF.Exp),
                   ss, 1, f"p2d.probs.{s}")
        ctx.wait(sca, f"p2d.htr.{NS - 1}")
        ctx.op(lambda: sca.copy(hT_2[(NS - 1) % 2], psE_bf[:, 1536:2048]),
               ss, 1, f"p2d.htev.{NS - 1}")
        if ctx.emit:
            for nm in ("s_ab", "s_ot"):
                if sems[nm].val > 0:
                    sca.wait_ge(sems[nm].h, sems[nm].val)

    def emit_vector(vec):
        def emit_ropek(s):
            bi = s % 2
            ctx.wait(vec, f"p1a.kmm.{s}")
            ctx.wait(vec, f"p1a.cs.{s}")
            if s >= 2:
                ctx.wait(vec, f"p1a.ktr.{s - 2}.0")
            kb = s % 2
            fin = emit_rope(vec, ps4[:, kb * 1024:(kb + 1) * 1024],
                            p1_cs[:, bi * 512:(bi + 1) * 512],
                            p1_tmp, krot2[bi], 8)
            ctx.op(fin, sv, 1, f"p1a.ropek.{s}")

        for s in range(NS):
            bi = s % 2
            ctx.wait(vec, f"p1a.x.{s}")
            if s >= 2:
                ctx.wait(vec, f"p1a.rstd.{s - 2}")
            fin = emit_bn(vec, xb_p1[:, bi * 4096:(bi + 1) * 4096], bi)
            ctx.op(fin, sv, 1, f"p1a.stats.{s}")
            ctx.wait(vec, f"p1a.rstd.{s}")
            if s >= 2:
                ctx.wait(vec, f"p1a.tr.{s - 2}.3")
            ctx.op(lambda bi=bi: vec.tensor_scalar(
                n1_2[bi], xb_p1[:, bi * 4096:(bi + 1) * 4096],
                statv(bi, 0), statv(bi, 3), ALU.subtract, ALU.mult),
                sv, 1, f"p1a.n1.{s}")
            if s >= 1:
                emit_ropek(s - 1)
        emit_ropek(NS - 1)

        if mp < 1:
            return
        sseq = list(range(12, 16)) + list(range(12))
        for si, s in enumerate(sseq):
            for dvb in range(2):
                q = (2 * s + dvb) % 4
                ctx.wait(vec, f"p1b.vmm0.{s}" if dvb == 0 else f"p1b.vmm.{s}")
                if si == 0 and dvb == 0:
                    ctx.wait(vec, "p1a.ktcw.2")
                    ctx.wait(vec, "p1a.ktcw.3")
                if si >= 2 and dvb == 0:
                    ctx.wait(vec, f"p1b.vcw.{sseq[si - 2]}")
                ctx.op(lambda s=s, dvb=dvb, q=q: vec.tensor_copy(
                    vcb[:, (s % 2) * 1024 + dvb * 512:
                        (s % 2) * 1024 + (dvb + 1) * 512],
                    ps4[:, q * 512:(q + 1) * 512]),
                    sv, 1, f"p1b.vev.{s}.{dvb}")

        if mp < 2:
            return
        prevroq = {}
        for db in range(8):
            for s in range(NS):
                bi = s % 2
                q = s % 4
                ctx.wait(vec, f"p2a.qmm.{db}.{s}")
                ctx.wait(vec, f"p2a.cs.{db}.{s}")
                if bi in prevroq:
                    ctx.wait(vec, prevroq[bi])
                fin = emit_rope(vec, ps4[:, q * 512:(q + 1) * 512],
                                p2a_cs[:, bi * 512:(bi + 1) * 512],
                                p2a_tmp, qrot2[bi], 4)
                ctx.op(fin, sv, 1, f"p2a.rope.{db}.{s}")
                prevroq[bi] = f"p2a.qtr.{db}.{s}"

        if mp < 3:
            return
        def p2b_masks(j, h):
            par = h % 2
            nck = 4 * j + 4
            for kc in range(4 * j, nck):
                gi = kc // 2
                half = gi % 2
                o = kc * 128 - j * 512
                ctx.wait(vec, f"p2b.exp.{j}.{h}.{gi}")
                sl_ = expT2[par][:, half * 1024 + (kc % 2) * 512 + o:
                                half * 1024 + (kc % 2) * 512 + o + 128]
                ctx.op(lambda sl_=sl_: vec.tensor_tensor(
                    sl_, sl_, tri[:], ALU.mult),
                    sv, 1, f"p2b.msk.{j}.{h}.{kc}")

        def p2b_outev(j, h):
            ctx.wait(vec, f"p2b.bcs.{j}.{h}")
            psAC = psA if h % 2 == 0 else psC
            use(("outw", h % 2), vec)
            ctx.op(lambda h=h, psAC=psAC: vec.tensor_tensor(
                outw[h % 2], psAC[:, 0:512], bc2[h % 2], ALU.mult),
                sv, 1, f"p2b.outev.{j}.{h}")
            rel(("outw", h % 2), f"p2b.otw.{j}.{h}")

        for j in range(4):
            for hp in range(0, H, 2):
                p2b_masks(j, hp)
                p2b_masks(j, hp + 1)
                p2b_outev(j, hp)
                p2b_outev(j, hp + 1)

        if mp < 4:
            return
        for db in range(8):
            for s in range(NS):
                q = s % 4
                bi = s % 2
                ctx.wait(vec, f"p2c.omm.{db}.{s}")
                ctx.wait(vec, f"p2c.x.{db}.{s}")
                use(("xpw", bi), vec)
                ctx.op(lambda q=q, bi=bi: vec.tensor_tensor(
                    xpw[bi], ps4[:, q * 512:(q + 1) * 512], xsb[bi], ALU.add),
                    sv, 1, f"p2c.ev.{db}.{s}")
                rel(("xpw", bi), f"p2c.xpw.{db}.{s}")

        if mp < 5:
            return
        if mp < 5:
            return
        def p2d_stats(s):
            ctx.wait(vec, f"p2d.xp.{s}")
            fin = emit_bn(vec, xp_sb[:, (s % 4) * 4096:(s % 4 + 1) * 4096],
                          s % 2)
            ctx.op(fin, sv, 1, f"p2d.stats.{s}")

        def p2d_n2(s):
            ctx.wait(vec, f"p2d.rstd.{s}")
            if s <= 1:
                ctx.wait(vec, "p2c.omm.7.15")
            if s >= 2:
                ctx.wait(vec, f"p2d.tr.{s - 2}.3")
            ctx.op(lambda s=s: vec.tensor_scalar(
                n2_2[s % 2], xp_sb[:, (s % 4) * 4096:(s % 4 + 1) * 4096],
                statv(s % 2, 0), statv(s % 2, 3), ALU.subtract, ALU.mult),
                sv, 1, f"p2d.n2.{s}")

        def p2d_gate_hsc(s):
            ctx.wait(vec, f"p2d.probs.{s}")
            if s >= 2:
                ctx.wait(vec, f"p2d.htr.{s - 2}")
            def vedge(f, tag, s=s):
                ctx.op(f, sv, 1, f"p2d.g{tag}.{s}")
                ctx.wait(vec, f"p2d.g{tag}.{s}")
            vedge(lambda: vec.max(ltm, probs), 1)
            vedge(lambda: vec.tensor_scalar(keept, probs, ltm[:, 1:2], None,
                                            ALU.is_ge), 2)
            vedge(lambda: vec.tensor_scalar(wsum, ltm[:, 0:1], ltm[:, 1:2],
                                            None, ALU.add), 3)
            vedge(lambda: vec.reciprocal(winv, wsum), 4)
            vedge(lambda: vec.tensor_tensor(wful, probs, keept, ALU.mult), 5)
            vedge(lambda: vec.tensor_scalar(wful, wful, winv, None, ALU.mult), 6)
            ctx.wait(vec, f"p2d.amm.{s}")
            for e in range(NE):
                ctx.op(lambda e=e, s=s: vec.tensor_scalar(
                    h_2[s % 2][:, e * 64:(e + 1) * 64],
                    psEa[:, e * 64:(e + 1) * 64],
                    wful[:, e:e + 1], None, ALU.mult),
                    sv, 1 if e == 7 else 0, f"p2d.hsc.{s}" if e == 7 else None)

        def p2d_bev(s):
            for db in range(8):
                q = db % 4
                ctx.wait(vec, f"p2d.bmm.{s}.{db}")
                if s >= 2 and db == 0:
                    ctx.wait(vec, f"out.{s - 2}")
                ctx.op(lambda s=s, db=db, q=q: vec.tensor_tensor(
                    outb[:, (s % 2) * 4096 + db * 512:
                         (s % 2) * 4096 + (db + 1) * 512],
                    ps4[:, q * 512:(q + 1) * 512],
                    xp_sb[:, (s % 4) * 4096 + db * 512:
                          (s % 4) * 4096 + (db + 1) * 512],
                    ALU.add),
                    sv, 1, f"p2d.bev.{s}.{db}")

        p2d_stats(0)
        p2d_n2(0)
        for s in range(NS):
            if s >= 1:
                p2d_gate_hsc(s - 1)
            if s + 1 < NS:
                p2d_stats(s + 1)
            if s >= 1:
                p2d_bev(s - 1)
            if s + 1 < NS:
                p2d_n2(s + 1)
        p2d_gate_hsc(NS - 1)
        p2d_bev(NS - 1)

    emitters = [emit_sync, emit_gpsimd, emit_tensor, emit_scalar, emit_vector]
    ctx.start_pass(emit=False)
    res.clear()
    for f in emitters:
        f(None)
    ctx.finish_pass1()

    sem_nums = sorted(s.h.num for s in sems.values())
    assert sem_nums == list(range(sem_nums[0], sem_nums[0] + len(sem_nums)))
    srange = range(sem_nums[0], sem_nums[-1] + 1)
    nc.all_engine_barrier()
    nc.gpsimd.dma_reset(srange)
    nc.gpsimd.sem_clear(srange)
    nc.all_engine_barrier()

    ctx.start_pass(emit=True)
    res.clear()
    with nc.Block() as block:
        @block.sync
        def _(sync):
            emit_sync(sync)

        @block.gpsimd
        def _(gp):
            emit_gpsimd(gp)

        @block.tensor
        def _(te):
            emit_tensor(te)

        @block.scalar
        def _(sca):
            emit_scalar(sca)

        @block.vector
        def _(vec):
            emit_vector(vec)

    return nc


def host_inputs(L0_attn_x, L0_attn_Wq, L0_attn_Wk, L0_attn_Wv, L0_attn_Wo,
                L0_ffn_gate_w, L0_ffn_lora_A, L0_ffn_lora_B):
    cos, sin = _llama3_rope_tables()
    gwt = np.ascontiguousarray(
        np.asarray(L0_ffn_gate_w, dtype=np.float32).T.reshape(32, 128, 8)
        .transpose(1, 0, 2).reshape(128, 256))
    shared = {
        "wqt": np.ascontiguousarray(np.asarray(L0_attn_Wq, np.float32).T),
        "wkt": np.ascontiguousarray(np.asarray(L0_attn_Wk, np.float32).T),
        "wvt": np.ascontiguousarray(np.asarray(L0_attn_Wv, np.float32).T),
        "wot": np.ascontiguousarray(np.asarray(L0_attn_Wo, np.float32).T),
        "gwt": gwt,
        "at": np.ascontiguousarray(
            np.asarray(L0_ffn_lora_A, np.float32).transpose(2, 0, 1)
            .reshape(D, NE * RK)),
        "bt": np.ascontiguousarray(
            np.asarray(L0_ffn_lora_B, np.float32).transpose(0, 2, 1)
            .reshape(NE * RK, D)),
        "cos": cos,
        "sin": sin,
        "idt_d": np.eye(128, dtype=BF),
        "tri_d": np.ascontiguousarray(
            (np.arange(128)[None, :] >= np.arange(128)[:, None]).astype(BF)),
        "ones2_d": np.ones((128, 128), dtype=BF),
        "cvec_d": np.full((128, 2), EPS, dtype=np.float32),
    }
    x = np.asarray(L0_attn_x, np.float32)
    in_maps = []
    for c in range(8):
        m = dict(shared)
        m["x"] = np.ascontiguousarray(x[c])
        in_maps.append(m)
    return in_maps


def kernel(L0_attn_x, L0_attn_Wq, L0_attn_Wk, L0_attn_Wv, L0_attn_Wo,
           L0_ffn_gate_w, L0_ffn_lora_A, L0_ffn_lora_B):
    in_maps = host_inputs(L0_attn_x, L0_attn_Wq, L0_attn_Wk, L0_attn_Wv,
                          L0_attn_Wo, L0_ffn_gate_w, L0_ffn_lora_A, L0_ffn_lora_B)
    nc = build_program(dbg=False)
    res = run_bass_kernel_spmd(nc, in_maps, core_ids=list(range(8)))
    out = np.stack([np.asarray(r["out"], dtype=np.float32) for r in res.results])
    return out


if __name__ == "__main__":
    nc = build_program(dbg=False)
    print("built ok")


# version 20
# speedup vs baseline: 1.0629x; 1.0092x over previous
import math
import sys

sys.path.insert(0, "/opt/trn_rl_repo")

import numpy as np
import ml_dtypes

import concourse.bass as bass
import concourse.mybir as mybir
from concourse.bass_utils import run_bass_kernel_spmd

F32 = mybir.dt.float32
BF16 = mybir.dt.bfloat16
AF = mybir.ActivationFunctionType
ALU = mybir.AluOpType
AX = mybir.AxisListType
BF = ml_dtypes.bfloat16

T = 2048
D = 4096
HD = 128
H = 32
KVH = 8
DKV = 1024
NE = 8
RK = 64
NS = 16
NG = 4
EPS = 1e-5
ISQ = 1.0 / math.sqrt(HD)


class Sem:
    def __init__(self, nc, name):
        self.h = nc.alloc_semaphore(name=name)
        self.name = name
        self.val = 0


class Ctx:

    def __init__(self, sems):
        self.sems = sems
        self.ck = {}
        self.first = True
        self.emit = False
        self.pending = []

    def start_pass(self, emit):
        self.emit = emit
        self.cnt = {}
        for s in self.sems.values():
            s.val = 0

    def nid(self, tag):
        self.cnt[tag] = self.cnt.get(tag, 0) + 1
        return f"{tag}.{self.cnt[tag]}" 

    def op(self, f, sem=None, n=0, name=None):
        ins = f() if self.emit else None
        if sem is not None and n > 0:
            sem.val += n
            if name is not None:
                if self.first:
                    assert name not in self.ck, f"dup ck {name}"
                    self.ck[name] = (sem.name, sem.val)
                else:
                    assert self.ck[name] == (sem.name, sem.val), (
                        f"ck mismatch {name}")
            if self.emit:
                ins.then_inc(sem.h, n)
        return ins

    def wait(self, eng, name):
        if not self.emit:
            self.pending.append(name)
            return
        sname, val = self.ck[name]
        eng.wait_ge(self.sems[sname].h, val)

    def finish_pass1(self):
        missing = [n for n in self.pending if n not in self.ck]
        assert not missing, f"unresolved checkpoints: {sorted(set(missing))[:10]}"
        self.first = False


def _llama3_rope_tables():
    base = 500000.0
    inv_freq = 1.0 / base ** (np.arange(0, HD, 2, dtype=np.float32) / HD)
    factor, low_f, high_f, old_ctx = 32.0, 1.0, 4.0, 8192
    low_wl, high_wl = old_ctx / low_f, old_ctx / high_f
    wavelen = 2 * math.pi / inv_freq
    inv_llama = np.where(wavelen > low_wl, inv_freq / factor, inv_freq)
    smooth = (old_ctx / wavelen - low_f) / (high_f - low_f)
    smoothed = (1 - smooth) * inv_llama / factor + smooth * inv_llama
    is_med = ~(wavelen < high_wl) & ~(wavelen > low_wl)
    inv = np.where(is_med, smoothed, inv_llama).astype(np.float32)
    pos = np.arange(T, dtype=np.float32)
    freqs = pos[:, None] * inv[None, :]
    emb = np.concatenate([freqs, freqs], axis=-1)
    return np.cos(emb).astype(np.float32), np.sin(emb).astype(np.float32)


def build_program(dbg=False, mp=5, ndb=8, p2as=5):
    nc = bass.Bass("TRN2", target_bir_lowering=False)

    X = nc.dram_tensor("x", [T, D], F32, kind="ExternalInput")
    WQT = nc.dram_tensor("wqt", [D, D], F32, kind="ExternalInput")
    WKT = nc.dram_tensor("wkt", [D, DKV], F32, kind="ExternalInput")
    WVT = nc.dram_tensor("wvt", [D, DKV], F32, kind="ExternalInput")
    WOT = nc.dram_tensor("wot", [D, D], F32, kind="ExternalInput")
    AT = nc.dram_tensor("at", [D, NE * RK], F32, kind="ExternalInput")
    GWT = nc.dram_tensor("gwt", [128, 256], F32, kind="ExternalInput")
    BT = nc.dram_tensor("bt", [NE * RK, D], F32, kind="ExternalInput")
    COS = nc.dram_tensor("cos", [T, HD], F32, kind="ExternalInput")
    SIN = nc.dram_tensor("sin", [T, HD], F32, kind="ExternalInput")
    IDT = nc.dram_tensor("idt_d", [128, 128], BF16, kind="ExternalInput")
    TRI = nc.dram_tensor("tri_d", [128, 128], BF16, kind="ExternalInput")
    ONES2 = nc.dram_tensor("ones2_d", [128, 128], BF16, kind="ExternalInput")
    CVEC = nc.dram_tensor("cvec_d", [128, 2], F32, kind="ExternalInput")
    OUT = nc.dram_tensor("out", [T, D], F32, kind="ExternalOutput")

    skind = "ExternalOutput" if dbg else "Internal"
    n1t_d = nc.dram_tensor("n1t_d", [32, 128, T], BF16, kind=skind)
    ktc_d = nc.dram_tensor("ktc_d", [KVH, 128, T], BF16, kind=skind)
    vc_d = nc.dram_tensor("vc_d", [NS, 128, DKV], BF16, kind=skind)
    qt_d = nc.dram_tensor("qt_d", [H, 128, T], BF16, kind=skind)
    ot_d = nc.dram_tensor("ot_d", [32, 128, T], BF16, kind=skind)
    xp_d = nc.dram_tensor("xp_d", [T, D], F32, kind=skind)

    BIG = nc.alloc_sbuf_tensor("big", [128, 65536], BF16)
    MED = nc.alloc_sbuf_tensor("med", [128, 32768], BF16)
    SMF = nc.alloc_sbuf_tensor("smf", [128, 2048], F32)
    SMB = nc.alloc_sbuf_tensor("smb", [128, 2048], BF16)
    STT = nc.alloc_sbuf_tensor("stt", [128, 160], F32)
    ident = nc.alloc_sbuf_tensor("ident", [128, 128], BF16)
    tri = nc.alloc_sbuf_tensor("tri", [128, 128], BF16)
    ones2 = nc.alloc_sbuf_tensor("ones2", [128, 128], BF16)
    cvec = nc.alloc_sbuf_tensor("cvec", [128, 2], F32)

    ps4 = nc.alloc_psum_tensor("ps4", [128, 2048], F32)
    psE = nc.alloc_psum_tensor("psE", [128, 1024], F32)
    psC = nc.alloc_psum_tensor("psC", [128, 512], F32)
    psD = nc.alloc_psum_tensor("psD", [128, 512], F32)
    psEa = psE.ap()
    psA = psEa[:, 0:512]
    psB = psEa[:, 512:1024]
    psE_bf = psEa.bitcast(BF16)

    pbA = psE_bf[:, 0:1024]
    pbB = psE_bf[:, 1024:2048]
    pbC = psC.ap().bitcast(BF16)
    pbD = psD.ap().bitcast(BF16)

    def pbh(half, ph=0):
        if ph == 0:
            return (pbA, pbC)[half]
        return (pbC, pbD)[half]

    def bigv(c0, c1):
        return BIG[:, c0:c1]

    wk_sb = bigv(0, 32768)
    wv_sb = bigv(0, 32768)
    n1tg = bigv(32768, 49152)
    ktcg = bigv(49152, 53248)
    krot2 = [bigv(53248, 54272), bigv(64512, 65536)]
    n1_2 = [bigv(54272, 58368), SMF.ap().bitcast(BF16)]
    p1_tmp = bigv(58368, 62464).bitcast(F32)
    p1_cs = bigv(62464, 64512).bitcast(F32)
    vcb = bigv(49152, 53248)
    xb_p1 = MED.ap().bitcast(F32)

    n1t_f = bigv(0, 65536)
    wq_sb = MED
    qrot2 = [SMB[:, 0:512], SMB[:, 1536:2048]]
    qtw = [SMB[:, 512:1024], SMB[:, 1024:1536]]
    p2a_tmp = SMF[:, 0:1024]
    p2a_cs = SMF[:, 1024:2048]

    ktc_sb = bigv(0, 16384)
    vc_sb = bigv(16384, 32768)
    qt_sb = [bigv(32768, 49152), bigv(49152, 65536)]
    expT2 = [MED[:, 16384:18432], MED[:, 20480:22528]]
    outw = [MED[:, 22528:23040], MED[:, 23040:23552]]
    bc2 = [SMF[:, 0:512], SMF[:, 1024:1536]]
    t_sb = SMF[:, 512:1024]

    ot_sb = bigv(0, 65536)
    wo_sb = MED
    xsb = [SMF[:, 0:512], SMF[:, 512:1024]]
    xpw = [SMF[:, 1024:1536], SMF[:, 1536:2048]]

    aT_sb = bigv(0, 16384)
    bT_sb = bigv(16384, 32768)
    outb = bigv(32768, 49152).bitcast(F32)
    n2t2 = [bigv(49152, 53248), bigv(53248, 57344)]
    n2_2 = [bigv(57344, 61440), bigv(61440, 65536)]
    xp_sb = MED.ap().bitcast(F32)
    gw_sb = SMF.ap().bitcast(BF16)[:, 3072:3328]
    h_2 = [SMB[:, 0:512], SMB[:, 512:1024]]
    hT_2 = [SMB[:, 1024:1536], SMB[:, 1536:2048]]

    def bnstv(par):
        return STT[:, par * 54: par * 54 + 48]

    def statv(par, i):
        return STT[:, par * 54 + 48 + i: par * 54 + 48 + i + 1]

    negm = STT[:, 108:109]
    m1 = STT[:, 109:110]
    m2 = STT[:, 110:111]
    wsum = STT[:, 111:112]
    winv = STT[:, 112:113]
    probs = STT[:, 114:122]
    ltm = STT[:, 122:130]
    pmask = STT[:, 130:138]
    keept = STT[:, 138:146]
    wful = STT[:, 146:154]

    sem_names = (["t", "v", "s", "ld_c", "ld_n1g", "ld_n1f", "ld_kv", "ld_ot",
                  "g_wk", "g_wv", "g_ab", "g_gw", "s_ab", "s_ot"]
                 + [f"{b}{i}" for b in ["ld_x", "ld_cs", "ld_qt", "ld_x2",
                                        "ld_xp", "g_out", "g_n1w", "g_kw",
                                        "g_vc", "g_wq", "g_qw", "g_ot",
                                        "g_wo", "g_xp"] for i in (0, 1)])
    sems = {n: Sem(nc, n) for n in sem_names}
    ctx = Ctx(sems)
    S = sems
    st_, sv, ss = sems["t"], sems["v"], sems["s"]

    res = {}

    def use(key, wait_eng):
        if res.get(key) is not None:
            ctx.wait(wait_eng, res[key])

    def rel(key, name):
        res[key] = name

    def emit_rope(vec, src_ap, cs_ap, tmp_ap, out_ap, nh):
        def v3(ap_, lo):
            return ap_.rearrange("p (h x) -> p h x", h=nh)[:, :, lo * 64:(lo + 1) * 64]

        def csb(lo, which):
            return cs_ap[:, which * 128 + lo * 64: which * 128 + (lo + 1) * 64] \
                .unsqueeze(1).broadcast_to([128, nh, 64])

        t1 = tmp_ap[:, 0: nh * 64].rearrange("p (h x) -> p h x", h=nh)
        t2 = tmp_ap[:, nh * 64: nh * 128].rearrange("p (h x) -> p h x", h=nh)
        ctx.op(lambda: vec.tensor_tensor(t1, v3(src_ap, 1), csb(0, 1), ALU.mult))
        ctx.op(lambda: vec.tensor_tensor(t2, v3(src_ap, 0), csb(0, 0), ALU.mult))
        ctx.op(lambda: vec.tensor_tensor(v3(out_ap, 0), t2, t1, ALU.subtract))
        ctx.op(lambda: vec.tensor_tensor(t1, v3(src_ap, 0), csb(1, 1), ALU.mult))
        ctx.op(lambda: vec.tensor_tensor(t2, v3(src_ap, 1), csb(1, 0), ALU.mult))
        return lambda: vec.tensor_tensor(v3(out_ap, 1), t2, t1, ALU.add)

    def emit_ln_scalar_pre(sca, par=0):
        nm = ctx.nid("lnln")
        ctx.op(lambda: sca.activation(out=statv(par, 2), in_=statv(par, 1),
                                      func=AF.Ln, bias=cvec[:, 0:1]), ss, 1, nm)
        ctx.wait(sca, nm)
        return lambda: sca.activation(out=statv(par, 3), in_=statv(par, 2),
                                      func=AF.Exp, scale=-0.5)

    def emit_bn(vec, x_ap, par=0):
        nm = ctx.nid("bnst")
        for i in range(8):
            ctx.op(lambda i=i, par=par: vec.bn_stats(
                bnstv(par)[:, i * 6:(i + 1) * 6],
                x_ap[:, i * 512:(i + 1) * 512]),
                sv, 1 if i == 7 else 0, nm if i == 7 else None)
        ctx.wait(vec, nm)
        return lambda: vec.bn_aggr(STT[:, par * 54 + 48: par * 54 + 50], bnstv(par))


    def sync_finish(sync):
        if ctx.emit:
            for nm, sm in sems.items():
                if nm.startswith("ld_") and sm.val > 0:
                    sync.wait_ge(sm.h, sm.val)

    def gp_finish(gp):
        if ctx.emit:
            for nm, sm in sems.items():
                if nm.startswith("g_") and sm.val > 0:
                    gp.wait_ge(sm.h, sm.val)

    def emit_sync(sync):
        consts = [(ident, IDT), (tri, TRI), (cvec, CVEC), (ones2, ONES2)]
        for i, (tns, src) in enumerate(consts):
            ctx.op(lambda tns=tns, src=src: sync.dma_start(out=tns[:], in_=src[:, :]),
                   S["ld_c"], 16, "consts" if i == 3 else None)

        for s in range(NS):
            t0 = s * 128
            bi = s % 2
            use(("xb", bi), sync)
            ctx.op(lambda bi=bi, t0=t0: sync.dma_start(
                out=xb_p1[:, bi * 4096:(bi + 1) * 4096], in_=X[t0:t0 + 128, :]),
                S[f"ld_x{bi}"], 16, f"p1a.x.{s}")
            rel(("xb", bi), f"p1a.n1.{s}")
            use(("cs1", bi), sync)
            ctx.op(lambda bi=bi, t0=t0: sync.dma_start(
                out=p1_cs[:, bi * 512: bi * 512 + 128], in_=COS[t0:t0 + 128, :]),
                S[f"ld_cs{bi}"], 16)
            ctx.op(lambda bi=bi, t0=t0: sync.dma_start(
                out=p1_cs[:, bi * 512 + 128: bi * 512 + 256], in_=SIN[t0:t0 + 128, :]),
                S[f"ld_cs{bi}"], 16, f"p1a.cs.{s}")
            rel(("cs1", bi), f"p1a.ropek.{s}")

        if mp < 1:
            sync_finish(sync)
            return
        for g in [0, 1, 2]:
            ctx.wait(sync, f"p1a.n1tw.{g}")
            if g == 0:
                ctx.wait(sync, f"p1b.vmm.15")
            else:
                ctx.wait(sync, f"p1b.vmm.{4 * g - 1}")
            for q4 in range(4):
                src = n1t_d[:, :, g * 512:(g + 1) * 512].rearrange(
                    "c p t -> p c t")[:, 8 * q4:8 * q4 + 8, :]
                dst = n1tg.rearrange("p (c t) -> p c t", c=32)[:, 8 * q4:8 * q4 + 8, :]
                ctx.op(lambda src=src, dst=dst: sync.dma_start(out=dst, in_=src),
                       S["ld_n1g"], 16, f"p1b.n1t.{g}" if q4 == 3 else None)

        if mp < 2:
            sync_finish(sync)
            return
        ctx.wait(sync, "p1b.vmm.11")
        ctx.wait(sync, "p1b.vev.11.1")
        ctx.wait(sync, "p1b.vcw.11")
        ctx.wait(sync, "p1b.vcw.10")
        ctx.wait(sync, "p1a.n1tw.2")
        ctx.wait(sync, "p1a.n1tw.3")
        for ci in range(32):
            ctx.op(lambda ci=ci: sync.dma_start(
                out=n1t_f[:, ci * 2048:(ci + 1) * 2048], in_=n1t_d[ci, :, :]),
                S["ld_n1f"], 16, f"p2a.n1t.{ci}")
        for db in range(ndb):
            for s in range(NS):
                t0 = s * 128
                bi = s % 2
                use(("cs2", bi), sync)
                ctx.op(lambda bi=bi, t0=t0: sync.dma_start(
                    out=p2a_cs[:, bi * 512: bi * 512 + 128], in_=COS[t0:t0 + 128, :]),
                    S[f"ld_cs{bi}"], 16)
                ctx.op(lambda bi=bi, t0=t0: sync.dma_start(
                    out=p2a_cs[:, bi * 512 + 128: bi * 512 + 256],
                    in_=SIN[t0:t0 + 128, :]), S[f"ld_cs{bi}"], 16,
                    f"p2a.cs.{db}.{s}")
                rel(("cs2", bi), f"p2a.rope.{db}.{s}")

        if mp < 3:
            sync_finish(sync)
            return
        ctx.wait(sync, "p2a.qmm.7.15")
        ctx.wait(sync, "p2a.qtr.7.15")
        ctx.wait(sync, "p1a.ktcw.3")
        ctx.wait(sync, "p1a.ktcw.2")
        ctx.op(lambda: sync.dma_start(
            out=ktc_sb.rearrange("p (h t) -> p h t", h=KVH),
            in_=ktc_d.ap().rearrange("h p t -> p h t")), S["ld_kv"], 16)
        for q2 in range(2):
            ctx.op(lambda q2=q2: sync.dma_start(
                out=vc_sb.rearrange("p (s d) -> p s d", s=NS)[:, 8 * q2:8 * q2 + 8, :],
                in_=vc_d.ap().rearrange("s p d -> p s d")[:, 8 * q2:8 * q2 + 8, :]),
                S["ld_kv"], 16, "p2b.kv" if q2 == 1 else None)
        for j in range(4):
            ctx.wait(sync, "p2a.qtw.7.15")
            ctx.wait(sync, "p2a.qtw.7.14")
            if j >= 2:
                ctx.wait(sync, f"p2b.sc.{j - 2}.31.{2 * (j - 2) + 1}")
            for q4 in range(4):
                src = qt_d[:, :, j * 512:(j + 1) * 512].rearrange(
                    "h p t -> p h t")[:, 8 * q4:8 * q4 + 8, :]
                dst = qt_sb[j % 2].rearrange("p (h t) -> p h t", h=H)[
                    :, 8 * q4:8 * q4 + 8, :]
                ctx.op(lambda src=src, dst=dst: sync.dma_start(out=dst, in_=src),
                       S[f"ld_qt{j % 2}"], 16, f"p2b.qt.{j}" if q4 == 3 else None)

        if mp < 4:
            sync_finish(sync)
            return
        ctx.wait(sync, "p2b.av.3.31.7")
        ctx.wait(sync, "p2b.otw.3.31")
        ctx.wait(sync, "p2b.otw.3.30")
        for ci in range(16):
            ctx.op(lambda ci=ci: sync.dma_start(
                out=ot_sb[:, ci * 2048:(ci + 1) * 2048], in_=ot_d[ci, :, :]),
                S["ld_ot"], 16, f"p2c.ot.{ci}")
        for db in range(8):
            for s in range(NS):
                t0 = s * 128
                bi = s % 2
                if db == 0 and s == 0:
                    ctx.wait(sync, "p2b.outev.3.31")
                use(("xs", bi), sync)
                ctx.op(lambda db=db, t0=t0, bi=bi: sync.dma_start(
                    out=xsb[bi], in_=X[t0:t0 + 128, db * 512:(db + 1) * 512]),
                    S[f"ld_x2{bi}"], 16, f"p2c.x.{db}.{s}")
                rel(("xs", bi), f"p2c.ev.{db}.{s}")

        if mp < 5:
            sync_finish(sync)
            return
        for s in range(NS):
            sl4 = s % 4
            ctx.wait(sync, f"p2c.xpw.7.{s}")
            if s < 2:
                ctx.wait(sync, "p2c.omm.6.15")
            elif s < 4:
                ctx.wait(sync, "p2c.omm.7.15")
            else:
                ctx.wait(sync, f"p2d.bev.{s - 4}.7")
            ctx.op(lambda s=s, sl4=sl4: sync.dma_start(
                out=xp_sb[:, sl4 * 4096:(sl4 + 1) * 4096],
                in_=xp_d[s * 128:(s + 1) * 128, :]),
                S[f"ld_xp{s % 2}"], 16, f"p2d.xp.{s}")
        sync_finish(sync)

    def emit_gpsimd(gp):
        for i in range(4):
            src = WKT.ap().rearrange("(c p) d -> p c d", p=128)[:, 8 * i: 8 * i + 8, :]
            dst = wk_sb.rearrange("p (c d) -> p c d", c=32)[:, 8 * i: 8 * i + 8, :]
            ctx.op(lambda src=src, dst=dst: gp.dma_start(out=dst, in_=src),
                   S["g_wk"], 16, f"p1a.wk.{i}")
        for g in range(NG):
            ctx.wait(gp, f"p1a.n1ev.{4 * g + 3}.3")
            for q4 in range(4):
                dsta = n1t_d[:, :, g * 512:(g + 1) * 512].rearrange(
                    "c p t -> p c t")[:, 8 * q4:8 * q4 + 8, :]
                srca = n1tg.rearrange("p (c t) -> p c t", c=32)[:, 8 * q4:8 * q4 + 8, :]
                ctx.op(lambda dsta=dsta, srca=srca: gp.dma_start(out=dsta, in_=srca),
                       S[f"g_n1w{g % 2}"], 16, f"p1a.n1tw.{g}" if q4 == 3 else None)
            ctx.wait(gp, f"p1a.kev.{4 * g + 3}.0")
            ctx.op(lambda g=g: gp.dma_start(
                out=ktc_d[:, :, g * 512:(g + 1) * 512].rearrange("h p t -> p h t"),
                in_=ktcg.rearrange("p (h t) -> p h t", h=KVH)),
                S[f"g_kw{g % 2}"], 16, f"p1a.ktcw.{g}")

        if mp < 1:
            gp_finish(gp)
            return
        ctx.wait(gp, "p1a.kmm.15")
        for i in range(4):
            src = WVT.ap().rearrange("(c p) d -> p c d", p=128)[:, 8 * i: 8 * i + 8, :]
            dst = wv_sb.rearrange("p (c d) -> p c d", c=32)[:, 8 * i: 8 * i + 8, :]
            ctx.op(lambda src=src, dst=dst: gp.dma_start(out=dst, in_=src),
                   S["g_wv"], 16, f"p1b.wv.{i}")
        for s in list(range(12, 16)) + list(range(12)):
            ctx.wait(gp, f"p1b.vev.{s}.1")
            ctx.op(lambda s=s: gp.dma_start(
                out=vc_d[s, :, :], in_=vcb[:, (s % 2) * 1024:(s % 2 + 1) * 1024]),
                S[f"g_vc{s % 2}"], 16, f"p1b.vcw.{s}")

        if mp < 2:
            gp_finish(gp)
            return
        def load_wq(db):
            for hh in range(2):
                if db >= 2:
                    ctx.wait(gp, f"p2a.qmm.{db - 2}.15")
                elif db == 0 and hh == 0:
                    ctx.wait(gp, "p1a.n1.15")
                for q4 in range(4):
                    src = WQT[:, db * 512 + hh * 256: db * 512 + (hh + 1) * 256] \
                        .rearrange("(c p) d -> p c d", p=128)[:, 8 * q4:8 * q4 + 8, :]
                    dst = wq_sb[:, (db % 2) * 16384:(db % 2 + 1) * 16384] \
                        .rearrange("p (c d) -> p c d", c=32)[
                        :, 8 * q4:8 * q4 + 8, hh * 256:(hh + 1) * 256]
                    ctx.op(lambda src=src, dst=dst: gp.dma_start(out=dst, in_=src),
                           S[f"g_wq{db % 2}"], 16,
                           f"p2a.wq.{db}.{hh}" if q4 == 3 else None)

        load_wq(0)
        if ndb > 1:
            load_wq(1)
        for db in range(ndb):
            if p2as >= 5:
                for s in range(NS):
                    ctx.wait(gp, f"p2a.qtev.{db}.{s}")
                    hbase = db * 4
                    dst = qt_d[hbase:hbase + 4, :, s * 128:(s + 1) * 128] \
                        .rearrange("h p t -> p h t")
                    src = qtw[s % 2].rearrange("p (h t) -> p h t", h=4)
                    ctx.op(lambda dst=dst, src=src: gp.dma_start(out=dst, in_=src),
                           S[f"g_qw{s % 2}"], 16, f"p2a.qtw.{db}.{s}")
            if db + 2 <= ndb - 1:
                load_wq(db + 2)

        if mp < 3:
            gp_finish(gp)
            return
        def load_wo(db):
            for hh in range(2):
                if db >= 2:
                    ctx.wait(gp, f"p2c.omm.{db - 2}.15")
                elif db == 0 and hh == 0:
                    ctx.wait(gp, "p2a.qmm.7.15")
                elif db == 1 and hh == 0:
                    ctx.wait(gp, "p2b.otw.3.31")
                    ctx.wait(gp, "p2b.otw.3.30")
                    ctx.wait(gp, "p2b.av.3.31.7")
                for q4 in range(4):
                    src = WOT[:, db * 512 + hh * 256: db * 512 + (hh + 1) * 256] \
                        .rearrange("(c p) d -> p c d", p=128)[:, 8 * q4:8 * q4 + 8, :]
                    dst = wo_sb[:, (db % 2) * 16384:(db % 2 + 1) * 16384] \
                        .rearrange("p (c d) -> p c d", c=32)[
                        :, 8 * q4:8 * q4 + 8, hh * 256:(hh + 1) * 256]
                    ctx.op(lambda src=src, dst=dst: gp.dma_start(out=dst, in_=src),
                           S[f"g_wo{db % 2}"], 16,
                           f"p2c.wo.{db}.{hh}" if q4 == 3 else None)

        load_wo(0)
        for j in range(4):
            for h in range(H):
                ctx.wait(gp, f"p2b.outev.{j}.{h}")
                ctx.op(lambda j=j, h=h: gp.dma_start(
                    out=ot_d[h, :, j * 512:(j + 1) * 512], in_=outw[h % 2]),
                    S[f"g_ot{h % 2}"], 16, f"p2b.otw.{j}.{h}")

        if mp < 4:
            gp_finish(gp)
            return
        load_wo(1)
        for db in range(8):
            for s in range(NS):
                t0 = s * 128
                ctx.wait(gp, f"p2c.ev.{db}.{s}")
                ctx.op(lambda db=db, t0=t0, s=s: gp.dma_start(
                    out=xp_d[t0:t0 + 128, db * 512:(db + 1) * 512], in_=xpw[s % 2]),
                    S[f"g_xp{s % 2}"], 16, f"p2c.xpw.{db}.{s}")
            if db + 2 <= 7:
                load_wo(db + 2)

        if mp < 5:
            gp_finish(gp)
            return
        ctx.wait(gp, "p2c.xpw.7.15")
        ctx.op(lambda: gp.dma_start(out=gw_sb, in_=GWT[:, :]),
               S["g_gw"], 16, "p2d.gw")
        ctx.wait(gp, "p2c.omm.7.15")
        for q4 in range(4):
            src = AT.ap().rearrange("(c p) r -> p c r", p=128)[:, 8 * q4:8 * q4 + 8, :]
            dst = aT_sb.rearrange("p (c r) -> p c r", c=32)[:, 8 * q4:8 * q4 + 8, :]
            ctx.op(lambda src=src, dst=dst: gp.dma_start(out=dst, in_=src),
                   S["g_ab"], 16, "p2d.at" if q4 == 3 else None)
        src = BT.ap().rearrange("(c p) d -> p c d", p=128)
        dst = bT_sb.rearrange("p (c d) -> p c d", c=4)
        ctx.op(lambda src=src, dst=dst: gp.dma_start(out=dst, in_=src),
               S["g_ab"], 16, "p2d.bt")
        for s in range(NS):
            ctx.wait(gp, f"p2d.bev.{s}.7")
            ctx.op(lambda s=s: gp.dma_start(
                out=OUT[s * 128:(s + 1) * 128, :],
                in_=outb[:, (s % 2) * 4096:(s % 2 + 1) * 4096]),
                S[f"g_out{s % 2}"], 16, f"out.{s}")
        gp_finish(gp)

    def emit_tensor(te):
        ctx.wait(te, "consts")
        def emit_ktr(s):
            kbi = s % 2
            half = s % 2
            ctx.wait(te, f"p1a.ropek.{s}")
            use(("pb1", half), te)
            for i in range(8):
                ctx.op(lambda half=half, i=i, kbi=kbi: te.transpose(
                    pbh(half, 0)[:, i * 128:(i + 1) * 128],
                    krot2[kbi][:, i * 128:(i + 1) * 128], ident[:]),
                    st_, 1 if i == 7 else 0,
                    f"p1a.ktr.{s}.0" if i == 7 else None)
            rel(("pb1", half), f"p1a.kev.{s}.0")

        for s in range(NS):
            g, sl = s // 4, s % 4
            bi = s % 2
            for b in range(4):
                if b == 0:
                    ctx.wait(te, f"p1a.n1.{s}")
                half = b % 2
                use(("pb1", half), te)
                for i in range(8):
                    ci = 8 * b + i
                    ctx.op(lambda half=half, ci=ci, i=i, bi=bi: te.transpose(
                        pbh(half, 0)[:, i * 128:(i + 1) * 128],
                        n1_2[bi][:, ci * 128:(ci + 1) * 128], ident[:]),
                        st_, 1 if i == 7 else 0,
                        f"p1a.tr.{s}.{b}" if i == 7 else None)
                rel(("pb1", half), f"p1a.n1ev.{s}.{b}")
            if s >= 1:
                emit_ktr(s - 1)
            ctx.wait(te, f"p1a.n1ev.{s}.3")
            if s == 0:
                ctx.wait(te, "p1a.wk.3")
            kb = s % 2
            use(("kps", kb), te)
            for dkb in range(2):
                for ci in range(32):
                    last = (dkb == 1 and ci == 31)
                    ctx.op(lambda kb=kb, dkb=dkb, ci=ci, sl=sl: te.matmul(
                        ps4[:, kb * 1024 + dkb * 512: kb * 1024 + (dkb + 1) * 512],
                        n1tg[:, ci * 512 + sl * 128: ci * 512 + sl * 128 + 128],
                        wk_sb[:, ci * 1024 + dkb * 512: ci * 1024 + (dkb + 1) * 512],
                        start=(ci == 0), stop=(ci == 31)),
                        st_, 1 if last else 0, f"p1a.kmm.{s}" if last else None)
            rel(("kps", kb), f"p1a.ropek.{s}")
        emit_ktr(NS - 1)

        if mp < 1:
            return
        rel(("ps4q", 0), "p1a.ropek.14")
        rel(("ps4q", 1), "p1a.ropek.14")
        rel(("ps4q", 2), "p1a.ropek.15")
        rel(("ps4q", 3), "p1a.ropek.15")
        first = True
        for s in list(range(12, 16)) + list(range(12)):
            g, sl = s // 4, s % 4
            if sl == 0 and g != 3:
                ctx.wait(te, f"p1b.n1t.{g}")
            if first:
                ctx.wait(te, "p1b.wv.3")
                ctx.wait(te, "p1a.kmm.15")
                first = False
            for dvb in range(2):
                q = (2 * s + dvb) % 4
                use(("ps4q", q), te)
                for ci in range(32):
                    name = (f"p1b.vmm.{s}" if dvb == 1 else f"p1b.vmm0.{s}") \
                        if ci == 31 else None
                    ctx.op(lambda q=q, ci=ci, sl=sl, dvb=dvb: te.matmul(
                        ps4[:, q * 512:(q + 1) * 512],
                        n1tg[:, ci * 512 + sl * 128: ci * 512 + sl * 128 + 128],
                        wv_sb[:, ci * 1024 + dvb * 512: ci * 1024 + (dvb + 1) * 512],
                        start=(ci == 0), stop=(ci == 31)),
                        st_, 1 if ci == 31 else 0, name)
                rel(("ps4q", q), f"p1b.vev.{s}.{dvb}")

        if mp < 2:
            return
        def emit_qtr(db, s):
            ctx.wait(te, f"p2a.rope.{db}.{s}")
            half = s % 2
            use(("pb1", half), te)
            for i in range(4):
                ctx.op(lambda half=half, i=i, s=s: te.transpose(
                    pbh(half, 0)[:, i * 128:(i + 1) * 128],
                    qrot2[s % 2][:, i * 128:(i + 1) * 128], ident[:]),
                    st_, 1 if i == 3 else 0,
                    f"p2a.qtr.{db}.{s}" if i == 3 else None)
            rel(("pb1", half), f"p2a.qtev.{db}.{s}")

        prev_q = None
        for db in range(8):
            for g in range(NG):
                for sl in range(4):
                    s = g * 4 + sl
                    if g == 0 and sl == 0:
                        ctx.wait(te, f"p2a.wq.{db}.1")
                    q = s % 4
                    use(("ps4q", q), te)
                    for ci in range(32):
                        if db == 0 and g == 0 and sl == 0:
                            ctx.wait(te, f"p2a.n1t.{ci}")
                        ctx.op(lambda q=q, ci=ci, s=s, db=db: te.matmul(
                            ps4[:, q * 512:(q + 1) * 512],
                            n1t_f[:, ci * 2048 + s * 128:
                                  ci * 2048 + s * 128 + 128],
                            wq_sb[:, (db % 2) * 16384 + ci * 512:
                                  (db % 2) * 16384 + (ci + 1) * 512],
                            start=(ci == 0), stop=(ci == 31)),
                            st_, 1 if ci == 31 else 0,
                            f"p2a.qmm.{db}.{s}" if ci == 31 else None)
                    rel(("ps4q", q), f"p2a.rope.{db}.{s}")
                    if prev_q is not None:
                        emit_qtr(*prev_q)
                    prev_q = (db, s)
        emit_qtr(*prev_q)


        if mp < 3:
            return
        rel(("sc", 0), "p2a.rope.7.13")
        rel(("sc", 1), "p2a.rope.7.15")
        rel(("psA",), res.get(("pb1", 0)))
        rel(("psC",), res.get(("pb1", 1)))
        rel(("psB",), "p2d.none" if False else None)
        res[("psB",)] = res.get(("psB",))
        def p2b_scores(j, h, gi):
            par = h % 2
            kvh = h // 4
            use(("sc", par), te)
            for p in range(2):
                kc = 2 * gi + p
                o = max(0, kc * 128 - j * 512)
                ctx.op(lambda par=par, p=p, kc=kc, kvh=kvh, h=h, j=j,
                       o=o: te.matmul(
                           ps4[:, par * 1024 + p * 512 + o:
                               par * 1024 + (p + 1) * 512],
                           ktc_sb[:, kvh * 2048 + kc * 128:
                                  kvh * 2048 + (kc + 1) * 128],
                           qt_sb[j % 2][:, h * 512 + o:(h + 1) * 512],
                           start=True, stop=True),
                       st_, 1 if p == 1 else 0,
                       f"p2b.sc.{j}.{h}.{gi}" if p == 1 else None)
            rel(("sc", par), f"p2b.exp.{j}.{h}.{gi}")

        def p2b_avsums(j, h, gi, nck):
            par = h % 2
            kvh = h // 4
            half = gi % 2
            psAC, kA = (psA, ("psA",)) if par == 0 else (psC, ("psC",))
            psBD, kB = (psB, ("psB",)) if par == 0 else (psD, ("psD",))
            ctx.wait(te, f"p2b.exp.{j}.{h}.{gi}")
            for p in range(2):
                kc = 2 * gi + p
                diag = kc >= 4 * j
                o = kc * 128 - j * 512 if diag else 0
                if diag:
                    ctx.wait(te, f"p2b.msk.{j}.{h}.{kc}")
                if kc == 0:
                    use(kA, te)
                    use(kB, te)
                last = (kc == nck - 1)
                e_sl = expT2[par][:, half * 1024 + p * 512 + o:
                                 half * 1024 + (p + 1) * 512]
                ctx.op(lambda kc=kc, o=o, e_sl=e_sl, kvh=kvh, last=last,
                       psAC=psAC: te.matmul(
                           psAC[:, o:512],
                           vc_sb[:, kc * 1024 + kvh * 128:
                                 kc * 1024 + (kvh + 1) * 128],
                           e_sl, start=(kc == 0), stop=last),
                       st_, 0)
                ctx.op(lambda kc=kc, o=o, e_sl=e_sl, last=last,
                       psBD=psBD: te.matmul(
                    psBD[:, o:512], ones2[:], e_sl,
                    start=(kc == 0), stop=last),
                    st_, 1 if p == 1 else 0,
                    f"p2b.av.{j}.{h}.{gi}" if p == 1 else None)
            if gi == nck // 2 - 1:
                rel(kA, f"p2b.outev.{j}.{h}")
                rel(kB, f"p2b.lns.{j}.{h}")

        for j in range(4):
            nck = 4 * j + 4
            ngrp = nck // 2
            for hp in range(0, H, 2):
                if hp == 0:
                    ctx.wait(te, f"p2b.qt.{j}")
                    if j == 0:
                        ctx.wait(te, "p2b.kv")
                for gi in range(ngrp):
                    p2b_scores(j, hp, gi)
                    p2b_scores(j, hp + 1, gi)
                    if gi > 0:
                        p2b_avsums(j, hp, gi - 1, nck)
                        p2b_avsums(j, hp + 1, gi - 1, nck)
                p2b_avsums(j, hp, ngrp - 1, nck)
                p2b_avsums(j, hp + 1, ngrp - 1, nck)


        if mp < 4:
            return
        rel(("ps4q", 0), "p2b.exp.3.30.7")
        rel(("ps4q", 1), "p2b.exp.3.30.7")
        rel(("ps4q", 2), "p2b.exp.3.31.7")
        rel(("ps4q", 3), "p2b.exp.3.31.7")
        for db in range(8):
            for s in range(NS):
                t0 = s * 128
                if s == 0:
                    ctx.wait(te, f"p2c.wo.{db}.1")
                q = s % 4
                use(("ps4q", q), te)
                for ci in range(32):
                    if db == 0 and s == 0:
                        ctx.wait(te, f"p2c.ot.{ci}")
                    ctx.op(lambda q=q, ci=ci, t0=t0, db=db: te.matmul(
                        ps4[:, q * 512:(q + 1) * 512],
                        ot_sb[:, ci * 2048 + t0: ci * 2048 + t0 + 128],
                        wo_sb[:, (db % 2) * 16384 + ci * 512:
                              (db % 2) * 16384 + (ci + 1) * 512],
                        start=(ci == 0), stop=(ci == 31)),
                        st_, 1 if ci == 31 else 0,
                        f"p2c.omm.{db}.{s}" if ci == 31 else None)
                rel(("ps4q", q), f"p2c.ev.{db}.{s}")

        if mp < 5:
            return
        rel(("pb1", 0), res.get(("psC",)))
        rel(("pb1", 1), res.get(("psD",)))

        def p2d_tr(s):
            for b in range(4):
                if b == 0:
                    ctx.wait(te, f"p2d.n2.{s}")
                half = b % 2
                use(("pb1", half), te)
                for i in range(8):
                    ci = 8 * b + i
                    ctx.op(lambda half=half, ci=ci, i=i, s=s: te.transpose(
                        pbh(half, 1)[:, i * 128:(i + 1) * 128],
                        n2_2[s % 2][:, ci * 128:(ci + 1) * 128], ident[:]),
                        st_, 1 if i == 7 else 0,
                        f"p2d.tr.{s}.{b}" if i == 7 else None)
                rel(("pb1", half), f"p2d.n2ev.{s}.{b}")

        def p2d_gmm(s):
            ctx.wait(te, f"p2d.n2ev.{s}.3")
            if s == 0:
                ctx.wait(te, "p2d.gw")
                use(("psB",), te)
            else:
                ctx.wait(te, f"p2d.probs.{s - 1}")
            for ci in range(32):
                ctx.op(lambda ci=ci, s=s: te.matmul(
                    psEa[:, 512:520], n2t2[s % 2][:, ci * 128:(ci + 1) * 128],
                    gw_sb[:, ci * 8:(ci + 1) * 8],
                    start=(ci == 0), stop=(ci == 31)),
                    st_, 1 if ci == 31 else 0, f"p2d.gmm.{s}" if ci == 31 else None)

        def p2d_amm(s):
            if s == 0:
                ctx.wait(te, "p2d.at")
                use(("psA",), te)
            for ci in range(32):
                ctx.op(lambda ci=ci, s=s: te.matmul(
                    psEa[:, 0:512], n2t2[s % 2][:, ci * 128:(ci + 1) * 128],
                    aT_sb[:, ci * 512:(ci + 1) * 512],
                    start=(ci == 0), stop=(ci == 31)),
                    st_, 1 if ci == 31 else 0, f"p2d.amm.{s}" if ci == 31 else None)

        def p2d_htr(s):
            ctx.wait(te, f"p2d.hsc.{s}")
            if s >= 1:
                ctx.wait(te, f"p2d.htev.{s - 1}")
            for i in range(4):
                ctx.op(lambda i=i, s=s: te.transpose(
                    psE_bf[:, 1536 + i * 128:1536 + (i + 1) * 128],
                    h_2[s % 2][:, i * 128:(i + 1) * 128], ident[:]),
                    st_, 1 if i == 3 else 0, f"p2d.htr.{s}" if i == 3 else None)

        def p2d_bmm(s):
            ctx.wait(te, f"p2d.htev.{s}")
            if s == 0:
                ctx.wait(te, "p2d.bt")
            for db in range(8):
                q = db % 4
                use(("ps4q", q), te)
                for rc in range(4):
                    ctx.op(lambda q=q, rc=rc, db=db, s=s: te.matmul(
                        ps4[:, q * 512:(q + 1) * 512],
                        hT_2[s % 2][:, rc * 128:(rc + 1) * 128],
                        bT_sb[:, rc * 4096 + db * 512: rc * 4096 + (db + 1) * 512],
                        start=(rc == 0), stop=(rc == 3)),
                        st_, 1 if rc == 3 else 0,
                        f"p2d.bmm.{s}.{db}" if rc == 3 else None)
                rel(("ps4q", q), f"p2d.bev.{s}.{db}")

        for s in range(NS):
            p2d_tr(s)
            if s >= 1:
                p2d_htr(s - 1)
                p2d_bmm(s - 1)
            p2d_gmm(s)
            p2d_amm(s)
        p2d_htr(NS - 1)
        p2d_bmm(NS - 1)

    def emit_scalar(sca):
        ctx.wait(sca, "consts")
        for s in range(NS):
            ctx.wait(sca, f"p1a.stats.{s}")
            fin = emit_ln_scalar_pre(sca, s % 2)
            ctx.op(fin, ss, 1, f"p1a.rstd.{s}")
            for b in range(4):
                half = b % 2
                ctx.wait(sca, f"p1a.tr.{s}.{b}")
                if s % 4 == 0 and b == 0 and s >= 4:
                    ctx.wait(sca, f"p1a.n1tw.{s // 4 - 1}")
                dst = n1tg.rearrange("p (c t) -> p c t", c=32)[
                    :, 8 * b: 8 * b + 8, (s % 4) * 128:(s % 4 + 1) * 128]
                srcp = pbh(half, 0).rearrange("p (c t) -> p c t", c=8)
                ctx.op(lambda dst=dst, srcp=srcp: sca.copy(dst, srcp),
                       ss, 1, f"p1a.n1ev.{s}.{b}")
            for sk in ([s - 1] if s >= 1 else []) + \
                    ([NS - 1] if s == NS - 1 else []):
                half = sk % 2
                ctx.wait(sca, f"p1a.ktr.{sk}.0")
                if sk % 4 == 0 and sk >= 4:
                    ctx.wait(sca, f"p1a.ktcw.{sk // 4 - 1}")
                dst = ktcg.rearrange("p (h t) -> p h t", h=KVH)[
                    :, :, (sk % 4) * 128:(sk % 4 + 1) * 128]
                srcp = pbh(half, 0).rearrange("p (h t) -> p h t", h=8)
                ctx.op(lambda dst=dst, srcp=srcp: sca.copy(dst, srcp),
                       ss, 1, f"p1a.kev.{sk}.0")
        if mp < 3:
            return
        for db in range(8):
            for s in range(NS):
                bi = s % 2
                half = s % 2
                ctx.wait(sca, f"p2a.qtr.{db}.{s}")
                use(("qtw", bi), sca)
                ctx.op(lambda bi=bi, half=half: sca.copy(
                    qtw[bi], pbh(half, 0)[:, 0:512]),
                    ss, 1, f"p2a.qtev.{db}.{s}")
                rel(("qtw", bi), f"p2a.qtw.{db}.{s}")
        def p2b_exp(j, h, gi):
            par = h % 2
            half = gi % 2
            ctx.wait(sca, f"p2b.sc.{j}.{h}.{gi}")
            use(("expT", par, half), sca)
            o0 = max(0, 2 * gi * 128 - j * 512)
            o1 = max(0, (2 * gi + 1) * 128 - j * 512)
            if o0 == 0 and o1 == 0:
                ctx.op(lambda par=par, half=half: sca.activation(
                    out=expT2[par][:, half * 1024:(half + 1) * 1024],
                    in_=ps4[:, par * 1024:(par + 1) * 1024],
                    func=AF.Exp, scale=ISQ),
                    ss, 1, f"p2b.exp.{j}.{h}.{gi}")
            else:
                for p, o in ((0, o0), (1, o1)):
                    ctx.op(lambda par=par, half=half, p=p, o=o: sca.activation(
                        out=expT2[par][:, half * 1024 + p * 512 + o:
                                       half * 1024 + (p + 1) * 512],
                        in_=ps4[:, par * 1024 + p * 512 + o:
                                par * 1024 + (p + 1) * 512],
                        func=AF.Exp, scale=ISQ),
                        ss, 1 if p == 1 else 0,
                        f"p2b.exp.{j}.{h}.{gi}" if p == 1 else None)
            rel(("expT", par, half), f"p2b.av.{j}.{h}.{gi}")

        def p2b_tail_s(j, h, ngrp):
            psBD = psB if h % 2 == 0 else psD
            ctx.wait(sca, f"p2b.av.{j}.{h}.{ngrp - 1}")
            ctx.op(lambda psBD=psBD: sca.activation(
                out=t_sb, in_=psBD[:, 0:512], func=AF.Ln),
                ss, 1, f"p2b.lns.{j}.{h}")
            use(("bcs", h % 2), sca)
            ctx.op(lambda h=h: sca.activation(
                out=bc2[h % 2], in_=t_sb, func=AF.Exp, scale=-1.0),
                ss, 1, f"p2b.bcs.{j}.{h}")
            rel(("bcs", h % 2), f"p2b.outev.{j}.{h}")

        for j in range(4):
            ngrp = (4 * j + 4) // 2
            for hp in range(0, H, 2):
                for gi in range(ngrp):
                    p2b_exp(j, hp, gi)
                    p2b_exp(j, hp + 1, gi)
                p2b_tail_s(j, hp, ngrp)
                p2b_tail_s(j, hp + 1, ngrp)
        if mp < 4:
            return
        ctx.wait(sca, "p2b.av.3.31.7")
        ctx.wait(sca, "p2b.otw.3.31")
        ctx.wait(sca, "p2b.otw.3.30")
        for ci in range(16, 32):
            ctx.op(lambda ci=ci: sca.dma_start(
                out=ot_sb[:, ci * 2048:(ci + 1) * 2048], in_=ot_d[ci, :, :]),
                S["s_ot"], 16, f"p2c.ot.{ci}")
        if mp < 5:
            return
        ctx.wait(sca, "p2d.stats.0")
        fin = emit_ln_scalar_pre(sca, 0)
        ctx.op(fin, ss, 1, "p2d.rstd.0")
        for s in range(NS):
            for b in (0, 1):
                ctx.wait(sca, f"p2d.tr.{s}.{b}")
                ctx.op(lambda s=s, b=b: sca.copy(
                    n2t2[s % 2][:, b * 1024:(b + 1) * 1024], pbh(b % 2, 1)),
                    ss, 1, f"p2d.n2ev.{s}.{b}")
            if s >= 1:
                ctx.wait(sca, f"p2d.htr.{s - 1}")
                ctx.op(lambda s=s: sca.copy(
                    hT_2[(s - 1) % 2], psE_bf[:, 1536:2048]),
                    ss, 1, f"p2d.htev.{s - 1}")
            for b in (2, 3):
                ctx.wait(sca, f"p2d.tr.{s}.{b}")
                ctx.op(lambda s=s, b=b: sca.copy(
                    n2t2[s % 2][:, b * 1024:(b + 1) * 1024], pbh(b % 2, 1)),
                    ss, 1, f"p2d.n2ev.{s}.{b}")
            if s + 1 < NS:
                ctx.wait(sca, f"p2d.stats.{s + 1}")
                fin = emit_ln_scalar_pre(sca, (s + 1) % 2)
                ctx.op(fin, ss, 1, f"p2d.rstd.{s + 1}")
            ctx.wait(sca, f"p2d.gmm.{s}")
            if s >= 1:
                ctx.wait(sca, f"p2d.g5.{s - 1}")
            ctx.op(lambda: sca.activation(out=probs, in_=psEa[:, 512:520],
                                          func=AF.Exp),
                   ss, 1, f"p2d.probs.{s}")
        ctx.wait(sca, f"p2d.htr.{NS - 1}")
        ctx.op(lambda: sca.copy(hT_2[(NS - 1) % 2], psE_bf[:, 1536:2048]),
               ss, 1, f"p2d.htev.{NS - 1}")
        if ctx.emit:
            for nm in ("s_ab", "s_ot"):
                if sems[nm].val > 0:
                    sca.wait_ge(sems[nm].h, sems[nm].val)

    def emit_vector(vec):
        ctx.wait(vec, "consts")
        def emit_ropek(s):
            bi = s % 2
            ctx.wait(vec, f"p1a.kmm.{s}")
            ctx.wait(vec, f"p1a.cs.{s}")
            if s >= 2:
                ctx.wait(vec, f"p1a.ktr.{s - 2}.0")
            kb = s % 2
            fin = emit_rope(vec, ps4[:, kb * 1024:(kb + 1) * 1024],
                            p1_cs[:, bi * 512:(bi + 1) * 512],
                            p1_tmp, krot2[bi], 8)
            ctx.op(fin, sv, 1, f"p1a.ropek.{s}")

        for s in range(NS):
            bi = s % 2
            ctx.wait(vec, f"p1a.x.{s}")
            if s >= 2:
                ctx.wait(vec, f"p1a.rstd.{s - 2}")
            fin = emit_bn(vec, xb_p1[:, bi * 4096:(bi + 1) * 4096], bi)
            ctx.op(fin, sv, 1, f"p1a.stats.{s}")
            ctx.wait(vec, f"p1a.rstd.{s}")
            if s >= 2:
                ctx.wait(vec, f"p1a.tr.{s - 2}.3")
            ctx.op(lambda bi=bi: vec.tensor_scalar(
                n1_2[bi], xb_p1[:, bi * 4096:(bi + 1) * 4096],
                statv(bi, 0), statv(bi, 3), ALU.subtract, ALU.mult),
                sv, 1, f"p1a.n1.{s}")
            if s >= 1:
                emit_ropek(s - 1)
        emit_ropek(NS - 1)

        if mp < 1:
            return
        sseq = list(range(12, 16)) + list(range(12))
        for si, s in enumerate(sseq):
            for dvb in range(2):
                q = (2 * s + dvb) % 4
                ctx.wait(vec, f"p1b.vmm0.{s}" if dvb == 0 else f"p1b.vmm.{s}")
                if si == 0 and dvb == 0:
                    ctx.wait(vec, "p1a.ktcw.2")
                    ctx.wait(vec, "p1a.ktcw.3")
                if si >= 2 and dvb == 0:
                    ctx.wait(vec, f"p1b.vcw.{sseq[si - 2]}")
                ctx.op(lambda s=s, dvb=dvb, q=q: vec.tensor_copy(
                    vcb[:, (s % 2) * 1024 + dvb * 512:
                        (s % 2) * 1024 + (dvb + 1) * 512],
                    ps4[:, q * 512:(q + 1) * 512]),
                    sv, 1, f"p1b.vev.{s}.{dvb}")

        if mp < 2:
            return
        prevroq = {}
        for db in range(8):
            for s in range(NS):
                bi = s % 2
                q = s % 4
                ctx.wait(vec, f"p2a.qmm.{db}.{s}")
                ctx.wait(vec, f"p2a.cs.{db}.{s}")
                if bi in prevroq:
                    ctx.wait(vec, prevroq[bi])
                fin = emit_rope(vec, ps4[:, q * 512:(q + 1) * 512],
                                p2a_cs[:, bi * 512:(bi + 1) * 512],
                                p2a_tmp, qrot2[bi], 4)
                ctx.op(fin, sv, 1, f"p2a.rope.{db}.{s}")
                prevroq[bi] = f"p2a.qtr.{db}.{s}"

        if mp < 3:
            return
        def p2b_masks(j, h):
            par = h % 2
            nck = 4 * j + 4
            for kc in range(4 * j, nck):
                gi = kc // 2
                half = gi % 2
                o = kc * 128 - j * 512
                ctx.wait(vec, f"p2b.exp.{j}.{h}.{gi}")
                sl_ = expT2[par][:, half * 1024 + (kc % 2) * 512 + o:
                                half * 1024 + (kc % 2) * 512 + o + 128]
                ctx.op(lambda sl_=sl_: vec.tensor_tensor(
                    sl_, sl_, tri[:], ALU.mult),
                    sv, 1, f"p2b.msk.{j}.{h}.{kc}")

        def p2b_outev(j, h):
            ctx.wait(vec, f"p2b.bcs.{j}.{h}")
            psAC = psA if h % 2 == 0 else psC
            use(("outw", h % 2), vec)
            ctx.op(lambda h=h, psAC=psAC: vec.tensor_tensor(
                outw[h % 2], psAC[:, 0:512], bc2[h % 2], ALU.mult),
                sv, 1, f"p2b.outev.{j}.{h}")
            rel(("outw", h % 2), f"p2b.otw.{j}.{h}")

        for j in range(4):
            for hp in range(0, H, 2):
                p2b_masks(j, hp)
                p2b_masks(j, hp + 1)
                p2b_outev(j, hp)
                p2b_outev(j, hp + 1)

        if mp < 4:
            return
        for db in range(8):
            for s in range(NS):
                q = s % 4
                bi = s % 2
                ctx.wait(vec, f"p2c.omm.{db}.{s}")
                ctx.wait(vec, f"p2c.x.{db}.{s}")
                use(("xpw", bi), vec)
                ctx.op(lambda q=q, bi=bi: vec.tensor_tensor(
                    xpw[bi], ps4[:, q * 512:(q + 1) * 512], xsb[bi], ALU.add),
                    sv, 1, f"p2c.ev.{db}.{s}")
                rel(("xpw", bi), f"p2c.xpw.{db}.{s}")

        if mp < 5:
            return
        if mp < 5:
            return
        def p2d_stats(s):
            ctx.wait(vec, f"p2d.xp.{s}")
            fin = emit_bn(vec, xp_sb[:, (s % 4) * 4096:(s % 4 + 1) * 4096],
                          s % 2)
            ctx.op(fin, sv, 1, f"p2d.stats.{s}")

        def p2d_n2(s):
            ctx.wait(vec, f"p2d.rstd.{s}")
            if s <= 1:
                ctx.wait(vec, "p2c.omm.7.15")
            if s >= 2:
                ctx.wait(vec, f"p2d.tr.{s - 2}.3")
            ctx.op(lambda s=s: vec.tensor_scalar(
                n2_2[s % 2], xp_sb[:, (s % 4) * 4096:(s % 4 + 1) * 4096],
                statv(s % 2, 0), statv(s % 2, 3), ALU.subtract, ALU.mult),
                sv, 1, f"p2d.n2.{s}")

        def p2d_gate_hsc(s):
            ctx.wait(vec, f"p2d.probs.{s}")
            if s >= 2:
                ctx.wait(vec, f"p2d.htr.{s - 2}")
            def vedge(f, tag, s=s):
                ctx.op(f, sv, 1, f"p2d.g{tag}.{s}")
                ctx.wait(vec, f"p2d.g{tag}.{s}")
            vedge(lambda: vec.max(ltm, probs), 1)
            vedge(lambda: vec.tensor_scalar(keept, probs, ltm[:, 1:2], None,
                                            ALU.is_ge), 2)
            vedge(lambda: vec.tensor_scalar(wsum, ltm[:, 0:1], ltm[:, 1:2],
                                            None, ALU.add), 3)
            vedge(lambda: vec.reciprocal(winv, wsum), 4)
            vedge(lambda: vec.tensor_tensor(wful, probs, keept, ALU.mult), 5)
            vedge(lambda: vec.tensor_scalar(wful, wful, winv, None, ALU.mult), 6)
            ctx.wait(vec, f"p2d.amm.{s}")
            for e in range(NE):
                ctx.op(lambda e=e, s=s: vec.tensor_scalar(
                    h_2[s % 2][:, e * 64:(e + 1) * 64],
                    psEa[:, e * 64:(e + 1) * 64],
                    wful[:, e:e + 1], None, ALU.mult),
                    sv, 1 if e == 7 else 0, f"p2d.hsc.{s}" if e == 7 else None)

        def p2d_bev(s):
            for db in range(8):
                q = db % 4
                ctx.wait(vec, f"p2d.bmm.{s}.{db}")
                if s >= 2 and db == 0:
                    ctx.wait(vec, f"out.{s - 2}")
                ctx.op(lambda s=s, db=db, q=q: vec.tensor_tensor(
                    outb[:, (s % 2) * 4096 + db * 512:
                         (s % 2) * 4096 + (db + 1) * 512],
                    ps4[:, q * 512:(q + 1) * 512],
                    xp_sb[:, (s % 4) * 4096 + db * 512:
                          (s % 4) * 4096 + (db + 1) * 512],
                    ALU.add),
                    sv, 1, f"p2d.bev.{s}.{db}")

        p2d_stats(0)
        p2d_n2(0)
        for s in range(NS):
            if s >= 1:
                p2d_gate_hsc(s - 1)
            if s + 1 < NS:
                p2d_stats(s + 1)
            if s >= 1:
                p2d_bev(s - 1)
            if s + 1 < NS:
                p2d_n2(s + 1)
        p2d_gate_hsc(NS - 1)
        p2d_bev(NS - 1)

    emitters = [emit_sync, emit_gpsimd, emit_tensor, emit_scalar, emit_vector]
    ctx.start_pass(emit=False)
    res.clear()
    for f in emitters:
        f(None)
    ctx.finish_pass1()

    sem_nums = sorted(s.h.num for s in sems.values())
    assert sem_nums == list(range(sem_nums[0], sem_nums[0] + len(sem_nums)))
    srange = range(sem_nums[0], sem_nums[-1] + 1)
    nc.all_engine_barrier()
    nc.gpsimd.dma_reset(srange)
    nc.gpsimd.sem_clear(srange)
    nc.all_engine_barrier()

    ctx.start_pass(emit=True)
    res.clear()
    with nc.Block() as block:
        @block.sync
        def _(sync):
            emit_sync(sync)

        @block.gpsimd
        def _(gp):
            emit_gpsimd(gp)

        @block.tensor
        def _(te):
            emit_tensor(te)

        @block.scalar
        def _(sca):
            emit_scalar(sca)

        @block.vector
        def _(vec):
            emit_vector(vec)

    return nc


def host_inputs(L0_attn_x, L0_attn_Wq, L0_attn_Wk, L0_attn_Wv, L0_attn_Wo,
                L0_ffn_gate_w, L0_ffn_lora_A, L0_ffn_lora_B):
    cos, sin = _llama3_rope_tables()
    gwt = np.ascontiguousarray(
        np.asarray(L0_ffn_gate_w, dtype=np.float32).T.reshape(32, 128, 8)
        .transpose(1, 0, 2).reshape(128, 256))
    shared = {
        "wqt": np.ascontiguousarray(np.asarray(L0_attn_Wq, np.float32).T),
        "wkt": np.ascontiguousarray(np.asarray(L0_attn_Wk, np.float32).T),
        "wvt": np.ascontiguousarray(np.asarray(L0_attn_Wv, np.float32).T),
        "wot": np.ascontiguousarray(np.asarray(L0_attn_Wo, np.float32).T),
        "gwt": gwt,
        "at": np.ascontiguousarray(
            np.asarray(L0_ffn_lora_A, np.float32).transpose(2, 0, 1)
            .reshape(D, NE * RK)),
        "bt": np.ascontiguousarray(
            np.asarray(L0_ffn_lora_B, np.float32).transpose(0, 2, 1)
            .reshape(NE * RK, D)),
        "cos": cos,
        "sin": sin,
        "idt_d": np.eye(128, dtype=BF),
        "tri_d": np.ascontiguousarray(
            (np.arange(128)[None, :] >= np.arange(128)[:, None]).astype(BF)),
        "ones2_d": np.ones((128, 128), dtype=BF),
        "cvec_d": np.full((128, 2), EPS, dtype=np.float32),
    }
    x = np.asarray(L0_attn_x, np.float32)
    in_maps = []
    for c in range(8):
        m = dict(shared)
        m["x"] = np.ascontiguousarray(x[c])
        in_maps.append(m)
    return in_maps


def kernel(L0_attn_x, L0_attn_Wq, L0_attn_Wk, L0_attn_Wv, L0_attn_Wo,
           L0_ffn_gate_w, L0_ffn_lora_A, L0_ffn_lora_B):
    in_maps = host_inputs(L0_attn_x, L0_attn_Wq, L0_attn_Wk, L0_attn_Wv,
                          L0_attn_Wo, L0_ffn_gate_w, L0_ffn_lora_A, L0_ffn_lora_B)
    nc = build_program(dbg=False)
    res = run_bass_kernel_spmd(nc, in_maps, core_ids=list(range(8)))
    out = np.stack([np.asarray(r["out"], dtype=np.float32) for r in res.results])
    return out


if __name__ == "__main__":
    nc = build_program(dbg=False)
    print("built ok")
